# version 07
# baseline (speedup 1.0000x reference)
import numpy as np
import ml_dtypes
import concourse.bass as bass
import concourse.mybir as mybir
from concourse.bass_utils import run_bass_kernel_spmd
from contextlib import ExitStack

F32 = mybir.dt.float32
BF16 = mybir.dt.bfloat16
AF = mybir.ActivationFunctionType
ALU = mybir.AluOpType
AX = mybir.AxisListType

NCORE = 8
S, E, H, D, FF = 8192, 2048, 16, 128, 8192
T = S // NCORE
NDC = E // 128
FSL = FF // NCORE
NHL = 2
ISD = float(1.0 / np.sqrt(128.0))

DEBUG_STAGE = 99
DBG_SKIP_AG2 = False
DBG_SKIP_QKV_TAIL = False

ENGS = ("tensor", "vector", "scalar", "sync", "gpsimd")
SEMS = ("L", "G", "T", "V", "S", "C", "FS", "FG")

SC_A = 0
SC_B = 2048
SC_RSTD = 4096
SC_NMR = 5120
SC_T0 = 6144
SC_T1 = 7168
SM_THR1, SM_THR2, SM_NEGC0, SM_NEGC1, SM_ABS1, SM_ABS2 = 0, 1, 2, 3, 4, 5
SM_M0 = 6
SM_RES = 40


class KB:
    def __init__(self, nc, which, eng, sems, t):
        self.nc, self.which, self.eng, self.sems, self.t = nc, which, eng, sems, t
        self.c = {k: 0 for k in SEMS}
        self._pidv = None

    def pid(self):
        if self._pidv is None:
            self._pidv = self.eng.partition_id()
        return self._pidv

    def hsel(self, lh):
        if not hasattr(self, "_hsel"):
            p = self.pid()
            self._hsel = [p * NHL + j for j in range(NHL)]
        return self._hsel[lh]

    def town(self):
        if not hasattr(self, "_town"):
            self._town = self.pid() * T
        return self._town

    def on(self, name):
        return self.which == name

    def now(self, s):
        return self.c[s]

    def _go(self, engname, fn, inc, wait):
        if self.which == engname:
            for s, v in wait:
                if v > 0:
                    self.eng.wait_ge(self.sems[s], v)
            ins = fn(self.eng)
            for s, n in inc:
                ins = ins.then_inc(self.sems[s], n)
        for s, n in inc:
            self.c[s] += n

    def relay(self, q, extra_wait=()):
        qsem, rsem = ("L", "FS") if q == "sync" else ("G", "FG")
        if self.which == q:
            for s, v in extra_wait:
                if v > 0:
                    self.eng.wait_ge(self.sems[s], v)
            if self.c[qsem] > 0:
                self.eng.wait_ge(self.sems[qsem], self.c[qsem])
            self.eng.sem_inc(self.sems[rsem], 1)
        self.c[rsem] += 1
        return self.c[rsem]

    def fence(self):
        self.relay("sync")
        self.relay("gpsimd")
        tgt = [(s, self.c[s]) for s in ("FS", "FG", "T", "V", "S", "C")]
        for s, v in tgt:
            if v > 0:
                self.eng.wait_ge(self.sems[s], v)

    def mm(self, out, lhsT, rhs, start=True, stop=True, inc=(), wait=()):
        self._go("tensor",
                 lambda e: e.matmul(out, lhsT, rhs, start=start, stop=stop),
                 inc, wait)

    def tr(self, out, in_, ident, inc=(), wait=()):
        if not isinstance(ident, bass.AP):
            ident = ident[:, :]
        self._go("tensor", lambda e: e.transpose(out, in_, ident), inc, wait)

    def act(self, out, in_, func, bias=0.0, scale=1.0, inc=(), wait=()):
        w = [("S", self.c["S"])] + list(wait)
        ii = [x for x in inc if x[0] != "S"] + [("S", 1)]
        self._go("scalar",
                 lambda e: e.activation(out=out, in_=in_, func=func,
                                        bias=bias, scale=scale), ii, w)

    def scopy(self, out, in_, inc=(), wait=()):
        w = [("S", self.c["S"])] + list(wait)
        ii = [x for x in inc if x[0] != "S"] + [("S", 1)]
        self._go("scalar", lambda e: e.copy(out=out, in_=in_), ii, w)

    def v(self, method, *a, inc=(), wait=(), **kw):
        w = [("V", self.c["V"])] + list(wait)
        ii = [x for x in inc if x[0] != "V"] + [("V", 1)]
        self._go("vector", lambda e: getattr(e, method)(*a, **kw), ii, w)

    def dma(self, q, out, in_, inc=None, wait=(), slow=False):
        if inc is None:
            inc = [(("L" if q == "sync" else "G"), 16)]
        self._go(q, lambda e: e.dma_start(out=out, in_=in_,
                                          allow_slow_non_contiguous=slow),
                 inc, wait)

    def dma_dyn(self, q, out, in_fn, inc=None, wait=()):
        if inc is None:
            inc = [(("L" if q == "sync" else "G"), 16)]
        self._go(q, lambda e: e.dma_start(out=out, in_=in_fn(self.pid())),
                 inc, wait)

    def coll(self, kind, op, ins, outs, inc=(), wait=()):
        self._go("gpsimd",
                 lambda e: e.collective_compute(
                     kind, op, replica_groups=[list(range(NCORE))],
                     ins=[a.opt() for a in ins], outs=[a.opt() for a in outs]),
                 inc, wait)


def declare(nc, es):
    t = {}
    din = lambda n, sh, dt: nc.dram_tensor(n, sh, dt, kind="ExternalInput").ap()
    t["xs"] = din("xs", [T, E], F32)
    t["w1s"] = din("w1s", [FSL, E], F32)
    t["w2s"] = din("w2s", [E, FSL], F32)
    t["wq"] = din("wq", [E, E], F32)
    t["wk"] = din("wk", [E, E], F32)
    t["wv"] = din("wv", [E, E], F32)
    t["wo"] = din("wo", [E, E], F32)
    t["b1"] = din("b1", [64, 128], F32)
    t["b2h"] = din("b2h", [16, 128], F32)
    t["cosf"] = din("cosf", [128, T], F32)
    t["sinf"] = din("sinf", [128, T], F32)
    t["masks"] = din("masks", [4, 128, 512], BF16)
    t["idbf"] = din("idbf", [128, 128], BF16)
    t["idf"] = din("idf", [128, 128], F32)
    t["onesf"] = din("onesf", [128, 128], F32)
    t["onesb"] = din("onesb", [128, 8], BF16)

    t["out"] = nc.dram_tensor("out", [T, E], F32, kind="ExternalOutput").ap()

    t["ar_in"] = nc.dram_tensor("ar_in", [1, 2], F32).ap()
    t["ar_out"] = nc.dram_tensor("ar_out", [1, 2], F32).ap()
    t["ag1_in"] = nc.dram_tensor("ag1_in", [2, 128, 128, 128], BF16).ap()
    t["ag1_out"] = nc.dram_tensor("ag1_out", [NCORE, 2, 128, 128, 128], BF16).ap()
    t["ag2_in"] = nc.dram_tensor("ag2_in", [3, H, 128 * T], BF16).ap()
    t["ag2_out"] = nc.dram_tensor("ag2_out", [NCORE, 3, H, 128 * T], BF16).ap()
    t["nmx_in"] = nc.dram_tensor("nmx_in", [2 * H], F32).ap()
    t["nmx_out"] = nc.dram_tensor("nmx_out", [NCORE, 2 * H], F32).ap()
    t["negc_d"] = nc.dram_tensor("negc_d", [H], F32).ap()
    t["kvl"] = nc.dram_tensor("kvl", [2, NHL, 128, S], BF16).ap()
    t["ag3_in"] = nc.dram_tensor("ag3_in", [NHL, 128, S], BF16).ap()
    t["ag3_out"] = nc.dram_tensor("ag3_out", [NCORE, NHL, 128, S], BF16).ap()

    sb = lambda n, sh, dt: es.enter_context(nc.sbuf_tensor(n, sh, dt))
    t["x"] = sb("x", [128, NDC * T], F32)
    t["a32"] = sb("a32", [128, 16384], BF16)
    t["b32"] = sb("b32", [128, 16384], BF16)
    t["wstr"] = sb("wstr", [128, 8192], BF16)
    t["kbuf"] = sb("kbuf", [128, 2048], BF16)
    t["vbuf"] = sb("vbuf", [128, 2048], BF16)
    t["expb"] = sb("expb", [128, 1024], BF16)
    t["qst"] = sb("qst", [128, 2048], BF16)
    t["scr"] = sb("scr", [128, 8192], F32)
    t["cos"] = sb("cos", [128, T], F32)
    t["sin"] = sb("sin", [128, T], F32)
    t["mask"] = sb("mask", [128, 2048], BF16)
    t["idbf_t"] = sb("idbf_t", [128, 128], BF16)
    t["idf_t"] = sb("idf_t", [128, 128], F32)
    t["onesf_t"] = sb("onesf_t", [128, 128], F32)
    t["onesb_t"] = sb("onesb_t", [128, 8], BF16)
    t["b1_t"] = sb("b1_t", [128, 64], F32)
    t["b2h_t"] = sb("b2h_t", [128, 16], F32)
    t["small"] = sb("small", [128, 64], F32)

    ps = lambda n, sh, dt: es.enter_context(nc.psum_tensor(n, sh, dt))
    t["ps0"] = ps("ps0", [128, 1024], F32)
    t["ps1"] = ps("ps1", [128, 1024], F32)
    t["ps2"] = ps("ps2", [128, 1024], F32)
    t["psT"] = ps("psT", [128, 1024], BF16)
    t["psU"] = ps("psU", [128, 512], F32)
    return t


def phase_consts(kb, t):
    scr = t["scr"]
    for name, dst in (("cosf", t["cos"]), ("sinf", t["sin"]),
                      ("idbf", t["idbf_t"]), ("idf", t["idf_t"]),
                      ("onesf", t["onesf_t"]), ("onesb", t["onesb_t"])):
        kb.dma("sync", dst[:, :], t[name][:, :])
    for j in range(4):
        kb.dma("sync", t["mask"][:, j * 512:(j + 1) * 512], t["masks"][j])
    kb.dma("sync", scr[0:64, 0:128], t["b1"][:, :])
    kb.dma("sync", scr[0:16, 128:256], t["b2h"][:, :])
    kb.v("memset", t["small"][:, :], 0.0, inc=[("V", 1)])
    r = kb.relay("sync")
    kb.tr(t["psU"][:, 0:64], scr[0:64, 0:128], t["idf_t"][0:64, 0:64],
          wait=[("FS", r)])
    kb.tr(t["psU"][:, 64:80], scr[0:16, 128:256], t["idf_t"][0:16, 0:16],
          inc=[("T", 1)])
    kb.v("tensor_copy", t["b1_t"][:, :], t["psU"][:, 0:64],
         wait=[("T", kb.now("T"))])
    kb.v("tensor_copy", t["b2h_t"][:, :], t["psU"][:, 64:80], inc=[("V", 1)])
    kb.fence()


def phase_wprep(kb, t):
    scr, small = t["scr"], t["small"]
    ps2, psU = t["ps2"], t["psU"]
    onesf = t["onesf_t"]

    bguard = {SC_A: 0, SC_B: 0}
    for mat in range(2):
        nch = 8 if mat == 0 else 16
        wid = 2048 if mat == 0 else 1024
        src = t["w1s"] if mat == 0 else t["w2s"]
        acc = SM_ABS1 if mat == 0 else SM_ABS2
        for j in range(nch):
            buf = SC_A if j % 2 == 0 else SC_B
            kb.dma("sync", scr[:, buf:buf + wid],
                   src[j * 128:(j + 1) * 128, :],
                   wait=([("V", bguard[buf])] if bguard[buf] else []))
            r = kb.relay("sync")
            kb.v("reduce_sum", scr[0:128, SC_T0:SC_T0 + 1],
                 scr[:, buf:buf + wid], axis=AX.X, apply_absolute_value=True,
                 wait=[("FS", r)])
            kb.v("tensor_add", small[:, acc:acc + 1], small[:, acc:acc + 1],
                 scr[0:128, SC_T0:SC_T0 + 1], inc=[("V", 1)])
            bguard[buf] = kb.now("V")
    kb.mm(ps2[0:1, 0:2], onesf[:, 0:1], small[:, SM_ABS1:SM_ABS1 + 2],
          wait=[("V", kb.now("V"))], inc=[("T", 1)])
    kb.v("tensor_copy", scr[0:1, SC_T0:SC_T0 + 2], ps2[0:1, 0:2],
         wait=[("T", kb.now("T"))], inc=[("V", 1)])
    kb.dma("sync", t["ar_in"][:, :], scr[0:1, SC_T0:SC_T0 + 2],
           wait=[("V", kb.now("V"))])
    r = kb.relay("sync")
    kb.coll("AllReduce", ALU.add, [t["ar_in"]], [t["ar_out"]],
            wait=[("FS", r)], inc=[("C", 1)])
    kb.dma("sync", scr[0:1, SC_T0:SC_T0 + 2], t["ar_out"][:, :],
           wait=[("C", kb.now("C"))])
    r = kb.relay("sync")
    kb.v("tensor_scalar_mul", scr[0:1, SC_T0:SC_T0 + 1],
         scr[0:1, SC_T0:SC_T0 + 1], float(1.0 / (FF * E)), wait=[("FS", r)])
    kb.v("tensor_scalar_mul", scr[0:1, SC_T0 + 1:SC_T0 + 2],
         scr[0:1, SC_T0 + 1:SC_T0 + 2], float(1.0 / (E * FF)), inc=[("V", 1)])
    kb.mm(psU[:, 100:102], onesf[0:1, :], scr[0:1, SC_T0:SC_T0 + 2],
          wait=[("V", kb.now("V"))], inc=[("T", 1)])
    kb.v("tensor_copy", small[:, SM_M0:SM_M0 + 2], psU[:, 100:102],
         wait=[("T", kb.now("T"))], inc=[("V", 1)])
    Vm0 = kb.now("V")

    kb.v("memset", small[:, SM_ABS1:SM_ABS1 + 2], 0.0, wait=[("V", Vm0)],
         inc=[("V", 1)])
    bguard = {SC_A: kb.now("V"), SC_B: kb.now("V")}
    for mat in range(2):
        nch = 8 if mat == 0 else 16
        wid = 2048 if mat == 0 else 1024
        src = t["w1s"] if mat == 0 else t["w2s"]
        acc = SM_ABS1 if mat == 0 else SM_ABS2
        m0c = small[:, SM_M0 + mat:SM_M0 + mat + 1]
        for j in range(nch):
            buf = SC_A if j % 2 == 0 else SC_B
            kb.dma("sync", scr[:, buf:buf + wid],
                   src[j * 128:(j + 1) * 128, :], wait=[("V", bguard[buf])])
            r = kb.relay("sync")
            kb.v("tensor_scalar_mul", scr[:, SC_RSTD:SC_RSTD + wid],
                 scr[:, buf:buf + wid], -1.0, wait=[("FS", r)])
            kb.v("tensor_max", scr[:, SC_RSTD:SC_RSTD + wid],
                 scr[:, SC_RSTD:SC_RSTD + wid], scr[:, buf:buf + wid])
            kb.v("tensor_scalar_sub", scr[:, SC_RSTD:SC_RSTD + wid],
                 scr[:, SC_RSTD:SC_RSTD + wid], m0c)
            kb.v("reduce_sum", scr[0:128, SC_T1:SC_T1 + 1],
                 scr[:, SC_RSTD:SC_RSTD + wid], axis=AX.X)
            kb.v("tensor_add", small[:, acc:acc + 1], small[:, acc:acc + 1],
                 scr[0:128, SC_T1:SC_T1 + 1], inc=[("V", 1)])
            bguard[buf] = kb.now("V")
    kb.mm(ps2[0:1, 0:2], onesf[:, 0:1], small[:, SM_ABS1:SM_ABS1 + 2],
          wait=[("V", kb.now("V"))], inc=[("T", 1)])
    kb.v("tensor_copy", scr[0:1, SC_T0:SC_T0 + 2], ps2[0:1, 0:2],
         wait=[("T", kb.now("T"))], inc=[("V", 1)])
    kb.dma("sync", t["ar_in"][:, :], scr[0:1, SC_T0:SC_T0 + 2],
           wait=[("V", kb.now("V"))])
    r = kb.relay("sync")
    kb.coll("AllReduce", ALU.add, [t["ar_in"]], [t["ar_out"]],
            wait=[("FS", r)], inc=[("C", 1)])
    kb.dma("sync", scr[0:1, SC_T0:SC_T0 + 2], t["ar_out"][:, :],
           wait=[("C", kb.now("C"))])
    r = kb.relay("sync")
    kb.v("tensor_scalar_mul", scr[0:1, SC_T0:SC_T0 + 1],
         scr[0:1, SC_T0:SC_T0 + 1], float(1.0 / (FF * E)), wait=[("FS", r)])
    kb.v("tensor_scalar_mul", scr[0:1, SC_T0 + 1:SC_T0 + 2],
         scr[0:1, SC_T0 + 1:SC_T0 + 2], float(1.0 / (E * FF)))
    kb.v("tensor_add", scr[0:1, SC_T0:SC_T0 + 2], scr[0:1, SC_T0:SC_T0 + 2],
         small[0:1, SM_M0:SM_M0 + 2])
    kb.v("tensor_scalar_mul", scr[0:1, SC_T0:SC_T0 + 2],
         scr[0:1, SC_T0:SC_T0 + 2], float(1.0 + 3.51e-7))
    kb.v("tensor_scalar_max", scr[0:1, SC_T0:SC_T0 + 2],
         scr[0:1, SC_T0:SC_T0 + 2], 1e-8)
    kb.v("tensor_scalar_mul", scr[0:1, SC_T0:SC_T0 + 2],
         scr[0:1, SC_T0:SC_T0 + 2], 0.5, inc=[("V", 1)])
    kb.mm(psU[:, 100:102], onesf[0:1, :], scr[0:1, SC_T0:SC_T0 + 2],
          wait=[("V", kb.now("V"))], inc=[("T", 1)])
    kb.v("tensor_copy", small[:, SM_THR1:SM_THR1 + 2], psU[:, 100:102],
         wait=[("T", kb.now("T"))], inc=[("V", 1)])
    Vthr = kb.now("V")

    bguard = {SC_A: Vthr, SC_B: Vthr}
    for mat in range(2):
        nch = 8 if mat == 0 else 16
        wid = 2048 if mat == 0 else 1024
        src = t["w1s"] if mat == 0 else t["w2s"]
        dst = t["a32"] if mat == 0 else t["b32"]
        thc = SM_THR1 if mat == 0 else SM_THR2
        thr = small[:, thc:thc + 1]
        for j in range(nch):
            buf = SC_A if j % 2 == 0 else SC_B
            kb.dma("sync", scr[:, buf:buf + wid],
                   src[j * 128:(j + 1) * 128, :], wait=[("V", bguard[buf])])
            r = kb.relay("sync")
            kb.v("tensor_scalar", scr[:, SC_T0:SC_T0 + wid],
                 scr[:, buf:buf + wid], -1.0, thr,
                 op0=ALU.mult, op1=ALU.is_gt, wait=[("FS", r)])
            if mat == 0:
                kb.v("scalar_tensor_tensor", dst[:, j * wid:(j + 1) * wid],
                     scr[:, buf:buf + wid], thr, scr[:, SC_T0:SC_T0 + wid],
                     op0=ALU.is_gt, op1=ALU.subtract, inc=[("V", 1)])
            else:
                kb.v("scalar_tensor_tensor", scr[:, SC_T0:SC_T0 + wid],
                     scr[:, buf:buf + wid], thr, scr[:, SC_T0:SC_T0 + wid],
                     op0=ALU.is_gt, op1=ALU.subtract)
                kb.v("tensor_scalar_mul", dst[:, j * wid:(j + 1) * wid],
                     scr[:, SC_T0:SC_T0 + wid], 0.5, inc=[("V", 1)])
            bguard[buf] = kb.now("V")
    Vq = kb.now("V")

    slot_relay = {0: 0, 1: 0}
    for mat in range(2):
        for grp in range(16):
            g = mat * 16 + grp
            slot = g % 2
            for k in range(8):
                tile = grp * 8 + k
                if mat == 0:
                    fl, dc = tile // 16, tile % 16
                    src = t["a32"][:, fl * 2048 + dc * 128:fl * 2048 + dc * 128 + 128]
                else:
                    dc, fl = tile // 8, tile % 8
                    src = t["b32"][:, dc * 1024 + fl * 128:dc * 1024 + fl * 128 + 128]
                w = []
                if k == 0:
                    w = [("V", kb.now("V"))] if g > 0 else [("V", Vq)]
                kb.tr(t["psT"][:, k * 128:(k + 1) * 128], src, t["idbf_t"],
                      wait=w, inc=([("T", 1)] if k == 7 else []))
            kb.v("tensor_copy", t["qst"][:, slot * 1024:slot * 1024 + 1024],
                 t["psT"][:, :],
                 wait=[("T", kb.now("T"))] +
                      ([("FS", slot_relay[slot])] if slot_relay[slot] else []),
                 inc=[("V", 1)])
            kb.dma("sync",
                   t["ag1_in"][mat, grp * 8:(grp + 1) * 8]
                   .rearrange("a p f -> p a f"),
                   t["qst"][:, slot * 1024:slot * 1024 + 1024]
                   .rearrange("p (a f) -> p a f", f=128),
                   wait=[("V", kb.now("V"))])
            slot_relay[slot] = kb.relay("sync")
    r = kb.relay("sync")
    kb.coll("AllGather", ALU.bypass, [t["ag1_in"]], [t["ag1_out"]],
            wait=[("FS", r)], inc=[("C", 1)])
    kb.fence()


def phase_xload(kb, t):
    scr, x, psU, idf = t["scr"], t["x"], t["psU"], t["idf_t"]
    vdone = {}
    for tt in range(8):
        buf = SC_A if tt % 2 == 0 else SC_B
        w = [("V", vdone[tt - 2])] if tt - 2 in vdone else []
        kb.dma("sync", scr[:, buf:buf + 2048],
               t["xs"][tt * 128:(tt + 1) * 128, :], wait=w)
        r = kb.relay("sync")
        for q in range(4):
            for k in range(4):
                dc = q * 4 + k
                kb.tr(psU[:, k * 128:(k + 1) * 128],
                      scr[:, buf + dc * 128:buf + dc * 128 + 128], idf,
                      wait=([("FS", r)] if q == 0 and k == 0 else []) +
                           ([("V", kb.now("V"))] if k == 0 else []),
                      inc=([("T", 1)] if k == 3 else []))
            out = x.rearrange("p (d u) -> p d u", u=T)[:, q * 4:(q + 1) * 4,
                                                       tt * 128:(tt + 1) * 128]
            kb.v("tensor_copy", out,
                 psU[:, 0:512].rearrange("p (d u) -> p d u", u=128),
                 wait=[("T", kb.now("T"))], inc=[("V", 1)])
        vdone[tt] = kb.now("V")
    kb.fence()


def phase_ln(kb, t):
    x, scr, a32 = t["x"], t["scr"], t["a32"]
    ps0, ps2, psU = t["ps0"], t["ps2"], t["psU"]
    onesb, onesf = t["onesb_t"], t["onesf_t"]
    loc = {("sx", 0): (ps2, 0), ("sx", 1): (ps2, 512),
           ("sq", 0): (ps0, 0), ("sq", 1): (ps0, 512)}
    for th in range(2):
        for dc in range(NDC):
            it = th * NDC + dc
            sl = x[:, dc * T + th * 512: dc * T + th * 512 + 512]
            cb = (dc % 2) * 512
            kb.v("tensor_copy", t["expb"][:, cb:cb + 512], sl,
                 wait=([("T", kb.now("T"))] if it >= 2 else []), inc=[("V", 1)])
            kb.act(t["qst"][:, cb:cb + 512], sl, AF.Square,
                   wait=([("T", kb.now("T"))] if it >= 2 else []), inc=[("S", 1)])
            pp, c0 = loc[("sx", th)]
            kb.mm(pp[0:1, c0:c0 + 512], onesb[:, 0:1], t["expb"][:, cb:cb + 512],
                  start=(dc == 0), stop=(dc == NDC - 1),
                  wait=[("V", kb.now("V"))])
            pp, c0 = loc[("sq", th)]
            kb.mm(pp[0:1, c0:c0 + 512], onesb[:, 0:1], t["qst"][:, cb:cb + 512],
                  start=(dc == 0), stop=(dc == NDC - 1),
                  wait=[("S", kb.now("S"))], inc=[("T", 1)])
    Tst = kb.now("T")
    for th in range(2):
        pp, c0 = loc[("sx", th)]
        kb.v("tensor_scalar_mul", scr[0:1, SC_T0 + th * 512:SC_T0 + th * 512 + 512],
             pp[0:1, c0:c0 + 512], 1.0 / E, wait=[("T", Tst)])
        pp, c0 = loc[("sq", th)]
        kb.v("tensor_scalar_mul", scr[0:1, SC_T1 + th * 512:SC_T1 + th * 512 + 512],
             pp[0:1, c0:c0 + 512], 1.0 / E)
    kb.v("tensor_mul", scr[0:1, SC_NMR:SC_NMR + 1024], scr[0:1, SC_T0:SC_T0 + 1024],
         scr[0:1, SC_T0:SC_T0 + 1024])
    kb.v("tensor_sub", scr[0:1, SC_T1:SC_T1 + 1024], scr[0:1, SC_T1:SC_T1 + 1024],
         scr[0:1, SC_NMR:SC_NMR + 1024])
    kb.v("tensor_scalar_add", scr[0:1, SC_T1:SC_T1 + 1024],
         scr[0:1, SC_T1:SC_T1 + 1024], 1e-5, inc=[("V", 1)])
    kb.act(scr[0:1, SC_T1:SC_T1 + 1024], scr[0:1, SC_T1:SC_T1 + 1024], AF.Ln,
           wait=[("V", kb.now("V"))], inc=[("S", 1)])
    kb.act(scr[0:1, SC_T1:SC_T1 + 1024], scr[0:1, SC_T1:SC_T1 + 1024], AF.Exp,
           scale=-0.5, inc=[("S", 1)])
    kb.v("tensor_mul", scr[0:1, SC_T0:SC_T0 + 1024], scr[0:1, SC_T0:SC_T0 + 1024],
         scr[0:1, SC_T1:SC_T1 + 1024], wait=[("S", kb.now("S"))])
    kb.v("tensor_scalar_mul", scr[0:1, SC_T0:SC_T0 + 1024],
         scr[0:1, SC_T0:SC_T0 + 1024], -1.0, inc=[("V", 1)])
    for th in range(2):
        kb.mm(psU[:, 0:512], onesf[0:1, :],
              scr[0:1, SC_T1 + th * 512:SC_T1 + th * 512 + 512],
              wait=[("V", kb.now("V"))], inc=[("T", 1)])
        kb.v("tensor_copy", scr[:, SC_RSTD + th * 512:SC_RSTD + th * 512 + 512],
             psU[:, 0:512], wait=[("T", kb.now("T"))], inc=[("V", 1)])
        kb.mm(psU[:, 0:512], onesf[0:1, :],
              scr[0:1, SC_T0 + th * 512:SC_T0 + th * 512 + 512],
              wait=[("V", kb.now("V"))], inc=[("T", 1)])
        kb.v("tensor_copy", scr[:, SC_NMR + th * 512:SC_NMR + th * 512 + 512],
             psU[:, 0:512], wait=[("T", kb.now("T"))], inc=[("V", 1)])
    Vbc = kb.now("V")
    for dc in range(NDC):
        for th in range(2):
            sl = x[:, dc * T + th * 512: dc * T + th * 512 + 512]
            kb.v("tensor_mul", scr[:, SC_T1:SC_T1 + 512], sl,
                 scr[:, SC_RSTD + th * 512:SC_RSTD + th * 512 + 512],
                 wait=([("V", Vbc)] if dc == 0 and th == 0 else []))
            kb.v("tensor_add", a32[:, dc * T + th * 512: dc * T + th * 512 + 512],
                 scr[:, SC_T1:SC_T1 + 512],
                 scr[:, SC_NMR + th * 512:SC_NMR + th * 512 + 512],
                 inc=[("V", 1)])
    kb.fence()


def phase_ffn(kb, t):
    a32, b32, x, wstr = t["a32"], t["b32"], t["x"], t["wstr"]
    ps0, ps1 = t["ps0"], t["ps1"]
    ag1 = t["ag1_out"]
    for th in range(2):
        for fh in range(2):
            tdone, sdone = {}, {}
            for fp in range(16):
                for half in range(2):
                    fl = fp * 2 + half
                    fc = fh * 32 + fl
                    rk, fli = fc // 8, fc % 8
                    ws = (fl % 2) * 4096
                    w = [("T", tdone[fl - 2])] if fl - 2 in tdone else []
                    kb.dma("sync", wstr[:, ws:ws + 2048]
                           .rearrange("p (a f) -> p a f", f=128),
                           ag1[rk, 0, fli * 16:(fli + 1) * 16]
                           .rearrange("a p f -> p a f"), wait=w)
                r = kb.relay("sync")
                for half in range(2):
                    fl = fp * 2 + half
                    fc = fh * 32 + fl
                    ws = (fl % 2) * 4096
                    pslot = (fl % 2) * 512
                    for dc in range(NDC):
                        w = []
                        if dc == 0:
                            w = [("FS", r)]
                            if fl - 2 in sdone:
                                w.append(("S", sdone[fl - 2]))
                        kb.mm(ps0[:, pslot:pslot + 512],
                              wstr[:, ws + dc * 128:ws + dc * 128 + 128],
                              a32[:, dc * T + th * 512:dc * T + th * 512 + 512],
                              start=(dc == 0), stop=(dc == NDC - 1),
                              wait=w, inc=([("T", 1)] if dc == NDC - 1 else []))
                    tdone[fl] = kb.now("T")
                    kb.act(b32[:, fl * 512:fl * 512 + 512], ps0[:, pslot:pslot + 512],
                           AF.Gelu, bias=t["b1_t"][:, fc:fc + 1],
                           wait=[("T", tdone[fl])], inc=[("S", 1)])
                    sdone[fl] = kb.now("S")
            Sh = kb.now("S")
            tdone2, vdone2 = {}, {}
            for dc in range(NDC):
                ws = (dc % 2) * 4096
                w = [("T", tdone2[dc - 2])] if dc - 2 in tdone2 else \
                    [("T", kb.now("T"))]
                for rr in range(4):
                    rk = fh * 4 + rr
                    kb.dma("sync", wstr[:, ws + rr * 1024:ws + rr * 1024 + 1024]
                           .rearrange("p (a f) -> p a f", f=128),
                           ag1[rk, 1, dc * 8:(dc + 1) * 8]
                           .rearrange("a p f -> p a f"),
                           wait=(w if rr == 0 else []))
                r = kb.relay("sync")
                pslot = (dc % 2) * 512
                for fl in range(32):
                    w = []
                    if fl == 0:
                        w = [("FS", r), ("S", Sh)]
                        if dc - 2 in vdone2:
                            w.append(("V", vdone2[dc - 2]))
                    kb.mm(ps1[:, pslot:pslot + 512],
                          wstr[:, ws + fl * 128:ws + fl * 128 + 128],
                          b32[:, fl * 512:fl * 512 + 512],
                          start=(fl == 0), stop=(fl == 31),
                          wait=w, inc=([("T", 1)] if fl == 31 else []))
                tdone2[dc] = kb.now("T")
                xs = x[:, dc * T + th * 512: dc * T + th * 512 + 512]
                if fh == 0:
                    kb.v("scalar_tensor_tensor", xs, ps1[:, pslot:pslot + 512],
                         t["b2h_t"][:, dc:dc + 1], xs, op0=ALU.add, op1=ALU.add,
                         wait=[("T", tdone2[dc])], inc=[("V", 1)])
                else:
                    kb.v("tensor_add", xs, xs, ps1[:, pslot:pslot + 512],
                         wait=[("T", tdone2[dc])], inc=[("V", 1)])
                vdone2[dc] = kb.now("V")
            kb.fence()


def phase_qkv(kb, t):
    a32, scr, qst, wstr = t["a32"], t["scr"], t["qst"], t["wstr"]
    ps0, ps2, psT = t["ps0"], t["ps2"], t["psT"]
    wsrc = [t["wq"], t["wk"], t["wv"]]
    tdone, edone, urelay = {}, {}, {}
    for pi in range(3):
        for ec in range(H):
            u = pi * H + ec
            ws = (u % 2) * 4096
            w = [("T", tdone[u - 2])] if u - 2 in tdone else []
            for piece in range(2):
                kb.dma("gpsimd", wstr[:, ws + piece * 1024:ws + piece * 1024 + 1024]
                       .rearrange("p (a f) -> p a f", f=128),
                       wsrc[pi][piece * 1024:(piece + 1) * 1024,
                                ec * 128:(ec + 1) * 128]
                       .rearrange("(a p) f -> p a f", p=128),
                       wait=(w if piece == 0 else []))
            r = kb.relay("gpsimd")
            for th in range(2):
                for dc in range(NDC):
                    w = []
                    if th == 0 and dc == 0:
                        w = [("FG", r)]
                        if u - 1 in edone:
                            w.append(("S", edone[u - 1][0]))
                            w.append(("V", edone[u - 1][1]))
                    kb.mm(ps0[:, th * 512:th * 512 + 512],
                          wstr[:, ws + dc * 128:ws + dc * 128 + 128],
                          a32[:, dc * T + th * 512:dc * T + th * 512 + 512],
                          start=(dc == 0), stop=(dc == NDC - 1), wait=w,
                          inc=([("T", 1)] if th == 1 and dc == NDC - 1 else []))
            tdone[u] = kb.now("T")
            qs = (u % 2) * 1024
            wsl = [("FS", urelay[u - 2])] if u - 2 in urelay else []
            wst = [("FS", urelay[u - 1])] if u - 1 in urelay else []
            if DBG_SKIP_QKV_TAIL:
                kb.scopy(qst[:, qs:qs + 1024], ps0[:, :], wait=[("T", tdone[u])],
                         inc=[("S", 1)])
                edone[u] = (kb.now("S"), kb.now("V"))
            elif pi < 2:
                kb.scopy(scr[:, SC_T1:SC_T1 + 1024], ps0[:, :],
                         wait=[("T", tdone[u])] +
                              ([("FS", urelay[u - 1])] if u - 1 in urelay else []),
                         inc=[("S", 1)])
                Sev = kb.now("S")
                kb.dma("sync", scr[0:64, SC_NMR:SC_NMR + 1024],
                       scr[64:128, SC_T1:SC_T1 + 1024], wait=[("S", Sev)])
                kb.dma("sync", scr[64:128, SC_NMR:SC_NMR + 1024],
                       scr[0:64, SC_T1:SC_T1 + 1024])
                rsw = kb.relay("sync")
                kb.v("tensor_mul", scr[:, SC_A:SC_A + 1024],
                     scr[:, SC_T1:SC_T1 + 1024], t["cos"][:, :],
                     wait=[("S", Sev)])
                kb.v("tensor_mul", scr[:, SC_B:SC_B + 1024],
                     scr[:, SC_NMR:SC_NMR + 1024], t["sin"][:, :],
                     wait=[("FS", rsw)])
                kb.v("tensor_add", qst[:, qs:qs + 1024], scr[:, SC_A:SC_A + 1024],
                     scr[:, SC_B:SC_B + 1024], inc=[("V", 1)], wait=wsl)
                kb.act(scr[:, SC_A:SC_A + 1024], qst[:, qs:qs + 1024], AF.Square,
                       wait=[("V", kb.now("V"))], inc=[("S", 1)])
                for th in range(2):
                    kb.mm(ps2[0:1, th * 512:th * 512 + 512], t["onesf_t"][:, 0:1],
                          scr[:, SC_A + th * 512:SC_A + th * 512 + 512],
                          wait=([("S", kb.now("S")), ("V", kb.now("V"))]
                                if th == 0 else []),
                          inc=([("T", 1)] if th == 1 else []))
                kb.v("reduce_max", t["small"][0:1, 8 + pi * H + ec:9 + pi * H + ec],
                     ps2[0:1, 0:1024], axis=AX.X,
                     wait=[("T", kb.now("T"))], inc=[("V", 1)])
                kb.dma("sync",
                       t["ag2_in"][pi, ec].rearrange("(p u) -> p u", p=128),
                       qst[:, qs:qs + 1024], wait=[("V", kb.now("V"))])
                urelay[u] = kb.relay("sync")
                edone[u] = (kb.now("S"), kb.now("V"))
            else:
                kb.scopy(qst[:, qs:qs + 1024], ps0[:, :],
                         wait=[("T", tdone[u])] + wsl, inc=[("S", 1)])
                for tc in range(8):
                    kb.tr(psT[:, tc * 128:(tc + 1) * 128],
                          qst[:, qs + tc * 128:qs + tc * 128 + 128], t["idbf_t"],
                          wait=([("S", kb.now("S")), ("V", kb.now("V"))]
                                if tc == 0 else []),
                          inc=([("T", 1)] if tc == 7 else []))
                kb.v("tensor_copy", t["expb"][:, :], psT[:, :],
                     wait=[("T", kb.now("T"))] + wst, inc=[("V", 1)])
                kb.dma("sync",
                       t["ag2_in"][pi, ec].rearrange("(p a u) -> p a u", p=128, a=8),
                       t["expb"][:, :].rearrange("p (a u) -> p a u", a=8),
                       wait=[("V", kb.now("V"))])
                urelay[u] = kb.relay("sync")
                edone[u] = (kb.now("S"), kb.now("V"))
    kb.dma("sync", t["nmx_in"].rearrange("(h c) -> h c", h=1),
           t["small"][0:1, 8:8 + 2 * H], wait=[("V", kb.now("V"))])
    r = kb.relay("sync")
    r2 = kb.relay("gpsimd")
    kb.coll("AllGather", ALU.bypass, [t["ag2_in"]], [t["ag2_out"]],
            wait=[("FS", r), ("FG", r2)], inc=[("C", 1)])
    kb.coll("AllGather", ALU.bypass, [t["nmx_in"]], [t["nmx_out"]],
            inc=[("C", 1)])
    kb.fence()


def phase_attention(kb, t):
    b32, scr, small = t["b32"], t["scr"], t["small"]
    ps0, ps1, ps2, psU = t["ps0"], t["ps1"], t["ps2"], t["psU"]
    ag2 = t["ag2_out"]
    onesf, onesb = t["onesf_t"], t["onesb_t"]

    for lh in range(NHL):
        kb.dma_dyn("sync",
                   b32[:, lh * S:(lh + 1) * S].rearrange("p (r u) -> p r u", u=T),
                   lambda pid, lh=lh:
                   ag2[0:NCORE, 0, bass.ds(kb.hsel(lh), 1)]
                   .rearrange("r h (p u) -> p (r h) u", p=128))
    for lh in range(NHL):
        kb.dma_dyn("sync",
                   t["kvl"][0, lh].rearrange("p (r u) -> p r u", u=T),
                   lambda pid, lh=lh:
                   ag2[0:NCORE, 1, bass.ds(kb.hsel(lh), 1)]
                   .rearrange("r h (p u) -> p (r h) u", p=128))
        kb.dma_dyn("sync",
                   t["kvl"][1, lh].rearrange("p (r g) -> p r g", g=T),
                   lambda pid, lh=lh:
                   ag2[0:NCORE, 2, bass.ds(kb.hsel(lh), 1)]
                   .rearrange("r h (p a u) -> p (r h) (a u)", p=128, a=8))
    kb.dma("sync", scr[0:16, SC_T0:SC_T0 + 16],
           t["nmx_out"].rearrange("r (p h) -> h (r p)", p=2), slow=True)
    r = kb.relay("sync")
    kb.v("reduce_max", scr[0:16, SC_T1:SC_T1 + 1],
         scr[0:16, SC_T0:SC_T0 + 16:2], axis=AX.X, wait=[("FS", r)])
    kb.v("reduce_max", scr[0:16, SC_T1 + 1:SC_T1 + 2],
         scr[0:16, SC_T0 + 1:SC_T0 + 16:2], axis=AX.X)
    kb.v("tensor_mul", scr[0:16, SC_T1 + 2:SC_T1 + 3],
         scr[0:16, SC_T1:SC_T1 + 1], scr[0:16, SC_T1 + 1:SC_T1 + 2],
         inc=[("V", 1)])
    kb.act(scr[0:16, SC_T1 + 2:SC_T1 + 3], scr[0:16, SC_T1 + 2:SC_T1 + 3],
           AF.Sqrt, wait=[("V", kb.now("V"))], inc=[("S", 1)])
    kb.v("tensor_scalar_mul", scr[0:16, SC_T1 + 2:SC_T1 + 3],
         scr[0:16, SC_T1 + 2:SC_T1 + 3], -ISD * 1.001,
         wait=[("S", kb.now("S"))], inc=[("V", 1)])
    kb.dma("sync", t["negc_d"].rearrange("(h c) -> h c", c=1),
           scr[0:16, SC_T1 + 2:SC_T1 + 3], wait=[("V", kb.now("V"))])
    r = kb.relay("sync")
    for lh in range(NHL):
        kb.dma_dyn("sync", scr[0:1, SC_T0 + 40 + lh:SC_T0 + 41 + lh],
                   lambda pid, lh=lh: t["negc_d"]
                   [bass.ds(kb.hsel(lh), 1)].rearrange("(h c) -> h c", c=1),
                   wait=([("FS", r)] if lh == 0 else []))
    r = kb.relay("sync")
    for lh in range(NHL):
        kb.mm(psU[:, 200 + lh:201 + lh], onesf[0:1, :],
              scr[0:1, SC_T0 + 40 + lh:SC_T0 + 41 + lh],
              wait=([("FS", r)] if lh == 0 else []), inc=[("T", 1)])
        kb.v("tensor_copy", small[:, SM_NEGC0 + lh:SM_NEGC0 + lh + 1],
             psU[:, 200 + lh:201 + lh], wait=[("T", kb.now("T"))], inc=[("V", 1)])
    kb.fence()

    piece_tcons, piece_r = {}, {}
    ps0_free = {0: 0, 1: 0}
    expb_free = {0: 0, 1: 0}
    av_free = {0: 0, 1: 0}
    sum_free = {0: 0, 512: 0}
    gseq = 0
    for lh in range(NHL):
        for i in range(16):
            nkb = 4 * (i + 1)
            tile = lh * 16 + i
            sumcol = (tile % 2) * 512
            avs = (tile % 2) * 512
            qsl = b32[:, lh * S + i * 512: lh * S + i * 512 + 512]
            base_gseq = gseq
            for kb_i in range(nkb):
                if kb_i % 8 == 0:
                    p = gseq
                    gseq += 1
                    rk = kb_i // 8
                    ks = (p % 2) * 1024
                    w = [("T", piece_tcons[p - 2])] if p - 2 in piece_tcons else []
                    kb.dma("sync", t["kbuf"][:, ks:ks + 1024],
                           t["kvl"][0, lh][:, rk * T:(rk + 1) * T], wait=w)
                    kb.dma("sync", t["vbuf"][:, ks:ks + 1024],
                           t["kvl"][1, lh][:, rk * T:(rk + 1) * T])
                    piece_r[p] = kb.relay("sync")
                p = base_gseq + kb_i // 8
                ks = (p % 2) * 1024
                koff = ks + (kb_i % 8) * 128
                ss = (kb_i % 2) * 512
                es = (kb_i % 2) * 512
                w = [("FS", piece_r[p])]
                if ps0_free[kb_i % 2]:
                    w.append(("S", ps0_free[kb_i % 2]))
                kb.mm(ps0[:, ss:ss + 512], t["kbuf"][:, koff:koff + 128], qsl,
                      start=True, stop=True, wait=w, inc=[("T", 1)])
                tsc = kb.now("T")
                kb.act(t["expb"][:, es:es + 512], ps0[:, ss:ss + 512], AF.Exp,
                       bias=small[:, SM_NEGC0 + lh:SM_NEGC0 + lh + 1], scale=ISD,
                       wait=[("T", max(tsc, expb_free[kb_i % 2]))], inc=[("S", 1)])
                ps0_free[kb_i % 2] = kb.now("S")
                diag = kb_i >= nkb - 4
                if diag:
                    j = kb_i - (nkb - 4)
                    kb.v("tensor_mul", t["expb"][:, es:es + 512],
                         t["expb"][:, es:es + 512],
                         t["mask"][:, j * 512:(j + 1) * 512],
                         wait=[("S", kb.now("S"))], inc=[("V", 1)])
                    gate = [("V", kb.now("V"))]
                else:
                    gate = [("S", kb.now("S"))]
                w = list(gate)
                if kb_i == 0 and sum_free[sumcol]:
                    w.append(("V", sum_free[sumcol]))
                kb.mm(ps2[0:1, sumcol:sumcol + 512], onesb[:, 0:1],
                      t["expb"][:, es:es + 512],
                      start=(kb_i == 0), stop=(kb_i == nkb - 1), wait=w)
                w = []
                if kb_i == 0 and av_free[tile % 2]:
                    w.append(("V", av_free[tile % 2]))
                kb.mm(ps1[:, avs:avs + 512], t["vbuf"][:, koff:koff + 128],
                      t["expb"][:, es:es + 512],
                      start=(kb_i == 0), stop=(kb_i == nkb - 1),
                      wait=w, inc=[("T", 1)])
                expb_free[kb_i % 2] = kb.now("T")
                piece_tcons[p] = kb.now("T")
            Ttile = kb.now("T")
            kb.v("tensor_copy", scr[0:1, SC_T0:SC_T0 + 512],
                 ps2[0:1, sumcol:sumcol + 512], wait=[("T", Ttile)])
            kb.v("reciprocal", scr[0:1, SC_T0:SC_T0 + 512],
                 scr[0:1, SC_T0:SC_T0 + 512], inc=[("V", 1)])
            sum_free[sumcol] = kb.now("V")
            kb.mm(psU[:, 0:512], onesf[0:1, :], scr[0:1, SC_T0:SC_T0 + 512],
                  wait=[("V", kb.now("V"))], inc=[("T", 1)])
            kb.v("tensor_copy", scr[:, SC_NMR:SC_NMR + 512], psU[:, 0:512],
                 wait=[("T", kb.now("T"))])
            kb.v("tensor_mul", qsl, ps1[:, avs:avs + 512],
                 scr[:, SC_NMR:SC_NMR + 512], inc=[("V", 1)])
            av_free[tile % 2] = kb.now("V")
    Vattn = kb.now("V")

    for lh in range(NHL):
        kb.dma("sync", t["ag3_in"][lh], b32[:, lh * S:(lh + 1) * S],
               wait=[("V", Vattn)])
    r = kb.relay("sync")
    kb.coll("AllGather", ALU.bypass, [t["ag3_in"]], [t["ag3_out"]],
            wait=[("FS", r)], inc=[("C", 1)])
    Cag3 = kb.now("C")
    kb.dma_dyn("sync", t["a32"][:, :].rearrange("p (g u) -> p g u", u=T),
               lambda pid: t["ag3_out"][0:NCORE, 0:NHL, :, bass.ds(kb.town(), T)]
               .rearrange("r j p u -> p (r j) u"),
               wait=[("C", Cag3)])
    kb.relay("sync")
    kb.fence()

    tdone, vdone = {}, {}
    for dc in range(NDC):
        ws = (dc % 2) * 4096
        w = [("T", tdone[dc - 2])] if dc - 2 in tdone else []
        for piece in range(2):
            kb.dma("gpsimd", t["wstr"][:, ws + piece * 1024:ws + piece * 1024 + 1024]
                   .rearrange("p (a f) -> p a f", f=128),
                   t["wo"][piece * 1024:(piece + 1) * 1024, dc * 128:(dc + 1) * 128]
                   .rearrange("(a p) f -> p a f", p=128),
                   wait=(w if piece == 0 else []))
        r = kb.relay("gpsimd")
        for th in range(2):
            pslot = ((dc * 2 + th) % 2) * 512
            for ec in range(H):
                w = []
                if ec == 0:
                    w = [("FG", r)]
                    k = dc * 2 + th - 2
                    if k in vdone:
                        w.append(("V", vdone[k]))
                kb.mm(t["ps1"][:, pslot:pslot + 512],
                      t["wstr"][:, ws + ec * 128:ws + ec * 128 + 128],
                      t["a32"][:, ec * T + th * 512:ec * T + th * 512 + 512],
                      start=(ec == 0), stop=(ec == H - 1),
                      wait=w, inc=([("T", 1)] if ec == H - 1 else []))
            xs = t["x"][:, dc * T + th * 512: dc * T + th * 512 + 512]
            kb.v("tensor_add", xs, xs, t["ps1"][:, pslot:pslot + 512],
                 wait=[("T", kb.now("T"))], inc=[("V", 1)])
            vdone[dc * 2 + th] = kb.now("V")
        tdone[dc] = kb.now("T")
    kb.fence()


def phase_store(kb, t, src, dtype):
    scr = t["scr"]
    ident = t["idf_t"] if dtype == F32 else t["idbf_t"]
    psx = t["psU"] if dtype == F32 else t["psT"]
    rdone = {}
    for tt in range(8):
        buf = SC_A if tt % 2 == 0 else SC_B
        for q in range(4):
            for k in range(4):
                dc = q * 4 + k
                sl = src[:, dc * T + tt * 128: dc * T + tt * 128 + 128]
                w = []
                if k == 0:
                    w = [("V", kb.now("V"))]
                    if tt - 2 in rdone and q == 0:
                        w.append(("FS", rdone[tt - 2]))
                kb.tr(psx[:, k * 128:(k + 1) * 128], sl, ident,
                      wait=w, inc=([("T", 1)] if k == 3 else []))
            kb.v("tensor_copy", scr[:, buf + q * 512:buf + q * 512 + 512],
                 psx[:, 0:512],
                 wait=[("T", kb.now("T"))] +
                      ([("FS", rdone[tt - 2])] if tt - 2 in rdone else []),
                 inc=[("V", 1)])
        kb.dma("sync", t["out"][tt * 128:(tt + 1) * 128, :],
               scr[:, buf:buf + 2048], wait=[("V", kb.now("V"))])
        rdone[tt] = kb.relay("sync")
    kb.fence()


def emit_stream(kb, t):
    stage = DEBUG_STAGE
    phase_consts(kb, t)
    if stage >= 2:
        phase_wprep(kb, t)
    phase_xload(kb, t)
    if stage == 0:
        phase_store(kb, t, t["x"], F32)
        return
    phase_ln(kb, t)
    if stage == 1:
        phase_store(kb, t, t["a32"], BF16)
        return
    phase_ffn(kb, t)
    if stage == 2:
        phase_store(kb, t, t["x"], F32)
        return
    phase_ln(kb, t)
    phase_qkv(kb, t)
    if stage == 3:
        phase_store(kb, t, t["x"], F32)
        return
    phase_attention(kb, t)
    if stage == 4:
        phase_store(kb, t, t["x"], F32)
        return
    phase_ln(kb, t)
    phase_ffn(kb, t)
    phase_store(kb, t, t["x"], F32)


def build_nc():
    nc = bass.Bass("TRN2", target_bir_lowering=False, debug=False)
    es = ExitStack()
    t = declare(nc, es)
    sems = {}
    for name in SEMS:
        sems[name] = es.enter_context(nc.semaphore("sem" + name))
    block = es.enter_context(nc.Block())
    counts = {}

    def make(eng_name):
        def run(eng):
            kbx = KB(nc, eng_name, eng, sems, t)
            emit_stream(kbx, t)
            counts[eng_name] = dict(kbx.c)
        return run

    for eng_name in ENGS:
        getattr(block, eng_name)(make(eng_name))
    vals = list(counts.values())
    for v in vals[1:]:
        assert v == vals[0], f"sem count mismatch: {counts}"
    es.close()
    return nc


def host_inputs(full):
    x = np.asarray(full["L0_attn_x"]).reshape(S, E)
    w1 = np.asarray(full["L0_ffn_w1"])
    w2 = np.asarray(full["L0_ffn_w2"])
    b1 = np.ascontiguousarray(np.asarray(full["L0_ffn_b1"]).reshape(64, 128))
    b2h = np.ascontiguousarray(
        (np.asarray(full["L0_ffn_b2"]) * 0.5).astype(np.float32).reshape(16, 128))

    pos = np.arange(S, dtype=np.float64)
    inv = 1.0 / (10000.0 ** (np.arange(0, D, 2, dtype=np.float64) / D))
    ang = pos[:, None] * inv[None, :]
    cos = np.cos(ang).T
    sin = np.sin(ang).T
    cosf = np.concatenate([cos, cos], 0).astype(np.float32)
    sinf = np.concatenate([-sin, sin], 0).astype(np.float32)

    kk = np.arange(128)[:, None]
    qq = np.arange(512)[None, :]
    masks = np.stack([(qq >= kk + 128 * j) for j in range(4)]) \
        .astype(ml_dtypes.bfloat16)

    shared = {
        "wq": np.asarray(full["L0_attn_wq"]), "wk": np.asarray(full["L0_attn_wk"]),
        "wv": np.asarray(full["L0_attn_wv"]), "wo": np.asarray(full["L0_attn_wo"]),
        "b1": b1, "b2h": b2h, "masks": np.asarray(masks),
        "idbf": np.asarray(np.eye(128, dtype=ml_dtypes.bfloat16)),
        "idf": np.eye(128, dtype=np.float32),
        "onesf": np.ones((128, 128), np.float32),
        "onesb": np.asarray(np.ones((128, 8), ml_dtypes.bfloat16)),
    }
    maps = []
    for c in range(NCORE):
        m = dict(shared)
        m["xs"] = np.ascontiguousarray(x[c * T:(c + 1) * T])
        m["w1s"] = np.ascontiguousarray(w1[c * FSL:(c + 1) * FSL])
        m["w2s"] = np.ascontiguousarray(w2[:, c * FSL:(c + 1) * FSL])
        m["cosf"] = np.ascontiguousarray(cosf[:, c * T:(c + 1) * T])
        m["sinf"] = np.ascontiguousarray(sinf[:, c * T:(c + 1) * T])
        maps.append(m)
    return maps


def kernel(**inputs):
    maps = host_inputs(inputs)
    nc = build_nc()
    res = run_bass_kernel_spmd(nc, maps, core_ids=list(range(NCORE)))
    out = np.concatenate([res.results[c]["out"] for c in range(NCORE)], axis=0)
    return np.ascontiguousarray(out.reshape(1, S, E).astype(np.float32))


# version 08
# speedup vs baseline: 1.3141x; 1.3141x over previous
import numpy as np
import ml_dtypes
import concourse.bass as bass
import concourse.mybir as mybir
from concourse.bass_utils import run_bass_kernel_spmd
from contextlib import ExitStack

F32 = mybir.dt.float32
BF16 = mybir.dt.bfloat16
AF = mybir.ActivationFunctionType
ALU = mybir.AluOpType
AX = mybir.AxisListType

NCORE = 8
S, E, H, D, FF = 8192, 2048, 16, 128, 8192
T = S // NCORE
NDC = E // 128
FSL = FF // NCORE
NHL = 2
ISD = float(1.0 / np.sqrt(128.0))

DEBUG_STAGE = 99
DBG_SKIP_AG2 = False
DBG_SKIP_QKV_TAIL = False

ENGS = ("tensor", "vector", "scalar", "sync", "gpsimd")
SEMS = ("L", "L0", "L1", "L2", "G", "G0", "G1", "T", "V", "S", "C", "FS", "FG")

SC_A = 0
SC_B = 2048
SC_RSTD = 4096
SC_NMR = 5120
SC_T0 = 6144
SC_T1 = 7168
SM_THR1, SM_THR2, SM_NEGC0, SM_NEGC1, SM_ABS1, SM_ABS2 = 0, 1, 2, 3, 4, 5
SM_M0 = 6
SM_RES = 40


class KB:
    def __init__(self, nc, which, eng, sems, t):
        self.nc, self.which, self.eng, self.sems, self.t = nc, which, eng, sems, t
        self.c = {k: 0 for k in SEMS}
        self._pidv = None

    def pid(self):
        if self._pidv is None:
            self._pidv = self.eng.partition_id()
        return self._pidv

    def hsel(self, lh):
        if not hasattr(self, "_hsel"):
            p = self.pid()
            self._hsel = [p * NHL + j for j in range(NHL)]
        return self._hsel[lh]

    def town(self):
        if not hasattr(self, "_town"):
            self._town = self.pid() * T
        return self._town

    def on(self, name):
        return self.which == name

    def now(self, s):
        return self.c[s]

    def _go(self, engname, fn, inc, wait):
        if self.which == engname:
            for s, v in wait:
                if v > 0:
                    self.eng.wait_ge(self.sems[s], v)
            ins = fn(self.eng)
            for s, n in inc:
                ins = ins.then_inc(self.sems[s], n)
        for s, n in inc:
            self.c[s] += n

    def relay(self, q, qsem=None, extra_wait=()):
        if qsem is None:
            qsem = "L" if q == "sync" else "G"
        rsem = "FS" if q == "sync" else "FG"
        if self.which == q:
            for s, v in extra_wait:
                if v > 0:
                    self.eng.wait_ge(self.sems[s], v)
            if self.c[qsem] > 0:
                self.eng.wait_ge(self.sems[qsem], self.c[qsem])
            self.eng.sem_inc(self.sems[rsem], 1)
        self.c[rsem] += 1
        return self.c[rsem]

    def fence(self):
        if self.which == "sync":
            for s in ("L", "L0", "L1", "L2"):
                if self.c[s] > 0:
                    self.eng.wait_ge(self.sems[s], self.c[s])
            self.eng.sem_inc(self.sems["FS"], 1)
        if self.which == "gpsimd":
            for s in ("G", "G0", "G1"):
                if self.c[s] > 0:
                    self.eng.wait_ge(self.sems[s], self.c[s])
            self.eng.sem_inc(self.sems["FG"], 1)
        self.c["FS"] += 1
        self.c["FG"] += 1
        tgt = [(s, self.c[s]) for s in ("FS", "FG", "T", "V", "S", "C")]
        for s, v in tgt:
            if v > 0:
                self.eng.wait_ge(self.sems[s], v)

    def mm(self, out, lhsT, rhs, start=True, stop=True, inc=(), wait=()):
        self._go("tensor",
                 lambda e: e.matmul(out, lhsT, rhs, start=start, stop=stop),
                 inc, wait)

    def tr(self, out, in_, ident, inc=(), wait=()):
        if not isinstance(ident, bass.AP):
            ident = ident[:, :]
        self._go("tensor", lambda e: e.transpose(out, in_, ident), inc, wait)

    def act(self, out, in_, func, bias=0.0, scale=1.0, inc=(), wait=()):
        w = [("S", self.c["S"])] + list(wait)
        ii = [x for x in inc if x[0] != "S"] + [("S", 1)]
        self._go("scalar",
                 lambda e: e.activation(out=out, in_=in_, func=func,
                                        bias=bias, scale=scale), ii, w)

    def scopy(self, out, in_, inc=(), wait=()):
        w = [("S", self.c["S"])] + list(wait)
        ii = [x for x in inc if x[0] != "S"] + [("S", 1)]
        self._go("scalar", lambda e: e.copy(out=out, in_=in_), ii, w)

    def v(self, method, *a, inc=(), wait=(), **kw):
        w = [("V", self.c["V"])] + list(wait)
        ii = [x for x in inc if x[0] != "V"] + [("V", 1)]
        self._go("vector", lambda e: getattr(e, method)(*a, **kw), ii, w)

    def dma(self, q, out, in_, inc=None, wait=(), slow=False, qsem=None):
        if inc is None:
            inc = [((qsem or ("L" if q == "sync" else "G")), 16)]
        self._go(q, lambda e: e.dma_start(out=out, in_=in_,
                                          allow_slow_non_contiguous=slow),
                 inc, wait)

    def dma_dyn(self, q, out, in_fn, inc=None, wait=(), qsem=None):
        if inc is None:
            inc = [((qsem or ("L" if q == "sync" else "G")), 16)]
        self._go(q, lambda e: e.dma_start(out=out, in_=in_fn(self.pid())),
                 inc, wait)

    def coll(self, kind, op, ins, outs, inc=(), wait=()):
        self._go("gpsimd",
                 lambda e: e.collective_compute(
                     kind, op, replica_groups=[list(range(NCORE))],
                     ins=[a.opt() for a in ins], outs=[a.opt() for a in outs]),
                 inc, wait)


def declare(nc, es):
    t = {}
    din = lambda n, sh, dt: nc.dram_tensor(n, sh, dt, kind="ExternalInput").ap()
    t["xs"] = din("xs", [T, E], F32)
    t["w1s"] = din("w1s", [FSL, E], F32)
    t["w2s"] = din("w2s", [E, FSL], F32)
    t["wq"] = din("wq", [E, E], F32)
    t["wk"] = din("wk", [E, E], F32)
    t["wv"] = din("wv", [E, E], F32)
    t["wo"] = din("wo", [E, E], F32)
    t["b1"] = din("b1", [64, 128], F32)
    t["b2h"] = din("b2h", [16, 128], F32)
    t["cosf"] = din("cosf", [128, T], F32)
    t["sinf"] = din("sinf", [128, T], F32)
    t["masks"] = din("masks", [4, 128, 512], BF16)
    t["idbf"] = din("idbf", [128, 128], BF16)
    t["idf"] = din("idf", [128, 128], F32)
    t["onesf"] = din("onesf", [128, 128], F32)
    t["onesb"] = din("onesb", [128, 8], BF16)

    t["out"] = nc.dram_tensor("out", [T, E], F32, kind="ExternalOutput").ap()

    t["ar_in"] = nc.dram_tensor("ar_in", [1, 2], F32).ap()
    t["ar_out"] = nc.dram_tensor("ar_out", [1, 2], F32).ap()
    t["ag1_in"] = nc.dram_tensor("ag1_in", [2, 128, 128, 128], BF16).ap()
    t["ag1_out"] = nc.dram_tensor("ag1_out", [NCORE, 2, 128, 128, 128], BF16).ap()
    t["ag2_in"] = nc.dram_tensor("ag2_in", [3, H, 128 * T], BF16).ap()
    t["ag2_out"] = nc.dram_tensor("ag2_out", [NCORE, 3, H, 128 * T], BF16).ap()
    t["nmx_in"] = nc.dram_tensor("nmx_in", [2 * H], F32).ap()
    t["nmx_out"] = nc.dram_tensor("nmx_out", [NCORE, 2 * H], F32).ap()
    t["negc_d"] = nc.dram_tensor("negc_d", [H], F32).ap()
    t["kvl"] = nc.dram_tensor("kvl", [2, NHL, 128, S], BF16).ap()
    t["ag3_in"] = nc.dram_tensor("ag3_in", [NHL, 128, S], BF16).ap()
    t["ag3_out"] = nc.dram_tensor("ag3_out", [NCORE, NHL, 128, S], BF16).ap()

    sb = lambda n, sh, dt: es.enter_context(nc.sbuf_tensor(n, sh, dt))
    t["x"] = sb("x", [128, NDC * T], F32)
    t["a32"] = sb("a32", [128, 16384], BF16)
    t["b32"] = sb("b32", [128, 16384], BF16)
    t["wstr"] = sb("wstr", [128, 8192], BF16)
    t["kbuf"] = sb("kbuf", [128, 2048], BF16)
    t["vbuf"] = sb("vbuf", [128, 2048], BF16)
    t["expb"] = sb("expb", [128, 1024], BF16)
    t["qst"] = sb("qst", [128, 2048], BF16)
    t["scr"] = sb("scr", [128, 8192], F32)
    t["cos"] = sb("cos", [128, T], F32)
    t["sin"] = sb("sin", [128, T], F32)
    t["mask"] = sb("mask", [128, 2048], BF16)
    t["idbf_t"] = sb("idbf_t", [128, 128], BF16)
    t["idf_t"] = sb("idf_t", [128, 128], F32)
    t["onesf_t"] = sb("onesf_t", [128, 128], F32)
    t["onesb_t"] = sb("onesb_t", [128, 8], BF16)
    t["b1_t"] = sb("b1_t", [128, 64], F32)
    t["b2h_t"] = sb("b2h_t", [128, 16], F32)
    t["small"] = sb("small", [128, 64], F32)

    ps = lambda n, sh, dt: es.enter_context(nc.psum_tensor(n, sh, dt))
    t["ps0"] = ps("ps0", [128, 1024], F32)
    t["ps1"] = ps("ps1", [128, 1024], F32)
    t["ps2"] = ps("ps2", [128, 1024], F32)
    t["psT"] = ps("psT", [128, 1024], BF16)
    t["psU"] = ps("psU", [128, 512], F32)
    return t


def phase_consts(kb, t):
    scr = t["scr"]
    for name, dst in (("cosf", t["cos"]), ("sinf", t["sin"]),
                      ("idbf", t["idbf_t"]), ("idf", t["idf_t"]),
                      ("onesf", t["onesf_t"]), ("onesb", t["onesb_t"])):
        kb.dma("sync", dst[:, :], t[name][:, :])
    for j in range(4):
        kb.dma("sync", t["mask"][:, j * 512:(j + 1) * 512], t["masks"][j])
    kb.dma("sync", scr[0:64, 0:128], t["b1"][:, :])
    kb.dma("sync", scr[0:16, 128:256], t["b2h"][:, :])
    kb.v("memset", t["small"][:, :], 0.0, inc=[("V", 1)])
    r = kb.relay("sync")
    kb.tr(t["psU"][:, 0:64], scr[0:64, 0:128], t["idf_t"][0:64, 0:64],
          wait=[("FS", r)])
    kb.tr(t["psU"][:, 64:80], scr[0:16, 128:256], t["idf_t"][0:16, 0:16],
          inc=[("T", 1)])
    kb.v("tensor_copy", t["b1_t"][:, :], t["psU"][:, 0:64],
         wait=[("T", kb.now("T"))])
    kb.v("tensor_copy", t["b2h_t"][:, :], t["psU"][:, 64:80], inc=[("V", 1)])
    kb.fence()


def phase_wprep(kb, t):
    scr, small = t["scr"], t["small"]
    ps2, psU = t["ps2"], t["psU"]
    onesf = t["onesf_t"]

    bguard = {SC_A: 0, SC_B: 0}
    for mat in range(2):
        nch = 8 if mat == 0 else 16
        wid = 2048 if mat == 0 else 1024
        src = t["w1s"] if mat == 0 else t["w2s"]
        acc = SM_ABS1 if mat == 0 else SM_ABS2
        for j in range(nch):
            buf = SC_A if j % 2 == 0 else SC_B
            kb.dma("sync", scr[:, buf:buf + wid],
                   src[j * 128:(j + 1) * 128, :],
                   wait=([("V", bguard[buf])] if bguard[buf] else []))
            r = kb.relay("sync")
            kb.v("reduce_sum", scr[0:128, SC_T0:SC_T0 + 1],
                 scr[:, buf:buf + wid], axis=AX.X, apply_absolute_value=True,
                 wait=[("FS", r)])
            kb.v("tensor_add", small[:, acc:acc + 1], small[:, acc:acc + 1],
                 scr[0:128, SC_T0:SC_T0 + 1], inc=[("V", 1)])
            bguard[buf] = kb.now("V")
    kb.mm(ps2[0:1, 0:2], onesf[:, 0:1], small[:, SM_ABS1:SM_ABS1 + 2],
          wait=[("V", kb.now("V"))], inc=[("T", 1)])
    kb.v("tensor_copy", scr[0:1, SC_T0:SC_T0 + 2], ps2[0:1, 0:2],
         wait=[("T", kb.now("T"))], inc=[("V", 1)])
    kb.dma("sync", t["ar_in"][:, :], scr[0:1, SC_T0:SC_T0 + 2],
           wait=[("V", kb.now("V"))])
    r = kb.relay("sync")
    kb.coll("AllReduce", ALU.add, [t["ar_in"]], [t["ar_out"]],
            wait=[("FS", r)], inc=[("C", 1)])
    kb.dma("sync", scr[0:1, SC_T0:SC_T0 + 2], t["ar_out"][:, :],
           wait=[("C", kb.now("C"))])
    r = kb.relay("sync")
    kb.v("tensor_scalar_mul", scr[0:1, SC_T0:SC_T0 + 1],
         scr[0:1, SC_T0:SC_T0 + 1], float(1.0 / (FF * E)), wait=[("FS", r)])
    kb.v("tensor_scalar_mul", scr[0:1, SC_T0 + 1:SC_T0 + 2],
         scr[0:1, SC_T0 + 1:SC_T0 + 2], float(1.0 / (E * FF)), inc=[("V", 1)])
    kb.mm(psU[:, 100:102], onesf[0:1, :], scr[0:1, SC_T0:SC_T0 + 2],
          wait=[("V", kb.now("V"))], inc=[("T", 1)])
    kb.v("tensor_copy", small[:, SM_M0:SM_M0 + 2], psU[:, 100:102],
         wait=[("T", kb.now("T"))], inc=[("V", 1)])
    Vm0 = kb.now("V")

    kb.v("memset", small[:, SM_ABS1:SM_ABS1 + 2], 0.0, wait=[("V", Vm0)],
         inc=[("V", 1)])
    bguard = {SC_A: kb.now("V"), SC_B: kb.now("V")}
    for mat in range(2):
        nch = 8 if mat == 0 else 16
        wid = 2048 if mat == 0 else 1024
        src = t["w1s"] if mat == 0 else t["w2s"]
        acc = SM_ABS1 if mat == 0 else SM_ABS2
        m0c = small[:, SM_M0 + mat:SM_M0 + mat + 1]
        for j in range(nch):
            buf = SC_A if j % 2 == 0 else SC_B
            kb.dma("sync", scr[:, buf:buf + wid],
                   src[j * 128:(j + 1) * 128, :], wait=[("V", bguard[buf])])
            r = kb.relay("sync")
            kb.v("tensor_scalar_mul", scr[:, SC_RSTD:SC_RSTD + wid],
                 scr[:, buf:buf + wid], -1.0, wait=[("FS", r)])
            kb.v("tensor_max", scr[:, SC_RSTD:SC_RSTD + wid],
                 scr[:, SC_RSTD:SC_RSTD + wid], scr[:, buf:buf + wid])
            kb.v("tensor_scalar_sub", scr[:, SC_RSTD:SC_RSTD + wid],
                 scr[:, SC_RSTD:SC_RSTD + wid], m0c)
            kb.v("reduce_sum", scr[0:128, SC_T1:SC_T1 + 1],
                 scr[:, SC_RSTD:SC_RSTD + wid], axis=AX.X)
            kb.v("tensor_add", small[:, acc:acc + 1], small[:, acc:acc + 1],
                 scr[0:128, SC_T1:SC_T1 + 1], inc=[("V", 1)])
            bguard[buf] = kb.now("V")
    kb.mm(ps2[0:1, 0:2], onesf[:, 0:1], small[:, SM_ABS1:SM_ABS1 + 2],
          wait=[("V", kb.now("V"))], inc=[("T", 1)])
    kb.v("tensor_copy", scr[0:1, SC_T0:SC_T0 + 2], ps2[0:1, 0:2],
         wait=[("T", kb.now("T"))], inc=[("V", 1)])
    kb.dma("sync", t["ar_in"][:, :], scr[0:1, SC_T0:SC_T0 + 2],
           wait=[("V", kb.now("V"))])
    r = kb.relay("sync")
    kb.coll("AllReduce", ALU.add, [t["ar_in"]], [t["ar_out"]],
            wait=[("FS", r)], inc=[("C", 1)])
    kb.dma("sync", scr[0:1, SC_T0:SC_T0 + 2], t["ar_out"][:, :],
           wait=[("C", kb.now("C"))])
    r = kb.relay("sync")
    kb.v("tensor_scalar_mul", scr[0:1, SC_T0:SC_T0 + 1],
         scr[0:1, SC_T0:SC_T0 + 1], float(1.0 / (FF * E)), wait=[("FS", r)])
    kb.v("tensor_scalar_mul", scr[0:1, SC_T0 + 1:SC_T0 + 2],
         scr[0:1, SC_T0 + 1:SC_T0 + 2], float(1.0 / (E * FF)))
    kb.v("tensor_add", scr[0:1, SC_T0:SC_T0 + 2], scr[0:1, SC_T0:SC_T0 + 2],
         small[0:1, SM_M0:SM_M0 + 2])
    kb.v("tensor_scalar_mul", scr[0:1, SC_T0:SC_T0 + 2],
         scr[0:1, SC_T0:SC_T0 + 2], float(1.0 + 3.51e-7))
    kb.v("tensor_scalar_max", scr[0:1, SC_T0:SC_T0 + 2],
         scr[0:1, SC_T0:SC_T0 + 2], 1e-8)
    kb.v("tensor_scalar_mul", scr[0:1, SC_T0:SC_T0 + 2],
         scr[0:1, SC_T0:SC_T0 + 2], 0.5, inc=[("V", 1)])
    kb.mm(psU[:, 100:102], onesf[0:1, :], scr[0:1, SC_T0:SC_T0 + 2],
          wait=[("V", kb.now("V"))], inc=[("T", 1)])
    kb.v("tensor_copy", small[:, SM_THR1:SM_THR1 + 2], psU[:, 100:102],
         wait=[("T", kb.now("T"))], inc=[("V", 1)])
    Vthr = kb.now("V")

    bguard = {SC_A: Vthr, SC_B: Vthr}
    for mat in range(2):
        nch = 8 if mat == 0 else 16
        wid = 2048 if mat == 0 else 1024
        src = t["w1s"] if mat == 0 else t["w2s"]
        dst = t["a32"] if mat == 0 else t["b32"]
        thc = SM_THR1 if mat == 0 else SM_THR2
        thr = small[:, thc:thc + 1]
        for j in range(nch):
            buf = SC_A if j % 2 == 0 else SC_B
            kb.dma("sync", scr[:, buf:buf + wid],
                   src[j * 128:(j + 1) * 128, :], wait=[("V", bguard[buf])])
            r = kb.relay("sync")
            kb.v("tensor_scalar", scr[:, SC_T0:SC_T0 + wid],
                 scr[:, buf:buf + wid], -1.0, thr,
                 op0=ALU.mult, op1=ALU.is_gt, wait=[("FS", r)])
            if mat == 0:
                kb.v("scalar_tensor_tensor", dst[:, j * wid:(j + 1) * wid],
                     scr[:, buf:buf + wid], thr, scr[:, SC_T0:SC_T0 + wid],
                     op0=ALU.is_gt, op1=ALU.subtract, inc=[("V", 1)])
            else:
                kb.v("scalar_tensor_tensor", scr[:, SC_T0:SC_T0 + wid],
                     scr[:, buf:buf + wid], thr, scr[:, SC_T0:SC_T0 + wid],
                     op0=ALU.is_gt, op1=ALU.subtract)
                kb.v("tensor_scalar_mul", dst[:, j * wid:(j + 1) * wid],
                     scr[:, SC_T0:SC_T0 + wid], 0.5, inc=[("V", 1)])
            bguard[buf] = kb.now("V")
    Vq = kb.now("V")

    slot_relay = {0: 0, 1: 0}
    for mat in range(2):
        for grp in range(16):
            g = mat * 16 + grp
            slot = g % 2
            for k in range(8):
                tile = grp * 8 + k
                if mat == 0:
                    fl, dc = tile // 16, tile % 16
                    src = t["a32"][:, fl * 2048 + dc * 128:fl * 2048 + dc * 128 + 128]
                else:
                    dc, fl = tile // 8, tile % 8
                    src = t["b32"][:, dc * 1024 + fl * 128:dc * 1024 + fl * 128 + 128]
                w = []
                if k == 0:
                    w = [("V", kb.now("V"))] if g > 0 else [("V", Vq)]
                kb.tr(t["psT"][:, k * 128:(k + 1) * 128], src, t["idbf_t"],
                      wait=w, inc=([("T", 1)] if k == 7 else []))
            kb.v("tensor_copy", t["qst"][:, slot * 1024:slot * 1024 + 1024],
                 t["psT"][:, :],
                 wait=[("T", kb.now("T"))] +
                      ([("FS", slot_relay[slot])] if slot_relay[slot] else []),
                 inc=[("V", 1)])
            kb.dma("sync",
                   t["ag1_in"][mat, grp * 8:(grp + 1) * 8]
                   .rearrange("a p f -> p a f"),
                   t["qst"][:, slot * 1024:slot * 1024 + 1024]
                   .rearrange("p (a f) -> p a f", f=128),
                   wait=[("V", kb.now("V"))])
            slot_relay[slot] = kb.relay("sync")
    r = kb.relay("sync")
    kb.coll("AllGather", ALU.bypass, [t["ag1_in"]], [t["ag1_out"]],
            wait=[("FS", r)], inc=[("C", 1)])
    kb.fence()


def phase_xload(kb, t):
    scr, x, psU, idf = t["scr"], t["x"], t["psU"], t["idf_t"]
    vdone = {}
    for tt in range(8):
        buf = SC_A if tt % 2 == 0 else SC_B
        w = [("V", vdone[tt - 2])] if tt - 2 in vdone else []
        kb.dma("sync", scr[:, buf:buf + 2048],
               t["xs"][tt * 128:(tt + 1) * 128, :], wait=w)
        r = kb.relay("sync")
        for q in range(4):
            for k in range(4):
                dc = q * 4 + k
                kb.tr(psU[:, k * 128:(k + 1) * 128],
                      scr[:, buf + dc * 128:buf + dc * 128 + 128], idf,
                      wait=([("FS", r)] if q == 0 and k == 0 else []) +
                           ([("V", kb.now("V"))] if k == 0 else []),
                      inc=([("T", 1)] if k == 3 else []))
            out = x.rearrange("p (d u) -> p d u", u=T)[:, q * 4:(q + 1) * 4,
                                                       tt * 128:(tt + 1) * 128]
            kb.v("tensor_copy", out,
                 psU[:, 0:512].rearrange("p (d u) -> p d u", u=128),
                 wait=[("T", kb.now("T"))], inc=[("V", 1)])
        vdone[tt] = kb.now("V")
    kb.fence()


def phase_ln(kb, t):
    x, scr, a32 = t["x"], t["scr"], t["a32"]
    ps0, ps2, psU = t["ps0"], t["ps2"], t["psU"]
    onesb, onesf = t["onesb_t"], t["onesf_t"]
    loc = {("sx", 0): (ps2, 0), ("sx", 1): (ps2, 512),
           ("sq", 0): (ps0, 0), ("sq", 1): (ps0, 512)}
    for th in range(2):
        for dc in range(NDC):
            it = th * NDC + dc
            sl = x[:, dc * T + th * 512: dc * T + th * 512 + 512]
            cb = (dc % 2) * 512
            kb.v("tensor_copy", t["expb"][:, cb:cb + 512], sl,
                 wait=([("T", kb.now("T"))] if it >= 2 else []), inc=[("V", 1)])
            kb.act(t["qst"][:, cb:cb + 512], sl, AF.Square,
                   wait=([("T", kb.now("T"))] if it >= 2 else []), inc=[("S", 1)])
            pp, c0 = loc[("sx", th)]
            kb.mm(pp[0:1, c0:c0 + 512], onesb[:, 0:1], t["expb"][:, cb:cb + 512],
                  start=(dc == 0), stop=(dc == NDC - 1),
                  wait=[("V", kb.now("V"))])
            pp, c0 = loc[("sq", th)]
            kb.mm(pp[0:1, c0:c0 + 512], onesb[:, 0:1], t["qst"][:, cb:cb + 512],
                  start=(dc == 0), stop=(dc == NDC - 1),
                  wait=[("S", kb.now("S"))], inc=[("T", 1)])
    Tst = kb.now("T")
    for th in range(2):
        pp, c0 = loc[("sx", th)]
        kb.v("tensor_scalar_mul", scr[0:1, SC_T0 + th * 512:SC_T0 + th * 512 + 512],
             pp[0:1, c0:c0 + 512], 1.0 / E, wait=[("T", Tst)])
        pp, c0 = loc[("sq", th)]
        kb.v("tensor_scalar_mul", scr[0:1, SC_T1 + th * 512:SC_T1 + th * 512 + 512],
             pp[0:1, c0:c0 + 512], 1.0 / E)
    kb.v("tensor_mul", scr[0:1, SC_NMR:SC_NMR + 1024], scr[0:1, SC_T0:SC_T0 + 1024],
         scr[0:1, SC_T0:SC_T0 + 1024])
    kb.v("tensor_sub", scr[0:1, SC_T1:SC_T1 + 1024], scr[0:1, SC_T1:SC_T1 + 1024],
         scr[0:1, SC_NMR:SC_NMR + 1024])
    kb.v("tensor_scalar_add", scr[0:1, SC_T1:SC_T1 + 1024],
         scr[0:1, SC_T1:SC_T1 + 1024], 1e-5, inc=[("V", 1)])
    kb.act(scr[0:1, SC_T1:SC_T1 + 1024], scr[0:1, SC_T1:SC_T1 + 1024], AF.Ln,
           wait=[("V", kb.now("V"))], inc=[("S", 1)])
    kb.act(scr[0:1, SC_T1:SC_T1 + 1024], scr[0:1, SC_T1:SC_T1 + 1024], AF.Exp,
           scale=-0.5, inc=[("S", 1)])
    kb.v("tensor_mul", scr[0:1, SC_T0:SC_T0 + 1024], scr[0:1, SC_T0:SC_T0 + 1024],
         scr[0:1, SC_T1:SC_T1 + 1024], wait=[("S", kb.now("S"))])
    kb.v("tensor_scalar_mul", scr[0:1, SC_T0:SC_T0 + 1024],
         scr[0:1, SC_T0:SC_T0 + 1024], -1.0, inc=[("V", 1)])
    for th in range(2):
        kb.mm(psU[:, 0:512], onesf[0:1, :],
              scr[0:1, SC_T1 + th * 512:SC_T1 + th * 512 + 512],
              wait=[("V", kb.now("V"))], inc=[("T", 1)])
        kb.v("tensor_copy", scr[:, SC_RSTD + th * 512:SC_RSTD + th * 512 + 512],
             psU[:, 0:512], wait=[("T", kb.now("T"))], inc=[("V", 1)])
        kb.mm(psU[:, 0:512], onesf[0:1, :],
              scr[0:1, SC_T0 + th * 512:SC_T0 + th * 512 + 512],
              wait=[("V", kb.now("V"))], inc=[("T", 1)])
        kb.v("tensor_copy", scr[:, SC_NMR + th * 512:SC_NMR + th * 512 + 512],
             psU[:, 0:512], wait=[("T", kb.now("T"))], inc=[("V", 1)])
    Vbc = kb.now("V")
    for dc in range(NDC):
        for th in range(2):
            sl = x[:, dc * T + th * 512: dc * T + th * 512 + 512]
            kb.v("tensor_mul", scr[:, SC_T1:SC_T1 + 512], sl,
                 scr[:, SC_RSTD + th * 512:SC_RSTD + th * 512 + 512],
                 wait=([("V", Vbc)] if dc == 0 and th == 0 else []))
            kb.v("tensor_add", a32[:, dc * T + th * 512: dc * T + th * 512 + 512],
                 scr[:, SC_T1:SC_T1 + 512],
                 scr[:, SC_NMR + th * 512:SC_NMR + th * 512 + 512],
                 inc=[("V", 1)])
    kb.fence()


def phase_ffn(kb, t):
    a32, b32, x, wstr = t["a32"], t["b32"], t["x"], t["wstr"]
    ps0, ps1 = t["ps0"], t["ps1"]
    ag1 = t["ag1_out"]
    for th in range(2):
        for fh in range(2):
            tdone, sdone, rr_ = {}, {}, {}
            Tb1 = kb.now("T")
            for fl in range(32):
                fc = fh * 32 + fl
                rk, fli = fc // 8, fc % 8
                ws = (fl % 4) * 2048
                w = [("T", Tb1 + fl - 3)] if fl >= 4 else []
                kb.dma("sync", wstr[:, ws:ws + 2048]
                       .rearrange("p (a f) -> p a f", f=128),
                       ag1[rk, 0, fli * 16:(fli + 1) * 16]
                       .rearrange("a p f -> p a f"), wait=w,
                       qsem="L" + str(fl % 3))
                rr_[fl] = kb.relay("sync", qsem="L" + str(fl % 3))
            for fl in range(32):
                fc = fh * 32 + fl
                ws = (fl % 4) * 2048
                pslot = (fl % 2) * 512
                for dc in range(NDC):
                    w = []
                    if dc == 0:
                        w = [("FS", rr_[fl])]
                        if fl - 2 in sdone:
                            w.append(("S", sdone[fl - 2]))
                    kb.mm(ps0[:, pslot:pslot + 512],
                          wstr[:, ws + dc * 128:ws + dc * 128 + 128],
                          a32[:, dc * T + th * 512:dc * T + th * 512 + 512],
                          start=(dc == 0), stop=(dc == NDC - 1),
                          wait=w, inc=([("T", 1)] if dc == NDC - 1 else []))
                tdone[fl] = kb.now("T")
                kb.act(b32[:, fl * 512:fl * 512 + 512], ps0[:, pslot:pslot + 512],
                       AF.Gelu, bias=t["b1_t"][:, fc:fc + 1],
                       wait=[("T", tdone[fl])], inc=[("S", 1)])
                sdone[fl] = kb.now("S")
            Sh = kb.now("S")
            tdone2, vdone2, r2_ = {}, {}, {}
            Tmm1 = kb.now("T")
            for dc in range(NDC):
                ws = (dc % 2) * 4096
                w = [("T", Tmm1 + dc - 1)] if dc >= 2 else [("T", Tmm1)]
                for rr in range(4):
                    rk = fh * 4 + rr
                    kb.dma("sync", wstr[:, ws + rr * 1024:ws + rr * 1024 + 1024]
                           .rearrange("p (a f) -> p a f", f=128),
                           ag1[rk, 1, dc * 8:(dc + 1) * 8]
                           .rearrange("a p f -> p a f"),
                           wait=(w if rr == 0 else []),
                           qsem="L" + str(dc % 2))
                r2_[dc] = kb.relay("sync", qsem="L" + str(dc % 2))
            for dc in range(NDC):
                ws = (dc % 2) * 4096
                pslot = (dc % 2) * 512
                for fl in range(32):
                    w = []
                    if fl == 0:
                        w = [("FS", r2_[dc]), ("S", Sh)]
                        if dc - 2 in vdone2:
                            w.append(("V", vdone2[dc - 2]))
                    kb.mm(ps1[:, pslot:pslot + 512],
                          wstr[:, ws + fl * 128:ws + fl * 128 + 128],
                          b32[:, fl * 512:fl * 512 + 512],
                          start=(fl == 0), stop=(fl == 31),
                          wait=w, inc=([("T", 1)] if fl == 31 else []))
                tdone2[dc] = kb.now("T")
                xs = x[:, dc * T + th * 512: dc * T + th * 512 + 512]
                if fh == 0:
                    kb.v("scalar_tensor_tensor", xs, ps1[:, pslot:pslot + 512],
                         t["b2h_t"][:, dc:dc + 1], xs, op0=ALU.add, op1=ALU.add,
                         wait=[("T", tdone2[dc])], inc=[("V", 1)])
                else:
                    kb.v("tensor_add", xs, xs, ps1[:, pslot:pslot + 512],
                         wait=[("T", tdone2[dc])], inc=[("V", 1)])
                vdone2[dc] = kb.now("V")
            kb.fence()


def phase_qkv(kb, t):
    a32, scr, qst, wstr = t["a32"], t["scr"], t["qst"], t["wstr"]
    ps0, ps2, psT = t["ps0"], t["ps2"], t["psT"]
    wsrc = [t["wq"], t["wk"], t["wv"]]
    tdone, edone, urelay = {}, {}, {}
    Tq0 = kb.now("T")
    rg_ = {}
    for pi in range(3):
        for ec in range(H):
            u = pi * H + ec
            ws = (u % 2) * 4096
            w = [("T", Tq0 + 2 * (u - 2) + 1)] if u >= 2 else []
            for piece in range(2):
                kb.dma("gpsimd", wstr[:, ws + piece * 1024:ws + piece * 1024 + 1024]
                       .rearrange("p (a f) -> p a f", f=128),
                       wsrc[pi][piece * 1024:(piece + 1) * 1024,
                                ec * 128:(ec + 1) * 128]
                       .rearrange("(a p) f -> p a f", p=128),
                       wait=(w if piece == 0 else []),
                       qsem="G" + str(u % 2))
            rg_[u] = kb.relay("gpsimd", qsem="G" + str(u % 2))
    for pi in range(3):
        for ec in range(H):
            u = pi * H + ec
            ws = (u % 2) * 4096
            r = rg_[u]
            for th in range(2):
                for dc in range(NDC):
                    w = []
                    if th == 0 and dc == 0:
                        w = [("FG", r)]
                        if u - 1 in edone:
                            w.append(("S", edone[u - 1][0]))
                            w.append(("V", edone[u - 1][1]))
                    kb.mm(ps0[:, th * 512:th * 512 + 512],
                          wstr[:, ws + dc * 128:ws + dc * 128 + 128],
                          a32[:, dc * T + th * 512:dc * T + th * 512 + 512],
                          start=(dc == 0), stop=(dc == NDC - 1), wait=w,
                          inc=([("T", 1)] if th == 1 and dc == NDC - 1 else []))
            tdone[u] = kb.now("T")
            assert tdone[u] == Tq0 + 2 * u + 1, (u, tdone[u], Tq0)
            qs = (u % 2) * 1024
            wsl = [("FS", urelay[u - 2])] if u - 2 in urelay else []
            wst = [("FS", urelay[u - 1])] if u - 1 in urelay else []
            if DBG_SKIP_QKV_TAIL:
                kb.scopy(qst[:, qs:qs + 1024], ps0[:, :], wait=[("T", tdone[u])],
                         inc=[("S", 1)])
                edone[u] = (kb.now("S"), kb.now("V"))
            elif pi < 2:
                kb.scopy(scr[:, SC_T1:SC_T1 + 1024], ps0[:, :],
                         wait=[("T", tdone[u])] +
                              ([("FS", urelay[u - 1])] if u - 1 in urelay else []),
                         inc=[("S", 1)])
                Sev = kb.now("S")
                kb.dma("sync", scr[0:64, SC_NMR:SC_NMR + 1024],
                       scr[64:128, SC_T1:SC_T1 + 1024], wait=[("S", Sev)])
                kb.dma("sync", scr[64:128, SC_NMR:SC_NMR + 1024],
                       scr[0:64, SC_T1:SC_T1 + 1024])
                rsw = kb.relay("sync")
                kb.v("tensor_mul", scr[:, SC_A:SC_A + 1024],
                     scr[:, SC_T1:SC_T1 + 1024], t["cos"][:, :],
                     wait=[("S", Sev)])
                kb.v("tensor_mul", scr[:, SC_B:SC_B + 1024],
                     scr[:, SC_NMR:SC_NMR + 1024], t["sin"][:, :],
                     wait=[("FS", rsw)])
                kb.v("tensor_add", qst[:, qs:qs + 1024], scr[:, SC_A:SC_A + 1024],
                     scr[:, SC_B:SC_B + 1024], inc=[("V", 1)], wait=wsl)
                kb.act(scr[:, SC_A:SC_A + 1024], qst[:, qs:qs + 1024], AF.Square,
                       wait=[("V", kb.now("V"))], inc=[("S", 1)])
                for th in range(2):
                    kb.mm(ps2[0:1, th * 512:th * 512 + 512], t["onesf_t"][:, 0:1],
                          scr[:, SC_A + th * 512:SC_A + th * 512 + 512],
                          wait=([("S", kb.now("S")), ("V", kb.now("V"))]
                                if th == 0 else []),
                          inc=([("T", 1)] if th == 1 else []))
                kb.v("reduce_max", t["small"][0:1, 8 + pi * H + ec:9 + pi * H + ec],
                     ps2[0:1, 0:1024], axis=AX.X,
                     wait=[("T", kb.now("T"))], inc=[("V", 1)])
                kb.dma("sync",
                       t["ag2_in"][pi, ec].rearrange("(p u) -> p u", p=128),
                       qst[:, qs:qs + 1024], wait=[("V", kb.now("V"))])
                urelay[u] = kb.relay("sync")
                edone[u] = (kb.now("S"), kb.now("V"))
            else:
                kb.scopy(qst[:, qs:qs + 1024], ps0[:, :],
                         wait=[("T", tdone[u])] + wsl, inc=[("S", 1)])
                for tc in range(8):
                    kb.tr(psT[:, tc * 128:(tc + 1) * 128],
                          qst[:, qs + tc * 128:qs + tc * 128 + 128], t["idbf_t"],
                          wait=([("S", kb.now("S")), ("V", kb.now("V"))]
                                if tc == 0 else []),
                          inc=([("T", 1)] if tc == 7 else []))
                kb.v("tensor_copy", t["expb"][:, :], psT[:, :],
                     wait=[("T", kb.now("T"))] + wst, inc=[("V", 1)])
                kb.dma("sync",
                       t["ag2_in"][pi, ec].rearrange("(p a u) -> p a u", p=128, a=8),
                       t["expb"][:, :].rearrange("p (a u) -> p a u", a=8),
                       wait=[("V", kb.now("V"))])
                urelay[u] = kb.relay("sync")
                edone[u] = (kb.now("S"), kb.now("V"))
    kb.dma("sync", t["nmx_in"].rearrange("(h c) -> h c", h=1),
           t["small"][0:1, 8:8 + 2 * H], wait=[("V", kb.now("V"))])
    r = kb.relay("sync")
    r2 = kb.relay("gpsimd")
    kb.coll("AllGather", ALU.bypass, [t["ag2_in"]], [t["ag2_out"]],
            wait=[("FS", r), ("FG", r2)], inc=[("C", 1)])
    kb.coll("AllGather", ALU.bypass, [t["nmx_in"]], [t["nmx_out"]],
            inc=[("C", 1)])
    kb.fence()


def phase_attention(kb, t):
    b32, scr, small = t["b32"], t["scr"], t["small"]
    ps0, ps1, ps2, psU = t["ps0"], t["ps1"], t["ps2"], t["psU"]
    ag2 = t["ag2_out"]
    onesf, onesb = t["onesf_t"], t["onesb_t"]

    for lh in range(NHL):
        kb.dma_dyn("sync",
                   b32[:, lh * S:(lh + 1) * S].rearrange("p (r u) -> p r u", u=T),
                   lambda pid, lh=lh:
                   ag2[0:NCORE, 0, bass.ds(kb.hsel(lh), 1)]
                   .rearrange("r h (p u) -> p (r h) u", p=128))
    for lh in range(NHL):
        kb.dma_dyn("sync",
                   t["kvl"][0, lh].rearrange("p (r u) -> p r u", u=T),
                   lambda pid, lh=lh:
                   ag2[0:NCORE, 1, bass.ds(kb.hsel(lh), 1)]
                   .rearrange("r h (p u) -> p (r h) u", p=128))
        kb.dma_dyn("sync",
                   t["kvl"][1, lh].rearrange("p (r g) -> p r g", g=T),
                   lambda pid, lh=lh:
                   ag2[0:NCORE, 2, bass.ds(kb.hsel(lh), 1)]
                   .rearrange("r h (p a u) -> p (r h) (a u)", p=128, a=8))
    kb.dma("sync", scr[0:16, SC_T0:SC_T0 + 16],
           t["nmx_out"].rearrange("r (p h) -> h (r p)", p=2), slow=True)
    r = kb.relay("sync")
    kb.v("reduce_max", scr[0:16, SC_T1:SC_T1 + 1],
         scr[0:16, SC_T0:SC_T0 + 16:2], axis=AX.X, wait=[("FS", r)])
    kb.v("reduce_max", scr[0:16, SC_T1 + 1:SC_T1 + 2],
         scr[0:16, SC_T0 + 1:SC_T0 + 16:2], axis=AX.X)
    kb.v("tensor_mul", scr[0:16, SC_T1 + 2:SC_T1 + 3],
         scr[0:16, SC_T1:SC_T1 + 1], scr[0:16, SC_T1 + 1:SC_T1 + 2],
         inc=[("V", 1)])
    kb.act(scr[0:16, SC_T1 + 2:SC_T1 + 3], scr[0:16, SC_T1 + 2:SC_T1 + 3],
           AF.Sqrt, wait=[("V", kb.now("V"))], inc=[("S", 1)])
    kb.v("tensor_scalar_mul", scr[0:16, SC_T1 + 2:SC_T1 + 3],
         scr[0:16, SC_T1 + 2:SC_T1 + 3], -ISD * 1.001,
         wait=[("S", kb.now("S"))], inc=[("V", 1)])
    kb.dma("sync", t["negc_d"].rearrange("(h c) -> h c", c=1),
           scr[0:16, SC_T1 + 2:SC_T1 + 3], wait=[("V", kb.now("V"))])
    r = kb.relay("sync")
    for lh in range(NHL):
        kb.dma_dyn("sync", scr[0:1, SC_T0 + 40 + lh:SC_T0 + 41 + lh],
                   lambda pid, lh=lh: t["negc_d"]
                   [bass.ds(kb.hsel(lh), 1)].rearrange("(h c) -> h c", c=1),
                   wait=([("FS", r)] if lh == 0 else []))
    r = kb.relay("sync")
    for lh in range(NHL):
        kb.mm(psU[:, 200 + lh:201 + lh], onesf[0:1, :],
              scr[0:1, SC_T0 + 40 + lh:SC_T0 + 41 + lh],
              wait=([("FS", r)] if lh == 0 else []), inc=[("T", 1)])
        kb.v("tensor_copy", small[:, SM_NEGC0 + lh:SM_NEGC0 + lh + 1],
             psU[:, 200 + lh:201 + lh], wait=[("T", kb.now("T"))], inc=[("V", 1)])
    kb.fence()

    piece_tcons = {}
    ps0_free = {0: 0, 1: 0}
    expb_free = {0: 0, 1: 0}
    av_free = {0: 0, 1: 0}
    sum_free = {0: 0, 512: 0}
    gseq = 0
    for lh in range(NHL):
        for i in range(16):
            nkb = 4 * (i + 1)
            tile = lh * 16 + i
            sumcol = (tile % 2) * 512
            avs = (tile % 2) * 512
            qsl = b32[:, lh * S + i * 512: lh * S + i * 512 + 512]
            npieces = (nkb + 7) // 8
            base_gseq = gseq
            Tt0 = kb.now("T")
            piece_r = {}
            for p_ in range(npieces):
                p = base_gseq + p_
                rk = p_
                ks = (p % 2) * 1024
                w = [("T", piece_tcons[p - 2])] if p - 2 in piece_tcons else []
                kb.dma("sync", t["kbuf"][:, ks:ks + 1024],
                       t["kvl"][0, lh][:, rk * T:(rk + 1) * T], wait=w,
                       qsem="L" + str(p % 3))
                kb.dma("sync", t["vbuf"][:, ks:ks + 1024],
                       t["kvl"][1, lh][:, rk * T:(rk + 1) * T],
                       qsem="L" + str(p % 3))
                piece_r[p] = kb.relay("sync", qsem="L" + str(p % 3))
                last_j = min(8 * p_ + 7, nkb - 1)
                piece_tcons[p] = (Tt0 + 2 * last_j + 3 if last_j < nkb - 1
                                  else Tt0 + 2 * nkb)
            gseq += npieces

            def unit_tail(k):
                es = (k % 2) * 512
                diag = k >= nkb - 4
                gate = [("V", vgate[k])] if diag else [("S", sgate[k])]
                w = list(gate)
                if k == 0 and sum_free[sumcol]:
                    w.append(("V", sum_free[sumcol]))
                kb.mm(ps2[0:1, sumcol:sumcol + 512], onesb[:, 0:1],
                      t["expb"][:, es:es + 512],
                      start=(k == 0), stop=(k == nkb - 1), wait=w)
                w = []
                if k == 0 and av_free[tile % 2]:
                    w.append(("V", av_free[tile % 2]))
                koff = ((base_gseq + k // 8) % 2) * 1024 + (k % 8) * 128
                kb.mm(ps1[:, avs:avs + 512], t["vbuf"][:, koff:koff + 128],
                      t["expb"][:, es:es + 512],
                      start=(k == 0), stop=(k == nkb - 1),
                      wait=w, inc=[("T", 1)])
                expb_free[k % 2] = kb.now("T")

            sgate, vgate = {}, {}
            for k in range(nkb):
                p = base_gseq + k // 8
                ks = (p % 2) * 1024
                koff = ks + (k % 8) * 128
                ss = (k % 2) * 512
                es = (k % 2) * 512
                w = [("FS", piece_r[p])]
                if ps0_free[k % 2]:
                    w.append(("S", ps0_free[k % 2]))
                kb.mm(ps0[:, ss:ss + 512], t["kbuf"][:, koff:koff + 128], qsl,
                      start=True, stop=True, wait=w, inc=[("T", 1)])
                tsc = kb.now("T")
                assert tsc == (Tt0 + 1 if k == 0 else Tt0 + 2 * k), \
                    (tile, k, tsc, Tt0)
                kb.act(t["expb"][:, es:es + 512], ps0[:, ss:ss + 512], AF.Exp,
                       bias=small[:, SM_NEGC0 + lh:SM_NEGC0 + lh + 1], scale=ISD,
                       wait=[("T", max(tsc, expb_free[k % 2]))], inc=[("S", 1)])
                ps0_free[k % 2] = kb.now("S")
                sgate[k] = kb.now("S")
                if k >= nkb - 4:
                    j = k - (nkb - 4)
                    kb.v("tensor_mul", t["expb"][:, es:es + 512],
                         t["expb"][:, es:es + 512],
                         t["mask"][:, j * 512:(j + 1) * 512],
                         wait=[("S", sgate[k])], inc=[("V", 1)])
                    vgate[k] = kb.now("V")
                if k >= 1:
                    unit_tail(k - 1)
            unit_tail(nkb - 1)
            Ttile = kb.now("T")
            kb.v("tensor_copy", scr[0:1, SC_T0:SC_T0 + 512],
                 ps2[0:1, sumcol:sumcol + 512], wait=[("T", Ttile)])
            kb.v("reciprocal", scr[0:1, SC_T0:SC_T0 + 512],
                 scr[0:1, SC_T0:SC_T0 + 512], inc=[("V", 1)])
            sum_free[sumcol] = kb.now("V")
            kb.mm(psU[:, 0:512], onesf[0:1, :], scr[0:1, SC_T0:SC_T0 + 512],
                  wait=[("V", kb.now("V"))], inc=[("T", 1)])
            kb.v("tensor_copy", scr[:, SC_NMR:SC_NMR + 512], psU[:, 0:512],
                 wait=[("T", kb.now("T"))])
            kb.v("tensor_mul", qsl, ps1[:, avs:avs + 512],
                 scr[:, SC_NMR:SC_NMR + 512], inc=[("V", 1)])
            av_free[tile % 2] = kb.now("V")
        Vattn = kb.now("V")

    for lh in range(NHL):
        kb.dma("sync", t["ag3_in"][lh], b32[:, lh * S:(lh + 1) * S],
               wait=[("V", Vattn)])
    r = kb.relay("sync")
    kb.coll("AllGather", ALU.bypass, [t["ag3_in"]], [t["ag3_out"]],
            wait=[("FS", r)], inc=[("C", 1)])
    Cag3 = kb.now("C")
    kb.dma_dyn("sync", t["a32"][:, :].rearrange("p (g u) -> p g u", u=T),
               lambda pid: t["ag3_out"][0:NCORE, 0:NHL, :, bass.ds(kb.town(), T)]
               .rearrange("r j p u -> p (r j) u"),
               wait=[("C", Cag3)])
    kb.relay("sync")
    kb.fence()

    tdone, vdone = {}, {}
    for dc in range(NDC):
        ws = (dc % 2) * 4096
        w = [("T", tdone[dc - 2])] if dc - 2 in tdone else []
        for piece in range(2):
            kb.dma("gpsimd", t["wstr"][:, ws + piece * 1024:ws + piece * 1024 + 1024]
                   .rearrange("p (a f) -> p a f", f=128),
                   t["wo"][piece * 1024:(piece + 1) * 1024, dc * 128:(dc + 1) * 128]
                   .rearrange("(a p) f -> p a f", p=128),
                   wait=(w if piece == 0 else []))
        r = kb.relay("gpsimd")
        for th in range(2):
            pslot = ((dc * 2 + th) % 2) * 512
            for ec in range(H):
                w = []
                if ec == 0:
                    w = [("FG", r)]
                    k = dc * 2 + th - 2
                    if k in vdone:
                        w.append(("V", vdone[k]))
                kb.mm(t["ps1"][:, pslot:pslot + 512],
                      t["wstr"][:, ws + ec * 128:ws + ec * 128 + 128],
                      t["a32"][:, ec * T + th * 512:ec * T + th * 512 + 512],
                      start=(ec == 0), stop=(ec == H - 1),
                      wait=w, inc=([("T", 1)] if ec == H - 1 else []))
            xs = t["x"][:, dc * T + th * 512: dc * T + th * 512 + 512]
            kb.v("tensor_add", xs, xs, t["ps1"][:, pslot:pslot + 512],
                 wait=[("T", kb.now("T"))], inc=[("V", 1)])
            vdone[dc * 2 + th] = kb.now("V")
        tdone[dc] = kb.now("T")
    kb.fence()


def phase_store(kb, t, src, dtype):
    scr = t["scr"]
    ident = t["idf_t"] if dtype == F32 else t["idbf_t"]
    psx = t["psU"] if dtype == F32 else t["psT"]
    rdone = {}
    for tt in range(8):
        buf = SC_A if tt % 2 == 0 else SC_B
        for q in range(4):
            for k in range(4):
                dc = q * 4 + k
                sl = src[:, dc * T + tt * 128: dc * T + tt * 128 + 128]
                w = []
                if k == 0:
                    w = [("V", kb.now("V"))]
                    if tt - 2 in rdone and q == 0:
                        w.append(("FS", rdone[tt - 2]))
                kb.tr(psx[:, k * 128:(k + 1) * 128], sl, ident,
                      wait=w, inc=([("T", 1)] if k == 3 else []))
            kb.v("tensor_copy", scr[:, buf + q * 512:buf + q * 512 + 512],
                 psx[:, 0:512],
                 wait=[("T", kb.now("T"))] +
                      ([("FS", rdone[tt - 2])] if tt - 2 in rdone else []),
                 inc=[("V", 1)])
        kb.dma("sync", t["out"][tt * 128:(tt + 1) * 128, :],
               scr[:, buf:buf + 2048], wait=[("V", kb.now("V"))])
        rdone[tt] = kb.relay("sync")
    kb.fence()


def emit_stream(kb, t):
    stage = DEBUG_STAGE
    phase_consts(kb, t)
    if stage >= 2:
        phase_wprep(kb, t)
    phase_xload(kb, t)
    if stage == 0:
        phase_store(kb, t, t["x"], F32)
        return
    phase_ln(kb, t)
    if stage == 1:
        phase_store(kb, t, t["a32"], BF16)
        return
    phase_ffn(kb, t)
    if stage == 2:
        phase_store(kb, t, t["x"], F32)
        return
    phase_ln(kb, t)
    phase_qkv(kb, t)
    if stage == 3:
        phase_store(kb, t, t["x"], F32)
        return
    phase_attention(kb, t)
    if stage == 4:
        phase_store(kb, t, t["x"], F32)
        return
    phase_ln(kb, t)
    phase_ffn(kb, t)
    phase_store(kb, t, t["x"], F32)


def build_nc():
    nc = bass.Bass("TRN2", target_bir_lowering=False, debug=False)
    es = ExitStack()
    t = declare(nc, es)
    sems = {}
    for name in SEMS:
        sems[name] = es.enter_context(nc.semaphore("sem" + name))
    block = es.enter_context(nc.Block())
    counts = {}

    def make(eng_name):
        def run(eng):
            kbx = KB(nc, eng_name, eng, sems, t)
            emit_stream(kbx, t)
            counts[eng_name] = dict(kbx.c)
        return run

    for eng_name in ENGS:
        getattr(block, eng_name)(make(eng_name))
    vals = list(counts.values())
    for v in vals[1:]:
        assert v == vals[0], f"sem count mismatch: {counts}"
    es.close()
    return nc


def host_inputs(full):
    x = np.asarray(full["L0_attn_x"]).reshape(S, E)
    w1 = np.asarray(full["L0_ffn_w1"])
    w2 = np.asarray(full["L0_ffn_w2"])
    b1 = np.ascontiguousarray(np.asarray(full["L0_ffn_b1"]).reshape(64, 128))
    b2h = np.ascontiguousarray(
        (np.asarray(full["L0_ffn_b2"]) * 0.5).astype(np.float32).reshape(16, 128))

    pos = np.arange(S, dtype=np.float64)
    inv = 1.0 / (10000.0 ** (np.arange(0, D, 2, dtype=np.float64) / D))
    ang = pos[:, None] * inv[None, :]
    cos = np.cos(ang).T
    sin = np.sin(ang).T
    cosf = np.concatenate([cos, cos], 0).astype(np.float32)
    sinf = np.concatenate([-sin, sin], 0).astype(np.float32)

    kk = np.arange(128)[:, None]
    qq = np.arange(512)[None, :]
    masks = np.stack([(qq >= kk + 128 * j) for j in range(4)]) \
        .astype(ml_dtypes.bfloat16)

    shared = {
        "wq": np.asarray(full["L0_attn_wq"]), "wk": np.asarray(full["L0_attn_wk"]),
        "wv": np.asarray(full["L0_attn_wv"]), "wo": np.asarray(full["L0_attn_wo"]),
        "b1": b1, "b2h": b2h, "masks": np.asarray(masks),
        "idbf": np.asarray(np.eye(128, dtype=ml_dtypes.bfloat16)),
        "idf": np.eye(128, dtype=np.float32),
        "onesf": np.ones((128, 128), np.float32),
        "onesb": np.asarray(np.ones((128, 8), ml_dtypes.bfloat16)),
    }
    maps = []
    for c in range(NCORE):
        m = dict(shared)
        m["xs"] = np.ascontiguousarray(x[c * T:(c + 1) * T])
        m["w1s"] = np.ascontiguousarray(w1[c * FSL:(c + 1) * FSL])
        m["w2s"] = np.ascontiguousarray(w2[:, c * FSL:(c + 1) * FSL])
        m["cosf"] = np.ascontiguousarray(cosf[:, c * T:(c + 1) * T])
        m["sinf"] = np.ascontiguousarray(sinf[:, c * T:(c + 1) * T])
        maps.append(m)
    return maps


def kernel(**inputs):
    maps = host_inputs(inputs)
    nc = build_nc()
    res = run_bass_kernel_spmd(nc, maps, core_ids=list(range(NCORE)))
    out = np.concatenate([res.results[c]["out"] for c in range(NCORE)], axis=0)
    return np.ascontiguousarray(out.reshape(1, S, E).astype(np.float32))


# version 09
# speedup vs baseline: 1.3406x; 1.0201x over previous
import numpy as np
import ml_dtypes
import concourse.bass as bass
import concourse.mybir as mybir
from concourse.bass_utils import run_bass_kernel_spmd
from contextlib import ExitStack

F32 = mybir.dt.float32
BF16 = mybir.dt.bfloat16
AF = mybir.ActivationFunctionType
ALU = mybir.AluOpType
AX = mybir.AxisListType

NCORE = 8
S, E, H, D, FF = 8192, 2048, 16, 128, 8192
T = S // NCORE
NDC = E // 128
FSL = FF // NCORE
NHL = 2
ISD = float(1.0 / np.sqrt(128.0))

DEBUG_STAGE = 99
DBG_SKIP_AG2 = False
DBG_SKIP_QKV_TAIL = False

ENGS = ("tensor", "vector", "scalar", "sync", "gpsimd")
SEMS = ("L", "L0", "L1", "L2", "G", "G0", "G1", "T", "V", "S", "C", "FS", "FG")

SC_A = 0
SC_B = 2048
SC_RSTD = 4096
SC_NMR = 5120
SC_T0 = 6144
SC_T1 = 7168
SM_THR1, SM_THR2, SM_NEGC0, SM_NEGC1, SM_ABS1, SM_ABS2 = 0, 1, 2, 3, 4, 5
SM_M0 = 6
SM_RES = 40


class KB:
    def __init__(self, nc, which, eng, sems, t):
        self.nc, self.which, self.eng, self.sems, self.t = nc, which, eng, sems, t
        self.c = {k: 0 for k in SEMS}
        self._pidv = None

    def pid(self):
        if self._pidv is None:
            self._pidv = self.eng.partition_id()
        return self._pidv

    def hsel(self, lh):
        if not hasattr(self, "_hsel"):
            p = self.pid()
            self._hsel = [p * NHL + j for j in range(NHL)]
        return self._hsel[lh]

    def town(self):
        if not hasattr(self, "_town"):
            self._town = self.pid() * T
        return self._town

    def on(self, name):
        return self.which == name

    def now(self, s):
        return self.c[s]

    def _go(self, engname, fn, inc, wait):
        if self.which == engname:
            for s, v in wait:
                if v > 0:
                    self.eng.wait_ge(self.sems[s], v)
            ins = fn(self.eng)
            for s, n in inc:
                ins = ins.then_inc(self.sems[s], n)
        for s, n in inc:
            self.c[s] += n

    def relay(self, q, qsem=None, extra_wait=()):
        if qsem is None:
            qsem = "L" if q == "sync" else "G"
        rsem = "FS" if q == "sync" else "FG"
        if self.which == q:
            for s, v in extra_wait:
                if v > 0:
                    self.eng.wait_ge(self.sems[s], v)
            if self.c[qsem] > 0:
                self.eng.wait_ge(self.sems[qsem], self.c[qsem])
            self.eng.sem_inc(self.sems[rsem], 1)
        self.c[rsem] += 1
        return self.c[rsem]

    def fence(self):
        if self.which == "sync":
            for s in ("L", "L0", "L1", "L2"):
                if self.c[s] > 0:
                    self.eng.wait_ge(self.sems[s], self.c[s])
            self.eng.sem_inc(self.sems["FS"], 1)
        if self.which == "gpsimd":
            for s in ("G", "G0", "G1"):
                if self.c[s] > 0:
                    self.eng.wait_ge(self.sems[s], self.c[s])
            self.eng.sem_inc(self.sems["FG"], 1)
        self.c["FS"] += 1
        self.c["FG"] += 1
        tgt = [(s, self.c[s]) for s in ("FS", "FG", "T", "V", "S", "C")]
        for s, v in tgt:
            if v > 0:
                self.eng.wait_ge(self.sems[s], v)

    def mm(self, out, lhsT, rhs, start=True, stop=True, inc=(), wait=()):
        self._go("tensor",
                 lambda e: e.matmul(out, lhsT, rhs, start=start, stop=stop),
                 inc, wait)

    def tr(self, out, in_, ident, inc=(), wait=()):
        if not isinstance(ident, bass.AP):
            ident = ident[:, :]
        self._go("tensor", lambda e: e.transpose(out, in_, ident), inc, wait)

    def act(self, out, in_, func, bias=0.0, scale=1.0, inc=(), wait=()):
        w = [("S", self.c["S"])] + list(wait)
        ii = [x for x in inc if x[0] != "S"] + [("S", 1)]
        self._go("scalar",
                 lambda e: e.activation(out=out, in_=in_, func=func,
                                        bias=bias, scale=scale), ii, w)

    def scopy(self, out, in_, inc=(), wait=()):
        w = [("S", self.c["S"])] + list(wait)
        ii = [x for x in inc if x[0] != "S"] + [("S", 1)]
        self._go("scalar", lambda e: e.copy(out=out, in_=in_), ii, w)

    def v(self, method, *a, inc=(), wait=(), **kw):
        w = [("V", self.c["V"])] + list(wait)
        ii = [x for x in inc if x[0] != "V"] + [("V", 1)]
        self._go("vector", lambda e: getattr(e, method)(*a, **kw), ii, w)

    def dma(self, q, out, in_, inc=None, wait=(), slow=False, qsem=None):
        if inc is None:
            inc = [((qsem or ("L" if q == "sync" else "G")), 16)]
        self._go(q, lambda e: e.dma_start(out=out, in_=in_,
                                          allow_slow_non_contiguous=slow),
                 inc, wait)

    def dma_dyn(self, q, out, in_fn, inc=None, wait=(), qsem=None):
        if inc is None:
            inc = [((qsem or ("L" if q == "sync" else "G")), 16)]
        self._go(q, lambda e: e.dma_start(out=out, in_=in_fn(self.pid())),
                 inc, wait)

    def coll(self, kind, op, ins, outs, inc=(), wait=()):
        self._go("gpsimd",
                 lambda e: e.collective_compute(
                     kind, op, replica_groups=[list(range(NCORE))],
                     ins=[a.opt() for a in ins], outs=[a.opt() for a in outs]),
                 inc, wait)


def declare(nc, es):
    t = {}
    din = lambda n, sh, dt: nc.dram_tensor(n, sh, dt, kind="ExternalInput").ap()
    t["xs"] = din("xs", [T, E], F32)
    t["w1s"] = din("w1s", [FSL, E], F32)
    t["w2s"] = din("w2s", [E, FSL], F32)
    t["wq"] = din("wq", [E, E], F32)
    t["wk"] = din("wk", [E, E], F32)
    t["wv"] = din("wv", [E, E], F32)
    t["wo"] = din("wo", [E, E], F32)
    t["b1"] = din("b1", [64, 128], F32)
    t["b2h"] = din("b2h", [16, 128], F32)
    t["cosf"] = din("cosf", [128, T], F32)
    t["sinf"] = din("sinf", [128, T], F32)
    t["masks"] = din("masks", [4, 128, 512], BF16)
    t["idbf"] = din("idbf", [128, 128], BF16)
    t["idf"] = din("idf", [128, 128], F32)
    t["onesf"] = din("onesf", [128, 128], F32)
    t["onesb"] = din("onesb", [128, 8], BF16)

    t["out"] = nc.dram_tensor("out", [T, E], F32, kind="ExternalOutput").ap()

    t["ar_in"] = nc.dram_tensor("ar_in", [1, 2], F32).ap()
    t["ar_out"] = nc.dram_tensor("ar_out", [1, 2], F32).ap()
    t["ag1_in0"] = nc.dram_tensor("ag1_in0", [128, 128, 128], BF16).ap()
    t["ag1_in1"] = nc.dram_tensor("ag1_in1", [128, 128, 128], BF16).ap()
    t["ag1_out0"] = nc.dram_tensor("ag1_out0", [NCORE, 128, 128, 128], BF16).ap()
    t["ag1_out1"] = nc.dram_tensor("ag1_out1", [NCORE, 128, 128, 128], BF16).ap()
    t["ag2q_in"] = nc.dram_tensor("ag2q_in", [H, 128 * T], BF16).ap()
    t["ag2k_in"] = nc.dram_tensor("ag2k_in", [H, 128 * T], BF16).ap()
    t["ag2v_in"] = nc.dram_tensor("ag2v_in", [H, 128 * T], BF16).ap()
    t["ag2q_out"] = nc.dram_tensor("ag2q_out", [NCORE, H, 128 * T], BF16).ap()
    t["ag2k_out"] = nc.dram_tensor("ag2k_out", [NCORE, H, 128 * T], BF16).ap()
    t["ag2v_out"] = nc.dram_tensor("ag2v_out", [NCORE, H, 128 * T], BF16).ap()
    t["nmx_in"] = nc.dram_tensor("nmx_in", [2 * H], F32).ap()
    t["nmx_out"] = nc.dram_tensor("nmx_out", [NCORE, 2 * H], F32).ap()
    t["negc_d"] = nc.dram_tensor("negc_d", [H], F32).ap()
    t["kvl"] = nc.dram_tensor("kvl", [2, NHL, 128, S], BF16).ap()
    t["ag3a_in"] = nc.dram_tensor("ag3a_in", [128, S], BF16).ap()
    t["ag3b_in"] = nc.dram_tensor("ag3b_in", [128, S], BF16).ap()
    t["ag3a_out"] = nc.dram_tensor("ag3a_out", [NCORE, 128, S], BF16).ap()
    t["ag3b_out"] = nc.dram_tensor("ag3b_out", [NCORE, 128, S], BF16).ap()

    sb = lambda n, sh, dt: es.enter_context(nc.sbuf_tensor(n, sh, dt))
    t["x"] = sb("x", [128, NDC * T], F32)
    t["a32"] = sb("a32", [128, 16384], BF16)
    t["b32"] = sb("b32", [128, 16384], BF16)
    t["wstr"] = sb("wstr", [128, 8192], BF16)
    t["kbuf"] = sb("kbuf", [128, 2048], BF16)
    t["vbuf"] = sb("vbuf", [128, 2048], BF16)
    t["expb"] = sb("expb", [128, 1024], BF16)
    t["qst"] = sb("qst", [128, 2048], BF16)
    t["scr"] = sb("scr", [128, 8192], F32)
    t["cos"] = sb("cos", [128, T], F32)
    t["sin"] = sb("sin", [128, T], F32)
    t["mask"] = sb("mask", [128, 2048], BF16)
    t["idbf_t"] = sb("idbf_t", [128, 128], BF16)
    t["idf_t"] = sb("idf_t", [128, 128], F32)
    t["onesf_t"] = sb("onesf_t", [128, 128], F32)
    t["onesb_t"] = sb("onesb_t", [128, 8], BF16)
    t["b1_t"] = sb("b1_t", [128, 64], F32)
    t["b2h_t"] = sb("b2h_t", [128, 16], F32)
    t["small"] = sb("small", [128, 64], F32)

    ps = lambda n, sh, dt: es.enter_context(nc.psum_tensor(n, sh, dt))
    t["ps0"] = ps("ps0", [128, 1024], F32)
    t["ps1"] = ps("ps1", [128, 1024], F32)
    t["ps2"] = ps("ps2", [128, 1024], F32)
    t["psT"] = ps("psT", [128, 1024], BF16)
    t["psU"] = ps("psU", [128, 512], F32)
    return t


def phase_consts(kb, t):
    scr = t["scr"]
    for name, dst in (("cosf", t["cos"]), ("sinf", t["sin"]),
                      ("idbf", t["idbf_t"]), ("idf", t["idf_t"]),
                      ("onesf", t["onesf_t"]), ("onesb", t["onesb_t"])):
        kb.dma("sync", dst[:, :], t[name][:, :])
    for j in range(4):
        kb.dma("sync", t["mask"][:, j * 512:(j + 1) * 512], t["masks"][j])
    kb.dma("sync", scr[0:64, 0:128], t["b1"][:, :])
    kb.dma("sync", scr[0:16, 128:256], t["b2h"][:, :])
    kb.v("memset", t["small"][:, :], 0.0, inc=[("V", 1)])
    r = kb.relay("sync")
    kb.tr(t["psU"][:, 0:64], scr[0:64, 0:128], t["idf_t"][0:64, 0:64],
          wait=[("FS", r)])
    kb.tr(t["psU"][:, 64:80], scr[0:16, 128:256], t["idf_t"][0:16, 0:16],
          inc=[("T", 1)])
    kb.v("tensor_copy", t["b1_t"][:, :], t["psU"][:, 0:64],
         wait=[("T", kb.now("T"))])
    kb.v("tensor_copy", t["b2h_t"][:, :], t["psU"][:, 64:80], inc=[("V", 1)])
    kb.fence()


def phase_wprep(kb, t):
    scr, small = t["scr"], t["small"]
    ps2, psU = t["ps2"], t["psU"]
    onesf = t["onesf_t"]

    bguard = {SC_A: 0, SC_B: 0}
    for mat in range(2):
        nch = 8 if mat == 0 else 16
        wid = 2048 if mat == 0 else 1024
        src = t["w1s"] if mat == 0 else t["w2s"]
        acc = SM_ABS1 if mat == 0 else SM_ABS2
        for j in range(nch):
            buf = SC_A if j % 2 == 0 else SC_B
            kb.dma("sync", scr[:, buf:buf + wid],
                   src[j * 128:(j + 1) * 128, :],
                   wait=([("V", bguard[buf])] if bguard[buf] else []))
            r = kb.relay("sync")
            kb.v("reduce_sum", scr[0:128, SC_T0:SC_T0 + 1],
                 scr[:, buf:buf + wid], axis=AX.X, apply_absolute_value=True,
                 wait=[("FS", r)])
            kb.v("tensor_add", small[:, acc:acc + 1], small[:, acc:acc + 1],
                 scr[0:128, SC_T0:SC_T0 + 1], inc=[("V", 1)])
            bguard[buf] = kb.now("V")
    kb.mm(ps2[0:1, 0:2], onesf[:, 0:1], small[:, SM_ABS1:SM_ABS1 + 2],
          wait=[("V", kb.now("V"))], inc=[("T", 1)])
    kb.v("tensor_copy", scr[0:1, SC_T0:SC_T0 + 2], ps2[0:1, 0:2],
         wait=[("T", kb.now("T"))], inc=[("V", 1)])
    kb.dma("sync", t["ar_in"][:, :], scr[0:1, SC_T0:SC_T0 + 2],
           wait=[("V", kb.now("V"))])
    r = kb.relay("sync")
    kb.coll("AllReduce", ALU.add, [t["ar_in"]], [t["ar_out"]],
            wait=[("FS", r)], inc=[("C", 1)])
    kb.dma("sync", scr[0:1, SC_T0:SC_T0 + 2], t["ar_out"][:, :],
           wait=[("C", kb.now("C"))])
    r = kb.relay("sync")
    kb.v("tensor_scalar_mul", scr[0:1, SC_T0:SC_T0 + 1],
         scr[0:1, SC_T0:SC_T0 + 1], float(1.0 / (FF * E)), wait=[("FS", r)])
    kb.v("tensor_scalar_mul", scr[0:1, SC_T0 + 1:SC_T0 + 2],
         scr[0:1, SC_T0 + 1:SC_T0 + 2], float(1.0 / (E * FF)), inc=[("V", 1)])
    kb.mm(psU[:, 100:102], onesf[0:1, :], scr[0:1, SC_T0:SC_T0 + 2],
          wait=[("V", kb.now("V"))], inc=[("T", 1)])
    kb.v("tensor_copy", small[:, SM_M0:SM_M0 + 2], psU[:, 100:102],
         wait=[("T", kb.now("T"))], inc=[("V", 1)])
    Vm0 = kb.now("V")

    kb.v("memset", small[:, SM_ABS1:SM_ABS1 + 2], 0.0, wait=[("V", Vm0)],
         inc=[("V", 1)])
    bguard = {SC_A: kb.now("V"), SC_B: kb.now("V")}
    for mat in range(2):
        nch = 8 if mat == 0 else 16
        wid = 2048 if mat == 0 else 1024
        src = t["w1s"] if mat == 0 else t["w2s"]
        acc = SM_ABS1 if mat == 0 else SM_ABS2
        m0c = small[:, SM_M0 + mat:SM_M0 + mat + 1]
        for j in range(nch):
            buf = SC_A if j % 2 == 0 else SC_B
            kb.dma("sync", scr[:, buf:buf + wid],
                   src[j * 128:(j + 1) * 128, :], wait=[("V", bguard[buf])])
            r = kb.relay("sync")
            kb.v("tensor_scalar_mul", scr[:, SC_RSTD:SC_RSTD + wid],
                 scr[:, buf:buf + wid], -1.0, wait=[("FS", r)])
            kb.v("tensor_max", scr[:, SC_RSTD:SC_RSTD + wid],
                 scr[:, SC_RSTD:SC_RSTD + wid], scr[:, buf:buf + wid])
            kb.v("tensor_scalar_sub", scr[:, SC_RSTD:SC_RSTD + wid],
                 scr[:, SC_RSTD:SC_RSTD + wid], m0c)
            kb.v("reduce_sum", scr[0:128, SC_T1:SC_T1 + 1],
                 scr[:, SC_RSTD:SC_RSTD + wid], axis=AX.X)
            kb.v("tensor_add", small[:, acc:acc + 1], small[:, acc:acc + 1],
                 scr[0:128, SC_T1:SC_T1 + 1], inc=[("V", 1)])
            bguard[buf] = kb.now("V")
    kb.mm(ps2[0:1, 0:2], onesf[:, 0:1], small[:, SM_ABS1:SM_ABS1 + 2],
          wait=[("V", kb.now("V"))], inc=[("T", 1)])
    kb.v("tensor_copy", scr[0:1, SC_T0:SC_T0 + 2], ps2[0:1, 0:2],
         wait=[("T", kb.now("T"))], inc=[("V", 1)])
    kb.dma("sync", t["ar_in"][:, :], scr[0:1, SC_T0:SC_T0 + 2],
           wait=[("V", kb.now("V"))])
    r = kb.relay("sync")
    kb.coll("AllReduce", ALU.add, [t["ar_in"]], [t["ar_out"]],
            wait=[("FS", r)], inc=[("C", 1)])
    kb.dma("sync", scr[0:1, SC_T0:SC_T0 + 2], t["ar_out"][:, :],
           wait=[("C", kb.now("C"))])
    r = kb.relay("sync")
    kb.v("tensor_scalar_mul", scr[0:1, SC_T0:SC_T0 + 1],
         scr[0:1, SC_T0:SC_T0 + 1], float(1.0 / (FF * E)), wait=[("FS", r)])
    kb.v("tensor_scalar_mul", scr[0:1, SC_T0 + 1:SC_T0 + 2],
         scr[0:1, SC_T0 + 1:SC_T0 + 2], float(1.0 / (E * FF)))
    kb.v("tensor_add", scr[0:1, SC_T0:SC_T0 + 2], scr[0:1, SC_T0:SC_T0 + 2],
         small[0:1, SM_M0:SM_M0 + 2])
    kb.v("tensor_scalar_mul", scr[0:1, SC_T0:SC_T0 + 2],
         scr[0:1, SC_T0:SC_T0 + 2], float(1.0 + 3.51e-7))
    kb.v("tensor_scalar_max", scr[0:1, SC_T0:SC_T0 + 2],
         scr[0:1, SC_T0:SC_T0 + 2], 1e-8)
    kb.v("tensor_scalar_mul", scr[0:1, SC_T0:SC_T0 + 2],
         scr[0:1, SC_T0:SC_T0 + 2], 0.5, inc=[("V", 1)])
    kb.mm(psU[:, 100:102], onesf[0:1, :], scr[0:1, SC_T0:SC_T0 + 2],
          wait=[("V", kb.now("V"))], inc=[("T", 1)])
    kb.v("tensor_copy", small[:, SM_THR1:SM_THR1 + 2], psU[:, 100:102],
         wait=[("T", kb.now("T"))], inc=[("V", 1)])
    Vthr = kb.now("V")

    bguard = {SC_A: Vthr, SC_B: Vthr}
    for mat in range(2):
        nch = 8 if mat == 0 else 16
        wid = 2048 if mat == 0 else 1024
        src = t["w1s"] if mat == 0 else t["w2s"]
        dst = t["a32"] if mat == 0 else t["b32"]
        thc = SM_THR1 if mat == 0 else SM_THR2
        thr = small[:, thc:thc + 1]
        for j in range(nch):
            buf = SC_A if j % 2 == 0 else SC_B
            kb.dma("sync", scr[:, buf:buf + wid],
                   src[j * 128:(j + 1) * 128, :], wait=[("V", bguard[buf])])
            r = kb.relay("sync")
            kb.v("tensor_scalar", scr[:, SC_T0:SC_T0 + wid],
                 scr[:, buf:buf + wid], -1.0, thr,
                 op0=ALU.mult, op1=ALU.is_gt, wait=[("FS", r)])
            if mat == 0:
                kb.v("scalar_tensor_tensor", dst[:, j * wid:(j + 1) * wid],
                     scr[:, buf:buf + wid], thr, scr[:, SC_T0:SC_T0 + wid],
                     op0=ALU.is_gt, op1=ALU.subtract, inc=[("V", 1)])
            else:
                kb.v("scalar_tensor_tensor", scr[:, SC_T0:SC_T0 + wid],
                     scr[:, buf:buf + wid], thr, scr[:, SC_T0:SC_T0 + wid],
                     op0=ALU.is_gt, op1=ALU.subtract)
                kb.v("tensor_scalar_mul", dst[:, j * wid:(j + 1) * wid],
                     scr[:, SC_T0:SC_T0 + wid], 0.5, inc=[("V", 1)])
            bguard[buf] = kb.now("V")
    Vq = kb.now("V")

    slot_relay = {0: 0, 1: 0}
    for mat in range(2):
        for grp in range(16):
            g = mat * 16 + grp
            slot = g % 2
            for k in range(8):
                tile = grp * 8 + k
                if mat == 0:
                    fl, dc = tile // 16, tile % 16
                    src = t["a32"][:, fl * 2048 + dc * 128:fl * 2048 + dc * 128 + 128]
                else:
                    dc, fl = tile // 8, tile % 8
                    src = t["b32"][:, dc * 1024 + fl * 128:dc * 1024 + fl * 128 + 128]
                w = []
                if k == 0:
                    w = [("V", kb.now("V"))] if g > 0 else [("V", Vq)]
                kb.tr(t["psT"][:, k * 128:(k + 1) * 128], src, t["idbf_t"],
                      wait=w, inc=([("T", 1)] if k == 7 else []))
            kb.v("tensor_copy", t["qst"][:, slot * 1024:slot * 1024 + 1024],
                 t["psT"][:, :],
                 wait=[("T", kb.now("T"))] +
                      ([("FS", slot_relay[slot])] if slot_relay[slot] else []),
                 inc=[("V", 1)])
            agt = t["ag1_in0"] if mat == 0 else t["ag1_in1"]
            kb.dma("sync",
                   agt[grp * 8:(grp + 1) * 8].rearrange("a p f -> p a f"),
                   t["qst"][:, slot * 1024:slot * 1024 + 1024]
                   .rearrange("p (a f) -> p a f", f=128),
                   wait=[("V", kb.now("V"))])
            slot_relay[slot] = kb.relay("sync")
        if mat == 0:
            r = kb.relay("sync")
            kb.coll("AllGather", ALU.bypass, [t["ag1_in0"]], [t["ag1_out0"]],
                    wait=[("FS", r)], inc=[("C", 1)])
    r = kb.relay("sync")
    kb.coll("AllGather", ALU.bypass, [t["ag1_in1"]], [t["ag1_out1"]],
            wait=[("FS", r)], inc=[("C", 1)])
    t["C_w1"] = kb.now("C") - 1
    t["C_w2"] = kb.now("C")
    tgt = [("FS", kb.now("FS")), ("T", kb.now("T")), ("V", kb.now("V"))]
    for s, v in tgt:
        if v > 0:
            kb.eng.wait_ge(kb.sems[s], v)


def phase_xload(kb, t):
    scr, x, psU, idf = t["scr"], t["x"], t["psU"], t["idf_t"]
    vdone = {}
    for tt in range(8):
        buf = SC_A if tt % 2 == 0 else SC_B
        w = [("V", vdone[tt - 2])] if tt - 2 in vdone else []
        kb.dma("sync", scr[:, buf:buf + 2048],
               t["xs"][tt * 128:(tt + 1) * 128, :], wait=w)
        r = kb.relay("sync")
        for q in range(4):
            for k in range(4):
                dc = q * 4 + k
                kb.tr(psU[:, k * 128:(k + 1) * 128],
                      scr[:, buf + dc * 128:buf + dc * 128 + 128], idf,
                      wait=([("FS", r)] if q == 0 and k == 0 else []) +
                           ([("V", kb.now("V"))] if k == 0 else []),
                      inc=([("T", 1)] if k == 3 else []))
            out = x.rearrange("p (d u) -> p d u", u=T)[:, q * 4:(q + 1) * 4,
                                                       tt * 128:(tt + 1) * 128]
            kb.v("tensor_copy", out,
                 psU[:, 0:512].rearrange("p (d u) -> p d u", u=128),
                 wait=[("T", kb.now("T"))], inc=[("V", 1)])
        vdone[tt] = kb.now("V")
    kb.fence()


def phase_ln(kb, t):
    x, scr, a32 = t["x"], t["scr"], t["a32"]
    ps0, ps2, psU = t["ps0"], t["ps2"], t["psU"]
    onesb, onesf = t["onesb_t"], t["onesf_t"]
    loc = {("sx", 0): (ps2, 0), ("sx", 1): (ps2, 512),
           ("sq", 0): (ps0, 0), ("sq", 1): (ps0, 512)}
    for th in range(2):
        for dc in range(NDC):
            it = th * NDC + dc
            sl = x[:, dc * T + th * 512: dc * T + th * 512 + 512]
            cb = (dc % 2) * 512
            kb.v("tensor_copy", t["expb"][:, cb:cb + 512], sl,
                 wait=([("T", kb.now("T"))] if it >= 2 else []), inc=[("V", 1)])
            kb.act(t["qst"][:, cb:cb + 512], sl, AF.Square,
                   wait=([("T", kb.now("T"))] if it >= 2 else []), inc=[("S", 1)])
            pp, c0 = loc[("sx", th)]
            kb.mm(pp[0:1, c0:c0 + 512], onesb[:, 0:1], t["expb"][:, cb:cb + 512],
                  start=(dc == 0), stop=(dc == NDC - 1),
                  wait=[("V", kb.now("V"))])
            pp, c0 = loc[("sq", th)]
            kb.mm(pp[0:1, c0:c0 + 512], onesb[:, 0:1], t["qst"][:, cb:cb + 512],
                  start=(dc == 0), stop=(dc == NDC - 1),
                  wait=[("S", kb.now("S"))], inc=[("T", 1)])
    Tst = kb.now("T")
    for th in range(2):
        pp, c0 = loc[("sx", th)]
        kb.v("tensor_scalar_mul", scr[0:1, SC_T0 + th * 512:SC_T0 + th * 512 + 512],
             pp[0:1, c0:c0 + 512], 1.0 / E, wait=[("T", Tst)])
        pp, c0 = loc[("sq", th)]
        kb.v("tensor_scalar_mul", scr[0:1, SC_T1 + th * 512:SC_T1 + th * 512 + 512],
             pp[0:1, c0:c0 + 512], 1.0 / E)
    kb.v("tensor_mul", scr[0:1, SC_NMR:SC_NMR + 1024], scr[0:1, SC_T0:SC_T0 + 1024],
         scr[0:1, SC_T0:SC_T0 + 1024])
    kb.v("tensor_sub", scr[0:1, SC_T1:SC_T1 + 1024], scr[0:1, SC_T1:SC_T1 + 1024],
         scr[0:1, SC_NMR:SC_NMR + 1024])
    kb.v("tensor_scalar_add", scr[0:1, SC_T1:SC_T1 + 1024],
         scr[0:1, SC_T1:SC_T1 + 1024], 1e-5, inc=[("V", 1)])
    kb.act(scr[0:1, SC_T1:SC_T1 + 1024], scr[0:1, SC_T1:SC_T1 + 1024], AF.Ln,
           wait=[("V", kb.now("V"))], inc=[("S", 1)])
    kb.act(scr[0:1, SC_T1:SC_T1 + 1024], scr[0:1, SC_T1:SC_T1 + 1024], AF.Exp,
           scale=-0.5, inc=[("S", 1)])
    kb.v("tensor_mul", scr[0:1, SC_T0:SC_T0 + 1024], scr[0:1, SC_T0:SC_T0 + 1024],
         scr[0:1, SC_T1:SC_T1 + 1024], wait=[("S", kb.now("S"))])
    kb.v("tensor_scalar_mul", scr[0:1, SC_T0:SC_T0 + 1024],
         scr[0:1, SC_T0:SC_T0 + 1024], -1.0, inc=[("V", 1)])
    for th in range(2):
        kb.mm(psU[:, 0:512], onesf[0:1, :],
              scr[0:1, SC_T1 + th * 512:SC_T1 + th * 512 + 512],
              wait=[("V", kb.now("V"))], inc=[("T", 1)])
        kb.v("tensor_copy", scr[:, SC_RSTD + th * 512:SC_RSTD + th * 512 + 512],
             psU[:, 0:512], wait=[("T", kb.now("T"))], inc=[("V", 1)])
        kb.mm(psU[:, 0:512], onesf[0:1, :],
              scr[0:1, SC_T0 + th * 512:SC_T0 + th * 512 + 512],
              wait=[("V", kb.now("V"))], inc=[("T", 1)])
        kb.v("tensor_copy", scr[:, SC_NMR + th * 512:SC_NMR + th * 512 + 512],
             psU[:, 0:512], wait=[("T", kb.now("T"))], inc=[("V", 1)])
    Vbc = kb.now("V")
    for dc in range(NDC):
        for th in range(2):
            sl = x[:, dc * T + th * 512: dc * T + th * 512 + 512]
            kb.v("tensor_mul", scr[:, SC_T1:SC_T1 + 512], sl,
                 scr[:, SC_RSTD + th * 512:SC_RSTD + th * 512 + 512],
                 wait=([("V", Vbc)] if dc == 0 and th == 0 else []))
            kb.v("tensor_add", a32[:, dc * T + th * 512: dc * T + th * 512 + 512],
                 scr[:, SC_T1:SC_T1 + 512],
                 scr[:, SC_NMR + th * 512:SC_NMR + th * 512 + 512],
                 inc=[("V", 1)])
    kb.fence()


def phase_ffn(kb, t):
    a32, b32, x, wstr = t["a32"], t["b32"], t["x"], t["wstr"]
    ps0, ps1 = t["ps0"], t["ps1"]
    ag1w1, ag1w2 = t["ag1_out0"], t["ag1_out1"]
    prev = t.setdefault("_ffn_prev", {"T2": 0, "S30": 0, "S31": 0, "V2a": 0, "V2b": 0})
    for th in range(2):
        for fh in range(2):
            tdone, sdone, rr_ = {}, {}, {}
            Tb1 = kb.now("T")
            for fl in range(32):
                fc = fh * 32 + fl
                rk, fli = fc // 8, fc % 8
                ws = (fl % 4) * 2048
                w = [("T", Tb1 + fl - 3)] if fl >= 4 else \
                    [("T", prev["T2"]), ("C", t.get("C_w1", 0))]
                kb.dma("sync", wstr[:, ws:ws + 2048]
                       .rearrange("p (a f) -> p a f", f=128),
                       ag1w1[rk, fli * 16:(fli + 1) * 16]
                       .rearrange("a p f -> p a f"), wait=w,
                       qsem="L" + str(fl % 3))
                rr_[fl] = kb.relay("sync", qsem="L" + str(fl % 3))
            for fl in range(32):
                fc = fh * 32 + fl
                ws = (fl % 4) * 2048
                pslot = (fl % 2) * 512
                for dc in range(NDC):
                    w = []
                    if dc == 0:
                        w = [("FS", rr_[fl])]
                        if fl - 2 in sdone:
                            w.append(("S", sdone[fl - 2]))
                        else:
                            w.append(("S", prev["S30"] if fl == 0 else prev["S31"]))
                    kb.mm(ps0[:, pslot:pslot + 512],
                          wstr[:, ws + dc * 128:ws + dc * 128 + 128],
                          a32[:, dc * T + th * 512:dc * T + th * 512 + 512],
                          start=(dc == 0), stop=(dc == NDC - 1),
                          wait=w, inc=([("T", 1)] if dc == NDC - 1 else []))
                tdone[fl] = kb.now("T")
                assert tdone[fl] == Tb1 + fl + 1
                kb.act(b32[:, fl * 512:fl * 512 + 512], ps0[:, pslot:pslot + 512],
                       AF.Gelu, bias=t["b1_t"][:, fc:fc + 1],
                       wait=[("T", tdone[fl])], inc=[("S", 1)])
                sdone[fl] = kb.now("S")
            Sh = kb.now("S")
            tdone2, vdone2, r2_ = {}, {}, {}
            Tmm1 = kb.now("T")
            for dc in range(NDC):
                ws = (dc % 2) * 4096
                w = [("T", Tmm1 + dc - 1)] if dc >= 2 else \
                    [("T", Tmm1), ("C", t.get("C_w2", 0))]
                for rr in range(4):
                    rk = fh * 4 + rr
                    kb.dma("sync", wstr[:, ws + rr * 1024:ws + rr * 1024 + 1024]
                           .rearrange("p (a f) -> p a f", f=128),
                           ag1w2[rk, dc * 8:(dc + 1) * 8]
                           .rearrange("a p f -> p a f"),
                           wait=(w if rr == 0 else []),
                           qsem="L" + str(dc % 2))
                r2_[dc] = kb.relay("sync", qsem="L" + str(dc % 2))
            for dc in range(NDC):
                ws = (dc % 2) * 4096
                pslot = (dc % 2) * 512
                for fl in range(32):
                    w = []
                    if fl == 0:
                        w = [("FS", r2_[dc]), ("S", Sh)]
                        if dc - 2 in vdone2:
                            w.append(("V", vdone2[dc - 2]))
                    if dc < 2 and fl == 0:
                        w.append(("V", prev["V2a"] if dc == 0 else prev["V2b"]))
                    kb.mm(ps1[:, pslot:pslot + 512],
                          wstr[:, ws + fl * 128:ws + fl * 128 + 128],
                          b32[:, fl * 512:fl * 512 + 512],
                          start=(fl == 0), stop=(fl == 31),
                          wait=w, inc=([("T", 1)] if fl == 31 else []))
                tdone2[dc] = kb.now("T")
                assert tdone2[dc] == Tmm1 + dc + 1
                xs = x[:, dc * T + th * 512: dc * T + th * 512 + 512]
                if fh == 0:
                    kb.v("scalar_tensor_tensor", xs, ps1[:, pslot:pslot + 512],
                         t["b2h_t"][:, dc:dc + 1], xs, op0=ALU.add, op1=ALU.add,
                         wait=[("T", tdone2[dc])], inc=[("V", 1)])
                else:
                    kb.v("tensor_add", xs, xs, ps1[:, pslot:pslot + 512],
                         wait=[("T", tdone2[dc])], inc=[("V", 1)])
                vdone2[dc] = kb.now("V")
            prev["T2"] = tdone2[NDC - 1]
            prev["S30"], prev["S31"] = sdone[30], sdone[31]
            prev["V2a"], prev["V2b"] = vdone2[NDC - 2], vdone2[NDC - 1]


def phase_qkv(kb, t):
    a32, scr, qst, wstr = t["a32"], t["scr"], t["qst"], t["wstr"]
    ps0, ps2, psT = t["ps0"], t["ps2"], t["psT"]
    wsrc = [t["wq"], t["wk"], t["wv"]]
    tdone, edone, urelay = {}, {}, {}
    Tq0 = kb.now("T")
    rg_ = {}
    for pi in range(3):
        for ec in range(H):
            u = pi * H + ec
            ws = (u % 2) * 4096
            w = [("T", Tq0 + 2 * (u - 2) + 1)] if u >= 2 else []
            for piece in range(2):
                kb.dma("gpsimd", wstr[:, ws + piece * 1024:ws + piece * 1024 + 1024]
                       .rearrange("p (a f) -> p a f", f=128),
                       wsrc[pi][piece * 1024:(piece + 1) * 1024,
                                ec * 128:(ec + 1) * 128]
                       .rearrange("(a p) f -> p a f", p=128),
                       wait=(w if piece == 0 else []),
                       qsem="G" + str(u % 2))
            rg_[u] = kb.relay("gpsimd", qsem="G" + str(u % 2))
    for pi in range(3):
        for ec in range(H):
            u = pi * H + ec
            ws = (u % 2) * 4096
            r = rg_[u]
            for th in range(2):
                for dc in range(NDC):
                    w = []
                    if th == 0 and dc == 0:
                        w = [("FG", r)]
                        if u - 1 in edone:
                            w.append(("S", edone[u - 1][0]))
                            w.append(("V", edone[u - 1][1]))
                    kb.mm(ps0[:, th * 512:th * 512 + 512],
                          wstr[:, ws + dc * 128:ws + dc * 128 + 128],
                          a32[:, dc * T + th * 512:dc * T + th * 512 + 512],
                          start=(dc == 0), stop=(dc == NDC - 1), wait=w,
                          inc=([("T", 1)] if th == 1 and dc == NDC - 1 else []))
            tdone[u] = kb.now("T")
            assert tdone[u] == Tq0 + 2 * u + 1, (u, tdone[u], Tq0)
            if u == H:
                kb.coll("AllGather", ALU.bypass, [t["ag2q_in"]], [t["ag2q_out"]],
                        wait=[("FS", urelay[H - 1])], inc=[("C", 1)])
                t["C_q"] = kb.now("C")
            if u == 2 * H:
                kb.coll("AllGather", ALU.bypass, [t["ag2k_in"]], [t["ag2k_out"]],
                        wait=[("FS", urelay[2 * H - 1])], inc=[("C", 1)])
                t["C_k"] = kb.now("C")
            qs = (u % 2) * 1024
            wsl = [("FS", urelay[u - 2])] if u - 2 in urelay else []
            wst = [("FS", urelay[u - 1])] if u - 1 in urelay else []
            if DBG_SKIP_QKV_TAIL:
                kb.scopy(qst[:, qs:qs + 1024], ps0[:, :], wait=[("T", tdone[u])],
                         inc=[("S", 1)])
                edone[u] = (kb.now("S"), kb.now("V"))
            elif pi < 2:
                kb.scopy(scr[:, SC_T1:SC_T1 + 1024], ps0[:, :],
                         wait=[("T", tdone[u])] +
                              ([("FS", urelay[u - 1])] if u - 1 in urelay else []),
                         inc=[("S", 1)])
                Sev = kb.now("S")
                kb.dma("sync", scr[0:64, SC_NMR:SC_NMR + 1024],
                       scr[64:128, SC_T1:SC_T1 + 1024], wait=[("S", Sev)])
                kb.dma("sync", scr[64:128, SC_NMR:SC_NMR + 1024],
                       scr[0:64, SC_T1:SC_T1 + 1024])
                rsw = kb.relay("sync")
                kb.v("tensor_mul", scr[:, SC_A:SC_A + 1024],
                     scr[:, SC_T1:SC_T1 + 1024], t["cos"][:, :],
                     wait=[("S", Sev)])
                kb.v("tensor_mul", scr[:, SC_B:SC_B + 1024],
                     scr[:, SC_NMR:SC_NMR + 1024], t["sin"][:, :],
                     wait=[("FS", rsw)])
                kb.v("tensor_add", qst[:, qs:qs + 1024], scr[:, SC_A:SC_A + 1024],
                     scr[:, SC_B:SC_B + 1024], inc=[("V", 1)], wait=wsl)
                kb.act(scr[:, SC_A:SC_A + 1024], qst[:, qs:qs + 1024], AF.Square,
                       wait=[("V", kb.now("V"))], inc=[("S", 1)])
                for th in range(2):
                    kb.mm(ps2[0:1, th * 512:th * 512 + 512], t["onesf_t"][:, 0:1],
                          scr[:, SC_A + th * 512:SC_A + th * 512 + 512],
                          wait=([("S", kb.now("S")), ("V", kb.now("V"))]
                                if th == 0 else []),
                          inc=([("T", 1)] if th == 1 else []))
                kb.v("reduce_max", t["small"][0:1, 8 + pi * H + ec:9 + pi * H + ec],
                     ps2[0:1, 0:1024], axis=AX.X,
                     wait=[("T", kb.now("T"))], inc=[("V", 1)])
                agt = t["ag2q_in"] if pi == 0 else t["ag2k_in"]
                kb.dma("sync",
                       agt[ec].rearrange("(p u) -> p u", p=128),
                       qst[:, qs:qs + 1024], wait=[("V", kb.now("V"))])
                urelay[u] = kb.relay("sync")
                edone[u] = (kb.now("S"), kb.now("V"))
            else:
                kb.scopy(qst[:, qs:qs + 1024], ps0[:, :],
                         wait=[("T", tdone[u])] + wsl, inc=[("S", 1)])
                for tc in range(8):
                    kb.tr(psT[:, tc * 128:(tc + 1) * 128],
                          qst[:, qs + tc * 128:qs + tc * 128 + 128], t["idbf_t"],
                          wait=([("S", kb.now("S")), ("V", kb.now("V"))]
                                if tc == 0 else []),
                          inc=([("T", 1)] if tc == 7 else []))
                kb.v("tensor_copy", t["expb"][:, :], psT[:, :],
                     wait=[("T", kb.now("T"))] + wst, inc=[("V", 1)])
                kb.dma("sync",
                       t["ag2v_in"][ec].rearrange("(p a u) -> p a u", p=128, a=8),
                       t["expb"][:, :].rearrange("p (a u) -> p a u", a=8),
                       wait=[("V", kb.now("V"))])
                urelay[u] = kb.relay("sync")
                edone[u] = (kb.now("S"), kb.now("V"))
    kb.dma("sync", t["nmx_in"].rearrange("(h c) -> h c", h=1),
           t["small"][0:1, 8:8 + 2 * H], wait=[("V", kb.now("V"))])
    r = kb.relay("sync")
    kb.coll("AllGather", ALU.bypass, [t["nmx_in"]], [t["nmx_out"]],
            wait=[("FS", r), ("FG", kb.now("FG"))], inc=[("C", 1)])
    t["C_nmx"] = kb.now("C")
    kb.coll("AllGather", ALU.bypass, [t["ag2v_in"]], [t["ag2v_out"]],
            inc=[("C", 1)])
    t["C_v"] = kb.now("C")


def phase_attention(kb, t):
    b32, scr, small = t["b32"], t["scr"], t["small"]
    ps0, ps1, ps2, psU = t["ps0"], t["ps1"], t["ps2"], t["psU"]
    onesf, onesb = t["onesf_t"], t["onesb_t"]

    for lh in range(NHL):
        kb.dma_dyn("sync",
                   b32[:, lh * S:(lh + 1) * S].rearrange("p (r u) -> p r u", u=T),
                   lambda pid, lh=lh:
                   t["ag2q_out"][0:NCORE, bass.ds(kb.hsel(lh), 1)]
                   .rearrange("r h (p u) -> p (r h) u", p=128),
                   wait=([("C", t["C_q"])] if lh == 0 else []))
    for lh in range(NHL):
        kb.dma_dyn("sync",
                   t["kvl"][0, lh].rearrange("p (r u) -> p r u", u=T),
                   lambda pid, lh=lh:
                   t["ag2k_out"][0:NCORE, bass.ds(kb.hsel(lh), 1)]
                   .rearrange("r h (p u) -> p (r h) u", p=128),
                   wait=([("C", t["C_k"])] if lh == 0 else []))
        kb.dma_dyn("sync",
                   t["kvl"][1, lh].rearrange("p (r g) -> p r g", g=T),
                   lambda pid, lh=lh:
                   t["ag2v_out"][0:NCORE, bass.ds(kb.hsel(lh), 1)]
                   .rearrange("r h (p a u) -> p (r h) (a u)", p=128, a=8),
                   wait=([("C", t["C_v"])] if lh == 0 else []))
    kb.dma("sync", scr[0:16, SC_T0:SC_T0 + 16],
           t["nmx_out"].rearrange("r (p h) -> h (r p)", p=2), slow=True,
           wait=[("C", t["C_nmx"])])
    r = kb.relay("sync")
    kb.v("reduce_max", scr[0:16, SC_T1:SC_T1 + 1],
         scr[0:16, SC_T0:SC_T0 + 16:2], axis=AX.X, wait=[("FS", r)])
    kb.v("reduce_max", scr[0:16, SC_T1 + 1:SC_T1 + 2],
         scr[0:16, SC_T0 + 1:SC_T0 + 16:2], axis=AX.X)
    kb.v("tensor_mul", scr[0:16, SC_T1 + 2:SC_T1 + 3],
         scr[0:16, SC_T1:SC_T1 + 1], scr[0:16, SC_T1 + 1:SC_T1 + 2],
         inc=[("V", 1)])
    kb.act(scr[0:16, SC_T1 + 2:SC_T1 + 3], scr[0:16, SC_T1 + 2:SC_T1 + 3],
           AF.Sqrt, wait=[("V", kb.now("V"))], inc=[("S", 1)])
    kb.v("tensor_scalar_mul", scr[0:16, SC_T1 + 2:SC_T1 + 3],
         scr[0:16, SC_T1 + 2:SC_T1 + 3], -ISD * 1.001,
         wait=[("S", kb.now("S"))], inc=[("V", 1)])
    kb.dma("sync", t["negc_d"].rearrange("(h c) -> h c", c=1),
           scr[0:16, SC_T1 + 2:SC_T1 + 3], wait=[("V", kb.now("V"))])
    r = kb.relay("sync")
    for lh in range(NHL):
        kb.dma_dyn("sync", scr[0:1, SC_T0 + 40 + lh:SC_T0 + 41 + lh],
                   lambda pid, lh=lh: t["negc_d"]
                   [bass.ds(kb.hsel(lh), 1)].rearrange("(h c) -> h c", c=1),
                   wait=([("FS", r)] if lh == 0 else []))
    r = kb.relay("sync")
    for lh in range(NHL):
        kb.mm(psU[:, 200 + lh:201 + lh], onesf[0:1, :],
              scr[0:1, SC_T0 + 40 + lh:SC_T0 + 41 + lh],
              wait=([("FS", r)] if lh == 0 else []), inc=[("T", 1)])
        kb.v("tensor_copy", small[:, SM_NEGC0 + lh:SM_NEGC0 + lh + 1],
             psU[:, 200 + lh:201 + lh], wait=[("T", kb.now("T"))], inc=[("V", 1)])
    kb.fence()

    piece_tcons = {}
    ps0_free = {0: 0, 1: 0}
    expb_free = {0: 0, 1: 0}
    av_free = {0: 0, 1: 0}
    sum_free = {0: 0, 512: 0}
    gseq = 0
    for lh in range(NHL):
        for i in range(16):
            nkb = 4 * (i + 1)
            tile = lh * 16 + i
            sumcol = (tile % 2) * 512
            avs = (tile % 2) * 512
            qsl = b32[:, lh * S + i * 512: lh * S + i * 512 + 512]
            npieces = (nkb + 7) // 8
            base_gseq = gseq
            Tt0 = kb.now("T")
            piece_r = {}
            for p_ in range(npieces):
                p = base_gseq + p_
                rk = p_
                ks = (p % 2) * 1024
                w = [("T", piece_tcons[p - 2])] if p - 2 in piece_tcons else []
                kb.dma("sync", t["kbuf"][:, ks:ks + 1024],
                       t["kvl"][0, lh][:, rk * T:(rk + 1) * T], wait=w,
                       qsem="L" + str(p % 3))
                kb.dma("sync", t["vbuf"][:, ks:ks + 1024],
                       t["kvl"][1, lh][:, rk * T:(rk + 1) * T],
                       qsem="L" + str(p % 3))
                piece_r[p] = kb.relay("sync", qsem="L" + str(p % 3))
                last_j = min(8 * p_ + 7, nkb - 1)
                piece_tcons[p] = (Tt0 + 2 * last_j + 3 if last_j < nkb - 1
                                  else Tt0 + 2 * nkb)
            gseq += npieces

            def unit_tail(k):
                es = (k % 2) * 512
                diag = k >= nkb - 4
                gate = [("V", vgate[k])] if diag else [("S", sgate[k])]
                w = list(gate)
                if k == 0 and sum_free[sumcol]:
                    w.append(("V", sum_free[sumcol]))
                kb.mm(ps2[0:1, sumcol:sumcol + 512], onesb[:, 0:1],
                      t["expb"][:, es:es + 512],
                      start=(k == 0), stop=(k == nkb - 1), wait=w)
                w = []
                if k == 0 and av_free[tile % 2]:
                    w.append(("V", av_free[tile % 2]))
                koff = ((base_gseq + k // 8) % 2) * 1024 + (k % 8) * 128
                kb.mm(ps1[:, avs:avs + 512], t["vbuf"][:, koff:koff + 128],
                      t["expb"][:, es:es + 512],
                      start=(k == 0), stop=(k == nkb - 1),
                      wait=w, inc=[("T", 1)])
                expb_free[k % 2] = kb.now("T")

            sgate, vgate = {}, {}
            for k in range(nkb):
                p = base_gseq + k // 8
                ks = (p % 2) * 1024
                koff = ks + (k % 8) * 128
                ss = (k % 2) * 512
                es = (k % 2) * 512
                w = [("FS", piece_r[p])]
                if ps0_free[k % 2]:
                    w.append(("S", ps0_free[k % 2]))
                kb.mm(ps0[:, ss:ss + 512], t["kbuf"][:, koff:koff + 128], qsl,
                      start=True, stop=True, wait=w, inc=[("T", 1)])
                tsc = kb.now("T")
                assert tsc == (Tt0 + 1 if k == 0 else Tt0 + 2 * k), \
                    (tile, k, tsc, Tt0)
                kb.act(t["expb"][:, es:es + 512], ps0[:, ss:ss + 512], AF.Exp,
                       bias=small[:, SM_NEGC0 + lh:SM_NEGC0 + lh + 1], scale=ISD,
                       wait=[("T", max(tsc, expb_free[k % 2]))], inc=[("S", 1)])
                ps0_free[k % 2] = kb.now("S")
                sgate[k] = kb.now("S")
                if k >= nkb - 4:
                    j = k - (nkb - 4)
                    kb.v("tensor_mul", t["expb"][:, es:es + 512],
                         t["expb"][:, es:es + 512],
                         t["mask"][:, j * 512:(j + 1) * 512],
                         wait=[("S", sgate[k])], inc=[("V", 1)])
                    vgate[k] = kb.now("V")
                if k >= 1:
                    unit_tail(k - 1)
            unit_tail(nkb - 1)
            Ttile = kb.now("T")
            kb.v("tensor_copy", scr[0:1, SC_T0:SC_T0 + 512],
                 ps2[0:1, sumcol:sumcol + 512], wait=[("T", Ttile)])
            kb.v("reciprocal", scr[0:1, SC_T0:SC_T0 + 512],
                 scr[0:1, SC_T0:SC_T0 + 512], inc=[("V", 1)])
            sum_free[sumcol] = kb.now("V")
            kb.mm(psU[:, 0:512], onesf[0:1, :], scr[0:1, SC_T0:SC_T0 + 512],
                  wait=[("V", kb.now("V"))], inc=[("T", 1)])
            kb.v("tensor_copy", scr[:, SC_NMR:SC_NMR + 512], psU[:, 0:512],
                 wait=[("T", kb.now("T"))])
            kb.v("tensor_mul", qsl, ps1[:, avs:avs + 512],
                 scr[:, SC_NMR:SC_NMR + 512], inc=[("V", 1)])
            av_free[tile % 2] = kb.now("V")
        agt = t["ag3a_in"] if lh == 0 else t["ag3b_in"]
        ago = t["ag3a_out"] if lh == 0 else t["ag3b_out"]
        kb.dma("gpsimd", agt[:, :], b32[:, lh * S:(lh + 1) * S],
               wait=[("V", kb.now("V"))], qsem="G" + str(lh))
        rr = kb.relay("gpsimd", qsem="G" + str(lh))
        kb.coll("AllGather", ALU.bypass, [agt], [ago],
                wait=[("FG", rr)], inc=[("C", 1)])
        t["C_ag3" + str(lh)] = kb.now("C")
    a32v = t["a32"][:, :].rearrange("p (g u) -> p g u", u=T)
    kb.dma_dyn("sync", a32v[:, 0:16:2, :],
               lambda pid: t["ag3a_out"][0:NCORE, :, bass.ds(kb.town(), T)]
               .rearrange("r p u -> p r u"),
               wait=[("C", t["C_ag30"])])
    kb.dma_dyn("sync", a32v[:, 1:16:2, :],
               lambda pid: t["ag3b_out"][0:NCORE, :, bass.ds(kb.town(), T)]
               .rearrange("r p u -> p r u"),
               wait=[("C", t["C_ag31"])])
    kb.relay("sync")
    kb.fence()

    tdone, vdone = {}, {}
    for dc in range(NDC):
        ws = (dc % 2) * 4096
        w = [("T", tdone[dc - 2])] if dc - 2 in tdone else []
        for piece in range(2):
            kb.dma("gpsimd", t["wstr"][:, ws + piece * 1024:ws + piece * 1024 + 1024]
                   .rearrange("p (a f) -> p a f", f=128),
                   t["wo"][piece * 1024:(piece + 1) * 1024, dc * 128:(dc + 1) * 128]
                   .rearrange("(a p) f -> p a f", p=128),
                   wait=(w if piece == 0 else []))
        r = kb.relay("gpsimd")
        for th in range(2):
            pslot = ((dc * 2 + th) % 2) * 512
            for ec in range(H):
                w = []
                if ec == 0:
                    w = [("FG", r)]
                    k = dc * 2 + th - 2
                    if k in vdone:
                        w.append(("V", vdone[k]))
                kb.mm(t["ps1"][:, pslot:pslot + 512],
                      t["wstr"][:, ws + ec * 128:ws + ec * 128 + 128],
                      t["a32"][:, ec * T + th * 512:ec * T + th * 512 + 512],
                      start=(ec == 0), stop=(ec == H - 1),
                      wait=w, inc=([("T", 1)] if ec == H - 1 else []))
            xs = t["x"][:, dc * T + th * 512: dc * T + th * 512 + 512]
            kb.v("tensor_add", xs, xs, t["ps1"][:, pslot:pslot + 512],
                 wait=[("T", kb.now("T"))], inc=[("V", 1)])
            vdone[dc * 2 + th] = kb.now("V")
        tdone[dc] = kb.now("T")
    kb.fence()


def phase_store(kb, t, src, dtype):
    scr = t["scr"]
    ident = t["idf_t"] if dtype == F32 else t["idbf_t"]
    psx = t["psU"] if dtype == F32 else t["psT"]
    rdone = {}
    for tt in range(8):
        buf = SC_A if tt % 2 == 0 else SC_B
        for q in range(4):
            for k in range(4):
                dc = q * 4 + k
                sl = src[:, dc * T + tt * 128: dc * T + tt * 128 + 128]
                w = []
                if k == 0:
                    w = [("V", kb.now("V"))]
                    if tt - 2 in rdone and q == 0:
                        w.append(("FS", rdone[tt - 2]))
                kb.tr(psx[:, k * 128:(k + 1) * 128], sl, ident,
                      wait=w, inc=([("T", 1)] if k == 3 else []))
            kb.v("tensor_copy", scr[:, buf + q * 512:buf + q * 512 + 512],
                 psx[:, 0:512],
                 wait=[("T", kb.now("T"))] +
                      ([("FS", rdone[tt - 2])] if tt - 2 in rdone else []),
                 inc=[("V", 1)])
        kb.dma("sync", t["out"][tt * 128:(tt + 1) * 128, :],
               scr[:, buf:buf + 2048], wait=[("V", kb.now("V"))])
        rdone[tt] = kb.relay("sync")
    kb.fence()


def emit_stream(kb, t):
    stage = DEBUG_STAGE
    t.pop("_ffn_prev", None)
    phase_consts(kb, t)
    if stage >= 2:
        phase_wprep(kb, t)
    phase_xload(kb, t)
    if stage == 0:
        phase_store(kb, t, t["x"], F32)
        return
    phase_ln(kb, t)
    if stage == 1:
        phase_store(kb, t, t["a32"], BF16)
        return
    phase_ffn(kb, t)
    if stage == 2:
        phase_store(kb, t, t["x"], F32)
        return
    phase_ln(kb, t)
    phase_qkv(kb, t)
    if stage == 3:
        phase_store(kb, t, t["x"], F32)
        return
    phase_attention(kb, t)
    if stage == 4:
        phase_store(kb, t, t["x"], F32)
        return
    phase_ln(kb, t)
    phase_ffn(kb, t)
    phase_store(kb, t, t["x"], F32)


def build_nc():
    nc = bass.Bass("TRN2", target_bir_lowering=False, debug=False)
    es = ExitStack()
    t = declare(nc, es)
    sems = {}
    for name in SEMS:
        sems[name] = es.enter_context(nc.semaphore("sem" + name))
    block = es.enter_context(nc.Block())
    counts = {}

    def make(eng_name):
        def run(eng):
            kbx = KB(nc, eng_name, eng, sems, t)
            emit_stream(kbx, t)
            counts[eng_name] = dict(kbx.c)
        return run

    for eng_name in ENGS:
        getattr(block, eng_name)(make(eng_name))
    vals = list(counts.values())
    for v in vals[1:]:
        assert v == vals[0], f"sem count mismatch: {counts}"
    es.close()
    return nc


def host_inputs(full):
    x = np.asarray(full["L0_attn_x"]).reshape(S, E)
    w1 = np.asarray(full["L0_ffn_w1"])
    w2 = np.asarray(full["L0_ffn_w2"])
    b1 = np.ascontiguousarray(np.asarray(full["L0_ffn_b1"]).reshape(64, 128))
    b2h = np.ascontiguousarray(
        (np.asarray(full["L0_ffn_b2"]) * 0.5).astype(np.float32).reshape(16, 128))

    pos = np.arange(S, dtype=np.float64)
    inv = 1.0 / (10000.0 ** (np.arange(0, D, 2, dtype=np.float64) / D))
    ang = pos[:, None] * inv[None, :]
    cos = np.cos(ang).T
    sin = np.sin(ang).T
    cosf = np.concatenate([cos, cos], 0).astype(np.float32)
    sinf = np.concatenate([-sin, sin], 0).astype(np.float32)

    kk = np.arange(128)[:, None]
    qq = np.arange(512)[None, :]
    masks = np.stack([(qq >= kk + 128 * j) for j in range(4)]) \
        .astype(ml_dtypes.bfloat16)

    shared = {
        "wq": np.asarray(full["L0_attn_wq"]), "wk": np.asarray(full["L0_attn_wk"]),
        "wv": np.asarray(full["L0_attn_wv"]), "wo": np.asarray(full["L0_attn_wo"]),
        "b1": b1, "b2h": b2h, "masks": np.asarray(masks),
        "idbf": np.asarray(np.eye(128, dtype=ml_dtypes.bfloat16)),
        "idf": np.eye(128, dtype=np.float32),
        "onesf": np.ones((128, 128), np.float32),
        "onesb": np.asarray(np.ones((128, 8), ml_dtypes.bfloat16)),
    }
    maps = []
    for c in range(NCORE):
        m = dict(shared)
        m["xs"] = np.ascontiguousarray(x[c * T:(c + 1) * T])
        m["w1s"] = np.ascontiguousarray(w1[c * FSL:(c + 1) * FSL])
        m["w2s"] = np.ascontiguousarray(w2[:, c * FSL:(c + 1) * FSL])
        m["cosf"] = np.ascontiguousarray(cosf[:, c * T:(c + 1) * T])
        m["sinf"] = np.ascontiguousarray(sinf[:, c * T:(c + 1) * T])
        maps.append(m)
    return maps


def kernel(**inputs):
    maps = host_inputs(inputs)
    nc = build_nc()
    res = run_bass_kernel_spmd(nc, maps, core_ids=list(range(NCORE)))
    out = np.concatenate([res.results[c]["out"] for c in range(NCORE)], axis=0)
    return np.ascontiguousarray(out.reshape(1, S, E).astype(np.float32))


# version 10
# speedup vs baseline: 1.3611x; 1.0153x over previous
import numpy as np
import ml_dtypes
import concourse.bass as bass
import concourse.mybir as mybir
from concourse.bass_utils import run_bass_kernel_spmd
from contextlib import ExitStack

F32 = mybir.dt.float32
BF16 = mybir.dt.bfloat16
AF = mybir.ActivationFunctionType
ALU = mybir.AluOpType
AX = mybir.AxisListType

NCORE = 8
S, E, H, D, FF = 8192, 2048, 16, 128, 8192
T = S // NCORE
NDC = E // 128
FSL = FF // NCORE
NHL = 2
ISD = float(1.0 / np.sqrt(128.0))

DEBUG_STAGE = 99
DBG_SKIP_AG2 = False
DBG_SKIP_QKV_TAIL = False

ENGS = ("tensor", "vector", "scalar", "sync", "gpsimd")
SEMS = ("L", "L0", "L1", "L2", "G", "G0", "G1", "T", "V", "S", "C", "FS", "FG")

SC_A = 0
SC_B = 2048
SC_RSTD = 4096
SC_NMR = 5120
SC_T0 = 6144
SC_T1 = 7168
SM_THR1, SM_THR2, SM_NEGC0, SM_NEGC1, SM_ABS1, SM_ABS2 = 0, 1, 2, 3, 4, 5
SM_M0 = 6
SM_RES = 40


class KB:
    def __init__(self, nc, which, eng, sems, t):
        self.nc, self.which, self.eng, self.sems, self.t = nc, which, eng, sems, t
        self.c = {k: 0 for k in SEMS}
        self._pidv = None

    def pid(self):
        if self._pidv is None:
            self._pidv = self.eng.partition_id()
        return self._pidv

    def hsel(self, lh):
        if not hasattr(self, "_hsel"):
            p = self.pid()
            self._hsel = [p * NHL + j for j in range(NHL)]
        return self._hsel[lh]

    def town(self):
        if not hasattr(self, "_town"):
            self._town = self.pid() * T
        return self._town

    def on(self, name):
        return self.which == name

    def now(self, s):
        return self.c[s]

    def _go(self, engname, fn, inc, wait):
        if self.which == engname:
            for s, v in wait:
                if v > 0:
                    self.eng.wait_ge(self.sems[s], v)
            ins = fn(self.eng)
            for s, n in inc:
                ins = ins.then_inc(self.sems[s], n)
        for s, n in inc:
            self.c[s] += n

    def relay(self, q, qsem=None, extra_wait=()):
        if qsem is None:
            qsem = "L" if q == "sync" else "G"
        rsem = "FS" if q == "sync" else "FG"
        if self.which == q:
            for s, v in extra_wait:
                if v > 0:
                    self.eng.wait_ge(self.sems[s], v)
            if self.c[qsem] > 0:
                self.eng.wait_ge(self.sems[qsem], self.c[qsem])
            self.eng.sem_inc(self.sems[rsem], 1)
        self.c[rsem] += 1
        return self.c[rsem]

    def fence(self):
        if self.which == "sync":
            for s in ("L", "L0", "L1", "L2"):
                if self.c[s] > 0:
                    self.eng.wait_ge(self.sems[s], self.c[s])
            self.eng.sem_inc(self.sems["FS"], 1)
        if self.which == "gpsimd":
            for s in ("G", "G0", "G1"):
                if self.c[s] > 0:
                    self.eng.wait_ge(self.sems[s], self.c[s])
            self.eng.sem_inc(self.sems["FG"], 1)
        self.c["FS"] += 1
        self.c["FG"] += 1
        tgt = [(s, self.c[s]) for s in ("FS", "FG", "T", "V", "S", "C")]
        for s, v in tgt:
            if v > 0:
                self.eng.wait_ge(self.sems[s], v)

    def mm(self, out, lhsT, rhs, start=True, stop=True, inc=(), wait=()):
        self._go("tensor",
                 lambda e: e.matmul(out, lhsT, rhs, start=start, stop=stop),
                 inc, wait)

    def tr(self, out, in_, ident, inc=(), wait=()):
        if not isinstance(ident, bass.AP):
            ident = ident[:, :]
        self._go("tensor", lambda e: e.transpose(out, in_, ident), inc, wait)

    def act(self, out, in_, func, bias=0.0, scale=1.0, inc=(), wait=()):
        w = [("S", self.c["S"])] + list(wait)
        ii = [x for x in inc if x[0] != "S"] + [("S", 1)]
        self._go("scalar",
                 lambda e: e.activation(out=out, in_=in_, func=func,
                                        bias=bias, scale=scale), ii, w)

    def scopy(self, out, in_, inc=(), wait=()):
        w = [("S", self.c["S"])] + list(wait)
        ii = [x for x in inc if x[0] != "S"] + [("S", 1)]
        self._go("scalar", lambda e: e.copy(out=out, in_=in_), ii, w)

    def v(self, method, *a, inc=(), wait=(), **kw):
        w = [("V", self.c["V"])] + list(wait)
        ii = [x for x in inc if x[0] != "V"] + [("V", 1)]
        self._go("vector", lambda e: getattr(e, method)(*a, **kw), ii, w)

    def dma(self, q, out, in_, inc=None, wait=(), slow=False, qsem=None):
        if inc is None:
            inc = [((qsem or ("L" if q == "sync" else "G")), 16)]
        self._go(q, lambda e: e.dma_start(out=out, in_=in_,
                                          allow_slow_non_contiguous=slow),
                 inc, wait)

    def dma_dyn(self, q, out, in_fn, inc=None, wait=(), qsem=None):
        if inc is None:
            inc = [((qsem or ("L" if q == "sync" else "G")), 16)]
        self._go(q, lambda e: e.dma_start(out=out, in_=in_fn(self.pid())),
                 inc, wait)

    def coll(self, kind, op, ins, outs, inc=(), wait=()):
        self._go("gpsimd",
                 lambda e: e.collective_compute(
                     kind, op, replica_groups=[list(range(NCORE))],
                     ins=[a.opt() for a in ins], outs=[a.opt() for a in outs]),
                 inc, wait)


def declare(nc, es):
    t = {}
    din = lambda n, sh, dt: nc.dram_tensor(n, sh, dt, kind="ExternalInput").ap()
    t["xs"] = din("xs", [T, E], F32)
    t["w1s"] = din("w1s", [FSL, E], F32)
    t["w2s"] = din("w2s", [E, FSL], F32)
    t["wq"] = din("wq", [E, E], F32)
    t["wk"] = din("wk", [E, E], F32)
    t["wv"] = din("wv", [E, E], F32)
    t["wo"] = din("wo", [E, E], F32)
    t["b1"] = din("b1", [64, 128], F32)
    t["b2h"] = din("b2h", [16, 128], F32)
    t["cosf"] = din("cosf", [128, T], F32)
    t["sinf"] = din("sinf", [128, T], F32)
    t["masks"] = din("masks", [4, 128, 512], BF16)
    t["idbf"] = din("idbf", [128, 128], BF16)
    t["idf"] = din("idf", [128, 128], F32)
    t["onesf"] = din("onesf", [128, 128], F32)
    t["onesb"] = din("onesb", [128, 8], BF16)

    t["out"] = nc.dram_tensor("out", [T, E], F32, kind="ExternalOutput").ap()

    t["ar_in"] = nc.dram_tensor("ar_in", [1, 2], F32).ap()
    t["ar_out"] = nc.dram_tensor("ar_out", [1, 2], F32).ap()
    t["ag1_in0"] = nc.dram_tensor("ag1_in0", [128, 128, 128], BF16).ap()
    t["ag1_in1"] = nc.dram_tensor("ag1_in1", [128, 128, 128], BF16).ap()
    t["ag1_out0"] = nc.dram_tensor("ag1_out0", [NCORE, 128, 128, 128], BF16).ap()
    t["ag1_out1"] = nc.dram_tensor("ag1_out1", [NCORE, 128, 128, 128], BF16).ap()
    t["ag2q_in"] = nc.dram_tensor("ag2q_in", [H, 128 * T], BF16).ap()
    t["ag2k_in"] = nc.dram_tensor("ag2k_in", [H, 128 * T], BF16).ap()
    t["ag2v_in"] = nc.dram_tensor("ag2v_in", [H, 128 * T], BF16).ap()
    t["ag2q_out"] = nc.dram_tensor("ag2q_out", [NCORE, H, 128 * T], BF16).ap()
    t["ag2k_out"] = nc.dram_tensor("ag2k_out", [NCORE, H, 128 * T], BF16).ap()
    t["ag2v_out"] = nc.dram_tensor("ag2v_out", [NCORE, H, 128 * T], BF16).ap()
    t["nmx_in"] = nc.dram_tensor("nmx_in", [2 * H], F32).ap()
    t["nmx_out"] = nc.dram_tensor("nmx_out", [NCORE, 2 * H], F32).ap()
    t["negc_d"] = nc.dram_tensor("negc_d", [H], F32).ap()
    t["kvl"] = nc.dram_tensor("kvl", [2, NHL, 128, S], BF16).ap()
    t["ag3a_in"] = nc.dram_tensor("ag3a_in", [128, S], BF16).ap()
    t["ag3b_in"] = nc.dram_tensor("ag3b_in", [128, S], BF16).ap()
    t["ag3a_out"] = nc.dram_tensor("ag3a_out", [NCORE, 128, S], BF16).ap()
    t["ag3b_out"] = nc.dram_tensor("ag3b_out", [NCORE, 128, S], BF16).ap()

    sb = lambda n, sh, dt: es.enter_context(nc.sbuf_tensor(n, sh, dt))
    t["x"] = sb("x", [128, NDC * T], F32)
    t["a32"] = sb("a32", [128, 16384], BF16)
    t["b32"] = sb("b32", [128, 16384], BF16)
    t["wstr"] = sb("wstr", [128, 8192], BF16)
    t["kbuf"] = sb("kbuf", [128, 2048], BF16)
    t["vbuf"] = sb("vbuf", [128, 2048], BF16)
    t["expb"] = sb("expb", [128, 1024], BF16)
    t["qst"] = sb("qst", [128, 2048], BF16)
    t["scr"] = sb("scr", [128, 8192], F32)
    t["cos"] = sb("cos", [128, T], F32)
    t["sin"] = sb("sin", [128, T], F32)
    t["mask"] = sb("mask", [128, 2048], BF16)
    t["idbf_t"] = sb("idbf_t", [128, 128], BF16)
    t["idf_t"] = sb("idf_t", [128, 128], F32)
    t["onesf_t"] = sb("onesf_t", [128, 128], F32)
    t["onesb_t"] = sb("onesb_t", [128, 8], BF16)
    t["b1_t"] = sb("b1_t", [128, 64], F32)
    t["b2h_t"] = sb("b2h_t", [128, 16], F32)
    t["small"] = sb("small", [128, 64], F32)

    ps = lambda n, sh, dt: es.enter_context(nc.psum_tensor(n, sh, dt))
    t["ps0"] = ps("ps0", [128, 1024], F32)
    t["ps1"] = ps("ps1", [128, 1024], F32)
    t["ps2"] = ps("ps2", [128, 1024], F32)
    t["psT"] = ps("psT", [128, 1024], BF16)
    t["psU"] = ps("psU", [128, 512], F32)
    return t


def phase_consts(kb, t):
    scr = t["scr"]
    for name, dst in (("cosf", t["cos"]), ("sinf", t["sin"]),
                      ("idbf", t["idbf_t"]), ("idf", t["idf_t"]),
                      ("onesf", t["onesf_t"]), ("onesb", t["onesb_t"])):
        kb.dma("sync", dst[:, :], t[name][:, :])
    for j in range(4):
        kb.dma("sync", t["mask"][:, j * 512:(j + 1) * 512], t["masks"][j])
    kb.dma("sync", scr[0:64, 0:128], t["b1"][:, :])
    kb.dma("sync", scr[0:16, 128:256], t["b2h"][:, :])
    kb.v("memset", t["small"][:, :], 0.0, inc=[("V", 1)])
    r = kb.relay("sync")
    kb.tr(t["psU"][:, 0:64], scr[0:64, 0:128], t["idf_t"][0:64, 0:64],
          wait=[("FS", r)])
    kb.tr(t["psU"][:, 64:80], scr[0:16, 128:256], t["idf_t"][0:16, 0:16],
          inc=[("T", 1)])
    kb.v("tensor_copy", t["b1_t"][:, :], t["psU"][:, 0:64],
         wait=[("T", kb.now("T"))])
    kb.v("tensor_copy", t["b2h_t"][:, :], t["psU"][:, 64:80], inc=[("V", 1)])
    kb.fence()


def phase_wprep(kb, t):
    scr, small = t["scr"], t["small"]
    ps2, psU = t["ps2"], t["psU"]
    onesf = t["onesf_t"]

    bguard = {SC_A: 0, SC_B: 0}
    for mat in range(2):
        nch = 8 if mat == 0 else 16
        wid = 2048 if mat == 0 else 1024
        src = t["w1s"] if mat == 0 else t["w2s"]
        acc = SM_ABS1 if mat == 0 else SM_ABS2
        for j in range(nch):
            buf = SC_A if j % 2 == 0 else SC_B
            kb.dma("sync", scr[:, buf:buf + wid],
                   src[j * 128:(j + 1) * 128, :],
                   wait=([("V", bguard[buf])] if bguard[buf] else []))
            r = kb.relay("sync")
            kb.v("reduce_sum", scr[0:128, SC_T0:SC_T0 + 1],
                 scr[:, buf:buf + wid], axis=AX.X, apply_absolute_value=True,
                 wait=[("FS", r)])
            kb.v("tensor_add", small[:, acc:acc + 1], small[:, acc:acc + 1],
                 scr[0:128, SC_T0:SC_T0 + 1], inc=[("V", 1)])
            bguard[buf] = kb.now("V")
    kb.mm(ps2[0:1, 0:2], onesf[:, 0:1], small[:, SM_ABS1:SM_ABS1 + 2],
          wait=[("V", kb.now("V"))], inc=[("T", 1)])
    kb.v("tensor_copy", scr[0:1, SC_T0:SC_T0 + 2], ps2[0:1, 0:2],
         wait=[("T", kb.now("T"))], inc=[("V", 1)])
    kb.dma("sync", t["ar_in"][:, :], scr[0:1, SC_T0:SC_T0 + 2],
           wait=[("V", kb.now("V"))])
    r = kb.relay("sync")
    kb.coll("AllReduce", ALU.add, [t["ar_in"]], [t["ar_out"]],
            wait=[("FS", r)], inc=[("C", 1)])
    kb.dma("sync", scr[0:1, SC_T0:SC_T0 + 2], t["ar_out"][:, :],
           wait=[("C", kb.now("C"))])
    r = kb.relay("sync")
    kb.v("tensor_scalar_mul", scr[0:1, SC_T0:SC_T0 + 1],
         scr[0:1, SC_T0:SC_T0 + 1], float(1.0 / (FF * E)), wait=[("FS", r)])
    kb.v("tensor_scalar_mul", scr[0:1, SC_T0 + 1:SC_T0 + 2],
         scr[0:1, SC_T0 + 1:SC_T0 + 2], float(1.0 / (E * FF)), inc=[("V", 1)])
    kb.mm(psU[:, 100:102], onesf[0:1, :], scr[0:1, SC_T0:SC_T0 + 2],
          wait=[("V", kb.now("V"))], inc=[("T", 1)])
    kb.v("tensor_copy", small[:, SM_M0:SM_M0 + 2], psU[:, 100:102],
         wait=[("T", kb.now("T"))], inc=[("V", 1)])
    Vm0 = kb.now("V")

    kb.v("memset", small[:, SM_ABS1:SM_ABS1 + 2], 0.0, wait=[("V", Vm0)],
         inc=[("V", 1)])
    bguard = {SC_A: kb.now("V"), SC_B: kb.now("V")}
    for mat in range(2):
        nch = 8 if mat == 0 else 16
        wid = 2048 if mat == 0 else 1024
        src = t["w1s"] if mat == 0 else t["w2s"]
        acc = SM_ABS1 if mat == 0 else SM_ABS2
        m0c = small[:, SM_M0 + mat:SM_M0 + mat + 1]
        for j in range(nch):
            buf = SC_A if j % 2 == 0 else SC_B
            kb.dma("sync", scr[:, buf:buf + wid],
                   src[j * 128:(j + 1) * 128, :], wait=[("V", bguard[buf])])
            r = kb.relay("sync")
            kb.v("tensor_scalar_mul", scr[:, SC_RSTD:SC_RSTD + wid],
                 scr[:, buf:buf + wid], -1.0, wait=[("FS", r)])
            kb.v("tensor_max", scr[:, SC_RSTD:SC_RSTD + wid],
                 scr[:, SC_RSTD:SC_RSTD + wid], scr[:, buf:buf + wid])
            kb.v("tensor_scalar_sub", scr[:, SC_RSTD:SC_RSTD + wid],
                 scr[:, SC_RSTD:SC_RSTD + wid], m0c)
            kb.v("reduce_sum", scr[0:128, SC_T1:SC_T1 + 1],
                 scr[:, SC_RSTD:SC_RSTD + wid], axis=AX.X)
            kb.v("tensor_add", small[:, acc:acc + 1], small[:, acc:acc + 1],
                 scr[0:128, SC_T1:SC_T1 + 1], inc=[("V", 1)])
            bguard[buf] = kb.now("V")
    kb.mm(ps2[0:1, 0:2], onesf[:, 0:1], small[:, SM_ABS1:SM_ABS1 + 2],
          wait=[("V", kb.now("V"))], inc=[("T", 1)])
    kb.v("tensor_copy", scr[0:1, SC_T0:SC_T0 + 2], ps2[0:1, 0:2],
         wait=[("T", kb.now("T"))], inc=[("V", 1)])
    kb.dma("sync", t["ar_in"][:, :], scr[0:1, SC_T0:SC_T0 + 2],
           wait=[("V", kb.now("V"))])
    r = kb.relay("sync")
    kb.coll("AllReduce", ALU.add, [t["ar_in"]], [t["ar_out"]],
            wait=[("FS", r)], inc=[("C", 1)])
    kb.dma("sync", scr[0:1, SC_T0:SC_T0 + 2], t["ar_out"][:, :],
           wait=[("C", kb.now("C"))])
    r = kb.relay("sync")
    kb.v("tensor_scalar_mul", scr[0:1, SC_T0:SC_T0 + 1],
         scr[0:1, SC_T0:SC_T0 + 1], float(1.0 / (FF * E)), wait=[("FS", r)])
    kb.v("tensor_scalar_mul", scr[0:1, SC_T0 + 1:SC_T0 + 2],
         scr[0:1, SC_T0 + 1:SC_T0 + 2], float(1.0 / (E * FF)))
    kb.v("tensor_add", scr[0:1, SC_T0:SC_T0 + 2], scr[0:1, SC_T0:SC_T0 + 2],
         small[0:1, SM_M0:SM_M0 + 2])
    kb.v("tensor_scalar_mul", scr[0:1, SC_T0:SC_T0 + 2],
         scr[0:1, SC_T0:SC_T0 + 2], float(1.0 + 3.51e-7))
    kb.v("tensor_scalar_max", scr[0:1, SC_T0:SC_T0 + 2],
         scr[0:1, SC_T0:SC_T0 + 2], 1e-8)
    kb.v("tensor_scalar_mul", scr[0:1, SC_T0:SC_T0 + 2],
         scr[0:1, SC_T0:SC_T0 + 2], 0.5, inc=[("V", 1)])
    kb.mm(psU[:, 100:102], onesf[0:1, :], scr[0:1, SC_T0:SC_T0 + 2],
          wait=[("V", kb.now("V"))], inc=[("T", 1)])
    kb.v("tensor_copy", small[:, SM_THR1:SM_THR1 + 2], psU[:, 100:102],
         wait=[("T", kb.now("T"))], inc=[("V", 1)])
    Vthr = kb.now("V")

    bguard = {SC_A: Vthr, SC_B: Vthr}
    for mat in range(2):
        nch = 8 if mat == 0 else 16
        wid = 2048 if mat == 0 else 1024
        src = t["w1s"] if mat == 0 else t["w2s"]
        dst = t["a32"] if mat == 0 else t["b32"]
        thc = SM_THR1 if mat == 0 else SM_THR2
        thr = small[:, thc:thc + 1]
        for j in range(nch):
            buf = SC_A if j % 2 == 0 else SC_B
            kb.dma("sync", scr[:, buf:buf + wid],
                   src[j * 128:(j + 1) * 128, :], wait=[("V", bguard[buf])])
            r = kb.relay("sync")
            kb.v("tensor_scalar", scr[:, SC_T0:SC_T0 + wid],
                 scr[:, buf:buf + wid], -1.0, thr,
                 op0=ALU.mult, op1=ALU.is_gt, wait=[("FS", r)])
            if mat == 0:
                kb.v("scalar_tensor_tensor", dst[:, j * wid:(j + 1) * wid],
                     scr[:, buf:buf + wid], thr, scr[:, SC_T0:SC_T0 + wid],
                     op0=ALU.is_gt, op1=ALU.subtract, inc=[("V", 1)])
            else:
                kb.v("scalar_tensor_tensor", scr[:, SC_T0:SC_T0 + wid],
                     scr[:, buf:buf + wid], thr, scr[:, SC_T0:SC_T0 + wid],
                     op0=ALU.is_gt, op1=ALU.subtract)
                kb.v("tensor_scalar_mul", dst[:, j * wid:(j + 1) * wid],
                     scr[:, SC_T0:SC_T0 + wid], 0.5, inc=[("V", 1)])
            bguard[buf] = kb.now("V")
    Vq = kb.now("V")

    slot_relay = {0: 0, 1: 0}
    for mat in range(2):
        for grp in range(16):
            g = mat * 16 + grp
            slot = g % 2
            for k in range(8):
                tile = grp * 8 + k
                if mat == 0:
                    fl, dc = tile // 16, tile % 16
                    src = t["a32"][:, fl * 2048 + dc * 128:fl * 2048 + dc * 128 + 128]
                else:
                    dc, fl = tile // 8, tile % 8
                    src = t["b32"][:, dc * 1024 + fl * 128:dc * 1024 + fl * 128 + 128]
                w = []
                if k == 0:
                    w = [("V", kb.now("V"))] if g > 0 else [("V", Vq)]
                kb.tr(t["psT"][:, k * 128:(k + 1) * 128], src, t["idbf_t"],
                      wait=w, inc=([("T", 1)] if k == 7 else []))
            kb.v("tensor_copy", t["qst"][:, slot * 1024:slot * 1024 + 1024],
                 t["psT"][:, :],
                 wait=[("T", kb.now("T"))] +
                      ([("FS", slot_relay[slot])] if slot_relay[slot] else []),
                 inc=[("V", 1)])
            agt = t["ag1_in0"] if mat == 0 else t["ag1_in1"]
            kb.dma("sync",
                   agt[grp * 8:(grp + 1) * 8].rearrange("a p f -> p a f"),
                   t["qst"][:, slot * 1024:slot * 1024 + 1024]
                   .rearrange("p (a f) -> p a f", f=128),
                   wait=[("V", kb.now("V"))])
            slot_relay[slot] = kb.relay("sync")
        if mat == 0:
            r = kb.relay("sync")
            kb.coll("AllGather", ALU.bypass, [t["ag1_in0"]], [t["ag1_out0"]],
                    wait=[("FS", r)], inc=[("C", 1)])
    r = kb.relay("sync")
    kb.coll("AllGather", ALU.bypass, [t["ag1_in1"]], [t["ag1_out1"]],
            wait=[("FS", r)], inc=[("C", 1)])
    t["C_w1"] = kb.now("C") - 1
    t["C_w2"] = kb.now("C")
    tgt = [("FS", kb.now("FS")), ("T", kb.now("T")), ("V", kb.now("V"))]
    for s, v in tgt:
        if v > 0:
            kb.eng.wait_ge(kb.sems[s], v)


def phase_xload(kb, t):
    scr, x, psU, idf = t["scr"], t["x"], t["psU"], t["idf_t"]
    vdone = {}
    for tt in range(8):
        buf = SC_A if tt % 2 == 0 else SC_B
        w = [("V", vdone[tt - 2])] if tt - 2 in vdone else []
        kb.dma("sync", scr[:, buf:buf + 2048],
               t["xs"][tt * 128:(tt + 1) * 128, :], wait=w)
        r = kb.relay("sync")
        for q in range(4):
            for k in range(4):
                dc = q * 4 + k
                kb.tr(psU[:, k * 128:(k + 1) * 128],
                      scr[:, buf + dc * 128:buf + dc * 128 + 128], idf,
                      wait=([("FS", r)] if q == 0 and k == 0 else []) +
                           ([("V", kb.now("V"))] if k == 0 else []),
                      inc=([("T", 1)] if k == 3 else []))
            out = x.rearrange("p (d u) -> p d u", u=T)[:, q * 4:(q + 1) * 4,
                                                       tt * 128:(tt + 1) * 128]
            kb.v("tensor_copy", out,
                 psU[:, 0:512].rearrange("p (d u) -> p d u", u=128),
                 wait=[("T", kb.now("T"))], inc=[("V", 1)])
        vdone[tt] = kb.now("V")
    kb.fence()


def phase_ln(kb, t):
    x, scr, a32 = t["x"], t["scr"], t["a32"]
    ps0, ps2, psU = t["ps0"], t["ps2"], t["psU"]
    onesb, onesf = t["onesb_t"], t["onesf_t"]
    loc = {("sx", 0): (ps2, 0), ("sx", 1): (ps2, 512),
           ("sq", 0): (ps0, 0), ("sq", 1): (ps0, 512)}
    for th in range(2):
        for dc in range(NDC):
            it = th * NDC + dc
            sl = x[:, dc * T + th * 512: dc * T + th * 512 + 512]
            cb = (dc % 2) * 512
            kb.v("tensor_copy", t["expb"][:, cb:cb + 512], sl,
                 wait=([("T", kb.now("T"))] if it >= 2 else []), inc=[("V", 1)])
            kb.act(t["qst"][:, cb:cb + 512], sl, AF.Square,
                   wait=([("T", kb.now("T"))] if it >= 2 else []), inc=[("S", 1)])
            pp, c0 = loc[("sx", th)]
            kb.mm(pp[0:1, c0:c0 + 512], onesb[:, 0:1], t["expb"][:, cb:cb + 512],
                  start=(dc == 0), stop=(dc == NDC - 1),
                  wait=[("V", kb.now("V"))])
            pp, c0 = loc[("sq", th)]
            kb.mm(pp[0:1, c0:c0 + 512], onesb[:, 0:1], t["qst"][:, cb:cb + 512],
                  start=(dc == 0), stop=(dc == NDC - 1),
                  wait=[("S", kb.now("S"))], inc=[("T", 1)])
    Tst = kb.now("T")
    for th in range(2):
        pp, c0 = loc[("sx", th)]
        kb.v("tensor_scalar_mul", scr[0:1, SC_T0 + th * 512:SC_T0 + th * 512 + 512],
             pp[0:1, c0:c0 + 512], 1.0 / E, wait=[("T", Tst)])
        pp, c0 = loc[("sq", th)]
        kb.v("tensor_scalar_mul", scr[0:1, SC_T1 + th * 512:SC_T1 + th * 512 + 512],
             pp[0:1, c0:c0 + 512], 1.0 / E)
    kb.v("tensor_mul", scr[0:1, SC_NMR:SC_NMR + 1024], scr[0:1, SC_T0:SC_T0 + 1024],
         scr[0:1, SC_T0:SC_T0 + 1024])
    kb.v("tensor_sub", scr[0:1, SC_T1:SC_T1 + 1024], scr[0:1, SC_T1:SC_T1 + 1024],
         scr[0:1, SC_NMR:SC_NMR + 1024])
    kb.v("tensor_scalar_add", scr[0:1, SC_T1:SC_T1 + 1024],
         scr[0:1, SC_T1:SC_T1 + 1024], 1e-5, inc=[("V", 1)])
    kb.act(scr[0:1, SC_T1:SC_T1 + 1024], scr[0:1, SC_T1:SC_T1 + 1024], AF.Ln,
           wait=[("V", kb.now("V"))], inc=[("S", 1)])
    kb.act(scr[0:1, SC_T1:SC_T1 + 1024], scr[0:1, SC_T1:SC_T1 + 1024], AF.Exp,
           scale=-0.5, inc=[("S", 1)])
    kb.v("tensor_mul", scr[0:1, SC_T0:SC_T0 + 1024], scr[0:1, SC_T0:SC_T0 + 1024],
         scr[0:1, SC_T1:SC_T1 + 1024], wait=[("S", kb.now("S"))])
    kb.v("tensor_scalar_mul", scr[0:1, SC_T0:SC_T0 + 1024],
         scr[0:1, SC_T0:SC_T0 + 1024], -1.0, inc=[("V", 1)])
    for th in range(2):
        kb.mm(psU[:, 0:512], onesf[0:1, :],
              scr[0:1, SC_T1 + th * 512:SC_T1 + th * 512 + 512],
              wait=[("V", kb.now("V"))], inc=[("T", 1)])
        kb.v("tensor_copy", scr[:, SC_RSTD + th * 512:SC_RSTD + th * 512 + 512],
             psU[:, 0:512], wait=[("T", kb.now("T"))], inc=[("V", 1)])
        kb.mm(psU[:, 0:512], onesf[0:1, :],
              scr[0:1, SC_T0 + th * 512:SC_T0 + th * 512 + 512],
              wait=[("V", kb.now("V"))], inc=[("T", 1)])
        kb.v("tensor_copy", scr[:, SC_NMR + th * 512:SC_NMR + th * 512 + 512],
             psU[:, 0:512], wait=[("T", kb.now("T"))], inc=[("V", 1)])
    Vbc = kb.now("V")
    for dc in range(NDC):
        for th in range(2):
            sl = x[:, dc * T + th * 512: dc * T + th * 512 + 512]
            kb.v("tensor_mul", scr[:, SC_T1:SC_T1 + 512], sl,
                 scr[:, SC_RSTD + th * 512:SC_RSTD + th * 512 + 512],
                 wait=([("V", Vbc)] if dc == 0 and th == 0 else []))
            kb.v("tensor_add", a32[:, dc * T + th * 512: dc * T + th * 512 + 512],
                 scr[:, SC_T1:SC_T1 + 512],
                 scr[:, SC_NMR + th * 512:SC_NMR + th * 512 + 512],
                 inc=[("V", 1)])
    kb.fence()


def phase_ffn(kb, t):
    a32, b32, x, wstr = t["a32"], t["b32"], t["x"], t["wstr"]
    ps0, ps1 = t["ps0"], t["ps1"]
    ag1w1, ag1w2 = t["ag1_out0"], t["ag1_out1"]
    prev = t.setdefault("_ffn_prev", {"T2": 0, "S30": 0, "S31": 0, "V2a": 0, "V2b": 0})
    for th in range(2):
        for fh in range(2):
            tdone, sdone, rr_ = {}, {}, {}
            Tb1 = kb.now("T")
            for fl in range(32):
                fc = fh * 32 + fl
                rk, fli = fc // 8, fc % 8
                ws = (fl % 4) * 2048
                w = [("T", Tb1 + fl - 3)] if fl >= 4 else \
                    [("T", prev["T2"]), ("C", t.get("C_w1", 0))]
                kb.dma("sync", wstr[:, ws:ws + 2048]
                       .rearrange("p (a f) -> p a f", f=128),
                       ag1w1[rk, fli * 16:(fli + 1) * 16]
                       .rearrange("a p f -> p a f"), wait=w,
                       qsem="L" + str(fl % 3))
                rr_[fl] = kb.relay("sync", qsem="L" + str(fl % 3))
            for fl in range(32):
                fc = fh * 32 + fl
                ws = (fl % 4) * 2048
                pslot = (fl % 2) * 512
                for dc in range(NDC):
                    w = []
                    if dc == 0:
                        w = [("FS", rr_[fl])]
                        if fl - 2 in sdone:
                            w.append(("S", sdone[fl - 2]))
                        else:
                            w.append(("S", prev["S30"] if fl == 0 else prev["S31"]))
                    kb.mm(ps0[:, pslot:pslot + 512],
                          wstr[:, ws + dc * 128:ws + dc * 128 + 128],
                          a32[:, dc * T + th * 512:dc * T + th * 512 + 512],
                          start=(dc == 0), stop=(dc == NDC - 1),
                          wait=w, inc=([("T", 1)] if dc == NDC - 1 else []))
                tdone[fl] = kb.now("T")
                assert tdone[fl] == Tb1 + fl + 1
                kb.act(b32[:, fl * 512:fl * 512 + 512], ps0[:, pslot:pslot + 512],
                       AF.Gelu, bias=t["b1_t"][:, fc:fc + 1],
                       wait=[("T", tdone[fl])], inc=[("S", 1)])
                sdone[fl] = kb.now("S")
            Sh = kb.now("S")
            tdone2, vdone2, r2_ = {}, {}, {}
            Tmm1 = kb.now("T")
            for dc in range(NDC):
                ws = (dc % 2) * 4096
                w = [("T", Tmm1 + dc - 1)] if dc >= 2 else \
                    [("T", Tmm1), ("C", t.get("C_w2", 0))]
                for rr in range(4):
                    rk = fh * 4 + rr
                    kb.dma("sync", wstr[:, ws + rr * 1024:ws + rr * 1024 + 1024]
                           .rearrange("p (a f) -> p a f", f=128),
                           ag1w2[rk, dc * 8:(dc + 1) * 8]
                           .rearrange("a p f -> p a f"),
                           wait=(w if rr == 0 else []),
                           qsem="L" + str(dc % 2))
                r2_[dc] = kb.relay("sync", qsem="L" + str(dc % 2))
            for dc in range(NDC):
                ws = (dc % 2) * 4096
                pslot = (dc % 2) * 512
                for fl in range(32):
                    w = []
                    if fl == 0:
                        w = [("FS", r2_[dc]), ("S", Sh)]
                        if dc - 2 in vdone2:
                            w.append(("V", vdone2[dc - 2]))
                    if dc < 2 and fl == 0:
                        w.append(("V", prev["V2a"] if dc == 0 else prev["V2b"]))
                    kb.mm(ps1[:, pslot:pslot + 512],
                          wstr[:, ws + fl * 128:ws + fl * 128 + 128],
                          b32[:, fl * 512:fl * 512 + 512],
                          start=(fl == 0), stop=(fl == 31),
                          wait=w, inc=([("T", 1)] if fl == 31 else []))
                tdone2[dc] = kb.now("T")
                assert tdone2[dc] == Tmm1 + dc + 1
                xs = x[:, dc * T + th * 512: dc * T + th * 512 + 512]
                if fh == 0:
                    kb.v("scalar_tensor_tensor", xs, ps1[:, pslot:pslot + 512],
                         t["b2h_t"][:, dc:dc + 1], xs, op0=ALU.add, op1=ALU.add,
                         wait=[("T", tdone2[dc])], inc=[("V", 1)])
                else:
                    kb.v("tensor_add", xs, xs, ps1[:, pslot:pslot + 512],
                         wait=[("T", tdone2[dc])], inc=[("V", 1)])
                vdone2[dc] = kb.now("V")
            prev["T2"] = tdone2[NDC - 1]
            prev["S30"], prev["S31"] = sdone[30], sdone[31]
            prev["V2a"], prev["V2b"] = vdone2[NDC - 2], vdone2[NDC - 1]


def phase_qkv(kb, t):
    a32, scr, qst, wstr = t["a32"], t["scr"], t["qst"], t["wstr"]
    ps0, ps1, ps2, psT = t["ps0"], t["ps1"], t["ps2"], t["psT"]
    wsrc = [t["wk"], t["wv"], t["wq"]]
    tdone, edone, urelay = {}, {}, {}
    Tq0 = kb.now("T")
    rg_ = {}
    for pi in range(3):
        for ec in range(H):
            u = pi * H + ec
            ws = (u % 2) * 4096
            w = [("T", Tq0 + 2 * (u - 2) + 1)] if u >= 2 else []
            for piece in range(2):
                kb.dma("gpsimd", wstr[:, ws + piece * 1024:ws + piece * 1024 + 1024]
                       .rearrange("p (a f) -> p a f", f=128),
                       wsrc[pi][piece * 1024:(piece + 1) * 1024,
                                ec * 128:(ec + 1) * 128]
                       .rearrange("(a p) f -> p a f", p=128),
                       wait=(w if piece == 0 else []),
                       qsem="G" + str(u % 2))
            rg_[u] = kb.relay("gpsimd", qsem="G" + str(u % 2))
    for pi in range(3):
        for ec in range(H):
            u = pi * H + ec
            ws = (u % 2) * 4096
            psel = ps0 if u % 2 == 0 else ps1
            r = rg_[u]
            for th in range(2):
                for dc in range(NDC):
                    w = []
                    if th == 0 and dc == 0:
                        w = [("FG", r)]
                        if u - 2 in edone:
                            w.append(("S", edone[u - 2][0]))
                            w.append(("V", edone[u - 2][1]))
                    kb.mm(psel[:, th * 512:th * 512 + 512],
                          wstr[:, ws + dc * 128:ws + dc * 128 + 128],
                          a32[:, dc * T + th * 512:dc * T + th * 512 + 512],
                          start=(dc == 0), stop=(dc == NDC - 1), wait=w,
                          inc=([("T", 1)] if th == 1 and dc == NDC - 1 else []))
            tdone[u] = kb.now("T")
            assert tdone[u] == Tq0 + 2 * u + 1, (u, tdone[u], Tq0)
            if u == H:
                kb.coll("AllGather", ALU.bypass, [t["ag2k_in"]], [t["ag2k_out"]],
                        wait=[("FS", urelay[H - 1])], inc=[("C", 1)])
                t["C_k"] = kb.now("C")
            if u == 2 * H:
                kb.coll("AllGather", ALU.bypass, [t["ag2v_in"]], [t["ag2v_out"]],
                        wait=[("FS", urelay[2 * H - 1])], inc=[("C", 1)])
                t["C_v"] = kb.now("C")
            qs = (u % 2) * 1024
            wsl = [("FS", urelay[u - 2])] if u - 2 in urelay else []
            wst = [("FS", urelay[u - 1])] if u - 1 in urelay else []
            if DBG_SKIP_QKV_TAIL:
                kb.scopy(qst[:, qs:qs + 1024], psel[:, :], wait=[("T", tdone[u])],
                         inc=[("S", 1)])
                edone[u] = (kb.now("S"), kb.now("V"))
            elif pi != 1:
                kb.scopy(scr[:, SC_T1:SC_T1 + 1024], psel[:, :],
                         wait=[("T", tdone[u])] +
                              ([("FS", urelay[u - 1])] if u - 1 in urelay else []),
                         inc=[("S", 1)])
                Sev = kb.now("S")
                kb.dma("sync", scr[0:64, SC_NMR:SC_NMR + 1024],
                       scr[64:128, SC_T1:SC_T1 + 1024], wait=[("S", Sev)])
                kb.dma("sync", scr[64:128, SC_NMR:SC_NMR + 1024],
                       scr[0:64, SC_T1:SC_T1 + 1024])
                rsw = kb.relay("sync")
                kb.v("tensor_mul", scr[:, SC_A:SC_A + 1024],
                     scr[:, SC_T1:SC_T1 + 1024], t["cos"][:, :],
                     wait=[("S", Sev)])
                kb.v("tensor_mul", scr[:, SC_B:SC_B + 1024],
                     scr[:, SC_NMR:SC_NMR + 1024], t["sin"][:, :],
                     wait=[("FS", rsw)])
                kb.v("tensor_add", qst[:, qs:qs + 1024], scr[:, SC_A:SC_A + 1024],
                     scr[:, SC_B:SC_B + 1024], inc=[("V", 1)], wait=wsl)
                kb.act(scr[:, SC_A:SC_A + 1024], qst[:, qs:qs + 1024], AF.Square,
                       wait=[("V", kb.now("V"))], inc=[("S", 1)])
                for th in range(2):
                    kb.mm(ps2[0:1, th * 512:th * 512 + 512], t["onesf_t"][:, 0:1],
                          scr[:, SC_A + th * 512:SC_A + th * 512 + 512],
                          wait=([("S", kb.now("S")), ("V", kb.now("V"))]
                                if th == 0 else []),
                          inc=([("T", 1)] if th == 1 else []))
                nsl = 8 + (0 if pi == 0 else H) + ec
                kb.v("reduce_max", t["small"][0:1, nsl:nsl + 1],
                     ps2[0:1, 0:1024], axis=AX.X,
                     wait=[("T", kb.now("T"))], inc=[("V", 1)])
                agt = t["ag2k_in"] if pi == 0 else t["ag2q_in"]
                kb.dma("sync",
                       agt[ec].rearrange("(p u) -> p u", p=128),
                       qst[:, qs:qs + 1024], wait=[("V", kb.now("V"))])
                urelay[u] = kb.relay("sync")
                edone[u] = (kb.now("S"), kb.now("V"))
            else:
                kb.scopy(qst[:, qs:qs + 1024], psel[:, :],
                         wait=[("T", tdone[u])] + wsl, inc=[("S", 1)])
                for tc in range(8):
                    kb.tr(psT[:, tc * 128:(tc + 1) * 128],
                          qst[:, qs + tc * 128:qs + tc * 128 + 128], t["idbf_t"],
                          wait=([("S", kb.now("S")), ("V", kb.now("V"))]
                                if tc == 0 else []),
                          inc=([("T", 1)] if tc == 7 else []))
                kb.v("tensor_copy", t["expb"][:, :], psT[:, :],
                     wait=[("T", kb.now("T"))] + wst, inc=[("V", 1)])
                kb.dma("sync",
                       t["ag2v_in"][ec].rearrange("(p a u) -> p a u", p=128, a=8),
                       t["expb"][:, :].rearrange("p (a u) -> p a u", a=8),
                       wait=[("V", kb.now("V"))])
                urelay[u] = kb.relay("sync")
                edone[u] = (kb.now("S"), kb.now("V"))
    r = urelay[3 * H - 1]
    kb.coll("AllGather", ALU.bypass, [t["ag2q_in"]], [t["ag2q_out"]],
            wait=[("FS", r)], inc=[("C", 1)])
    t["C_q"] = kb.now("C")
    kb.dma("sync", t["nmx_in"].rearrange("(h c) -> h c", h=1),
           t["small"][0:1, 8:8 + 2 * H], wait=[("V", kb.now("V"))])
    r = kb.relay("sync")
    kb.coll("AllGather", ALU.bypass, [t["nmx_in"]], [t["nmx_out"]],
            wait=[("FS", r), ("FG", kb.now("FG"))], inc=[("C", 1)])
    t["C_nmx"] = kb.now("C")
    t["qkv_S"] = kb.now("S")
    t["qkv_V"] = kb.now("V")


def phase_attention(kb, t):
    b32, scr, small = t["b32"], t["scr"], t["small"]
    ps0, ps1, ps2, psU = t["ps0"], t["ps1"], t["ps2"], t["psU"]
    onesf, onesb = t["onesf_t"], t["onesb_t"]

    kb.dma("gpsimd", scr[0:16, SC_T0:SC_T0 + 16],
           t["nmx_out"].rearrange("r (p h) -> h (r p)", p=2), slow=True,
           wait=[("C", t["C_nmx"])])
    r = kb.relay("gpsimd")
    kb.v("reduce_max", scr[0:16, SC_T1:SC_T1 + 1],
         scr[0:16, SC_T0:SC_T0 + 16:2], axis=AX.X, wait=[("FG", r)])
    kb.v("reduce_max", scr[0:16, SC_T1 + 1:SC_T1 + 2],
         scr[0:16, SC_T0 + 1:SC_T0 + 16:2], axis=AX.X)
    kb.v("tensor_mul", scr[0:16, SC_T1 + 2:SC_T1 + 3],
         scr[0:16, SC_T1:SC_T1 + 1], scr[0:16, SC_T1 + 1:SC_T1 + 2],
         inc=[("V", 1)])
    kb.act(scr[0:16, SC_T1 + 2:SC_T1 + 3], scr[0:16, SC_T1 + 2:SC_T1 + 3],
           AF.Sqrt, wait=[("V", kb.now("V"))], inc=[("S", 1)])
    kb.v("tensor_scalar_mul", scr[0:16, SC_T1 + 2:SC_T1 + 3],
         scr[0:16, SC_T1 + 2:SC_T1 + 3], -ISD * 1.001,
         wait=[("S", kb.now("S"))], inc=[("V", 1)])
    kb.dma("gpsimd", t["negc_d"].rearrange("(h c) -> h c", c=1),
           scr[0:16, SC_T1 + 2:SC_T1 + 3], wait=[("V", kb.now("V"))])
    r = kb.relay("gpsimd")
    for lh in range(NHL):
        kb.dma_dyn("gpsimd", scr[0:1, SC_T0 + 40 + lh:SC_T0 + 41 + lh],
                   lambda pid, lh=lh: t["negc_d"]
                   [bass.ds(kb.hsel(lh), 1)].rearrange("(h c) -> h c", c=1),
                   wait=([("FG", r)] if lh == 0 else []))
    r = kb.relay("gpsimd")
    for lh in range(NHL):
        kb.mm(psU[:, 200 + lh:201 + lh], onesf[0:1, :],
              scr[0:1, SC_T0 + 40 + lh:SC_T0 + 41 + lh],
              wait=([("FG", r)] if lh == 0 else []), inc=[("T", 1)])
        kb.v("tensor_copy", small[:, SM_NEGC0 + lh:SM_NEGC0 + lh + 1],
             psU[:, 200 + lh:201 + lh], wait=[("T", kb.now("T"))], inc=[("V", 1)])
    Vnegc = kb.now("V")

    piece_tcons = {}
    ps0_free = {0: t["qkv_S"], 1: t["qkv_S"]}
    expb_free = {0: 0, 1: 0}
    av_free = {0: 0, 1: 0}
    sum_free = {0: t["qkv_V"], 512: t["qkv_V"]}
    gseq = 0
    rbr = {}
    for lh in range(NHL):
        kb.dma_dyn("sync",
                   t["kvl"][0, lh].rearrange("p (r u) -> p r u", u=T),
                   lambda pid, lh=lh:
                   t["ag2k_out"][0:NCORE, bass.ds(kb.hsel(lh), 1)]
                   .rearrange("r h (p u) -> p (r h) u", p=128),
                   wait=([("C", t["C_k"])] if lh == 0 else []))
        kb.dma_dyn("sync",
                   t["kvl"][1, lh].rearrange("p (r g) -> p r g", g=T),
                   lambda pid, lh=lh:
                   t["ag2v_out"][0:NCORE, bass.ds(kb.hsel(lh), 1)]
                   .rearrange("r h (p a u) -> p (r h) (a u)", p=128, a=8),
                   wait=([("C", t["C_v"])] if lh == 0 else []))
        kb.dma_dyn("sync",
                   b32[:, lh * S:(lh + 1) * S].rearrange("p (r u) -> p r u", u=T),
                   lambda pid, lh=lh:
                   t["ag2q_out"][0:NCORE, bass.ds(kb.hsel(lh), 1)]
                   .rearrange("r h (p u) -> p (r h) u", p=128),
                   wait=([("C", t["C_q"])] if lh == 0 else []))
        rbr[lh] = kb.relay("sync")
        for i in range(16):
            nkb = 4 * (i + 1)
            tile = lh * 16 + i
            sumcol = (tile % 2) * 512
            avs = (tile % 2) * 512
            qsl = b32[:, lh * S + i * 512: lh * S + i * 512 + 512]
            npieces = (nkb + 7) // 8
            base_gseq = gseq
            Tt0 = kb.now("T")
            piece_r = {}
            for p_ in range(npieces):
                p = base_gseq + p_
                rk = p_
                ks = (p % 2) * 1024
                w = [("T", piece_tcons[p - 2])] if p - 2 in piece_tcons else []
                kb.dma("sync", t["kbuf"][:, ks:ks + 1024],
                       t["kvl"][0, lh][:, rk * T:(rk + 1) * T], wait=w,
                       qsem="L" + str(p % 3))
                kb.dma("sync", t["vbuf"][:, ks:ks + 1024],
                       t["kvl"][1, lh][:, rk * T:(rk + 1) * T],
                       qsem="L" + str(p % 3))
                piece_r[p] = kb.relay("sync", qsem="L" + str(p % 3))
                last_j = min(8 * p_ + 7, nkb - 1)
                piece_tcons[p] = (Tt0 + 2 * last_j + 3 if last_j < nkb - 1
                                  else Tt0 + 2 * nkb)
            gseq += npieces

            def unit_tail(k):
                es = (k % 2) * 512
                diag = k >= nkb - 4
                gate = [("V", vgate[k])] if diag else [("S", sgate[k])]
                w = list(gate)
                if k == 0 and sum_free[sumcol]:
                    w.append(("V", sum_free[sumcol]))
                kb.mm(ps2[0:1, sumcol:sumcol + 512], onesb[:, 0:1],
                      t["expb"][:, es:es + 512],
                      start=(k == 0), stop=(k == nkb - 1), wait=w)
                w = []
                if k == 0 and av_free[tile % 2]:
                    w.append(("V", av_free[tile % 2]))
                if k == 0 and tile == 0:
                    w.append(("S", t["qkv_S"]))
                koff = ((base_gseq + k // 8) % 2) * 1024 + (k % 8) * 128
                kb.mm(ps1[:, avs:avs + 512], t["vbuf"][:, koff:koff + 128],
                      t["expb"][:, es:es + 512],
                      start=(k == 0), stop=(k == nkb - 1),
                      wait=w, inc=[("T", 1)])
                expb_free[k % 2] = kb.now("T")

            sgate, vgate = {}, {}
            for k in range(nkb):
                p = base_gseq + k // 8
                ks = (p % 2) * 1024
                koff = ks + (k % 8) * 128
                ss = (k % 2) * 512
                es = (k % 2) * 512
                w = [("FS", piece_r[p])]
                if i == 0 and k == 0:
                    w.append(("FS", rbr[lh]))
                if ps0_free[k % 2]:
                    w.append(("S", ps0_free[k % 2]))
                kb.mm(ps0[:, ss:ss + 512], t["kbuf"][:, koff:koff + 128], qsl,
                      start=True, stop=True, wait=w, inc=[("T", 1)])
                tsc = kb.now("T")
                assert tsc == (Tt0 + 1 if k == 0 else Tt0 + 2 * k), \
                    (tile, k, tsc, Tt0)
                we = [("T", max(tsc, expb_free[k % 2]))]
                if lh == 0 and i == 0 and k == 0:
                    we.append(("V", Vnegc))
                kb.act(t["expb"][:, es:es + 512], ps0[:, ss:ss + 512], AF.Exp,
                       bias=small[:, SM_NEGC0 + lh:SM_NEGC0 + lh + 1], scale=ISD,
                       wait=we, inc=[("S", 1)])
                ps0_free[k % 2] = kb.now("S")
                sgate[k] = kb.now("S")
                if k >= nkb - 4:
                    j = k - (nkb - 4)
                    kb.v("tensor_mul", t["expb"][:, es:es + 512],
                         t["expb"][:, es:es + 512],
                         t["mask"][:, j * 512:(j + 1) * 512],
                         wait=[("S", sgate[k])], inc=[("V", 1)])
                    vgate[k] = kb.now("V")
                if k >= 1:
                    unit_tail(k - 1)
            unit_tail(nkb - 1)
            Ttile = kb.now("T")
            kb.v("tensor_copy", scr[0:1, SC_T0:SC_T0 + 512],
                 ps2[0:1, sumcol:sumcol + 512], wait=[("T", Ttile)])
            kb.v("reciprocal", scr[0:1, SC_T0:SC_T0 + 512],
                 scr[0:1, SC_T0:SC_T0 + 512], inc=[("V", 1)])
            sum_free[sumcol] = kb.now("V")
            kb.mm(psU[:, 0:512], onesf[0:1, :], scr[0:1, SC_T0:SC_T0 + 512],
                  wait=[("V", kb.now("V"))], inc=[("T", 1)])
            kb.v("tensor_copy", scr[:, SC_NMR:SC_NMR + 512], psU[:, 0:512],
                 wait=[("T", kb.now("T"))])
            kb.v("tensor_mul", qsl, ps1[:, avs:avs + 512],
                 scr[:, SC_NMR:SC_NMR + 512], inc=[("V", 1)])
            av_free[tile % 2] = kb.now("V")
        agt = t["ag3a_in"] if lh == 0 else t["ag3b_in"]
        ago = t["ag3a_out"] if lh == 0 else t["ag3b_out"]
        kb.dma("gpsimd", agt[:, :], b32[:, lh * S:(lh + 1) * S],
               wait=[("V", kb.now("V"))], qsem="G" + str(lh))
        rr = kb.relay("gpsimd", qsem="G" + str(lh))
        kb.coll("AllGather", ALU.bypass, [agt], [ago],
                wait=[("FG", rr)], inc=[("C", 1)])
        t["C_ag3" + str(lh)] = kb.now("C")
    a32v = t["a32"][:, :].rearrange("p (g u) -> p g u", u=T)
    kb.dma_dyn("sync", a32v[:, 0:16:2, :],
               lambda pid: t["ag3a_out"][0:NCORE, :, bass.ds(kb.town(), T)]
               .rearrange("r p u -> p r u"),
               wait=[("C", t["C_ag30"])])
    rA = kb.relay("sync")
    kb.dma_dyn("sync", a32v[:, 1:16:2, :],
               lambda pid: t["ag3b_out"][0:NCORE, :, bass.ds(kb.town(), T)]
               .rearrange("r p u -> p r u"),
               wait=[("C", t["C_ag31"])])
    rB = kb.relay("sync")

    tdone, vdone = {}, {}
    Vflash = kb.now("V")
    for dc in range(NDC):
        ws = (dc % 2) * 4096
        w = [("T", tdone[dc - 2])] if dc - 2 in tdone else []
        for piece in range(2):
            kb.dma("gpsimd", t["wstr"][:, ws + piece * 1024:ws + piece * 1024 + 1024]
                   .rearrange("p (a f) -> p a f", f=128),
                   t["wo"][piece * 1024:(piece + 1) * 1024, dc * 128:(dc + 1) * 128]
                   .rearrange("(a p) f -> p a f", p=128),
                   wait=(w if piece == 0 else []))
        r = kb.relay("gpsimd")
        for th in range(2):
            pslot = ((dc * 2 + th) % 2) * 512
            ecs = list(range(0, H, 2)) + list(range(1, H, 2))
            for n_, ec in enumerate(ecs):
                w = []
                if n_ == 0:
                    w = [("FG", r), ("FS", rA), ("V", Vflash)]
                    k = dc * 2 + th - 2
                    if k in vdone:
                        w.append(("V", vdone[k]))
                if n_ == H // 2:
                    w = [("FS", rB)]
                kb.mm(t["ps1"][:, pslot:pslot + 512],
                      t["wstr"][:, ws + ec * 128:ws + ec * 128 + 128],
                      t["a32"][:, ec * T + th * 512:ec * T + th * 512 + 512],
                      start=(n_ == 0), stop=(n_ == H - 1),
                      wait=w, inc=([("T", 1)] if n_ == H - 1 else []))
            xs = t["x"][:, dc * T + th * 512: dc * T + th * 512 + 512]
            kb.v("tensor_add", xs, xs, t["ps1"][:, pslot:pslot + 512],
                 wait=[("T", kb.now("T"))], inc=[("V", 1)])
            vdone[dc * 2 + th] = kb.now("V")
        tdone[dc] = kb.now("T")
    kb.fence()


def phase_store(kb, t, src, dtype):
    scr = t["scr"]
    ident = t["idf_t"] if dtype == F32 else t["idbf_t"]
    psx = t["psU"] if dtype == F32 else t["psT"]
    rdone = {}
    for tt in range(8):
        buf = SC_A if tt % 2 == 0 else SC_B
        for q in range(4):
            for k in range(4):
                dc = q * 4 + k
                sl = src[:, dc * T + tt * 128: dc * T + tt * 128 + 128]
                w = []
                if k == 0:
                    w = [("V", kb.now("V"))]
                    if tt - 2 in rdone and q == 0:
                        w.append(("FS", rdone[tt - 2]))
                kb.tr(psx[:, k * 128:(k + 1) * 128], sl, ident,
                      wait=w, inc=([("T", 1)] if k == 3 else []))
            kb.v("tensor_copy", scr[:, buf + q * 512:buf + q * 512 + 512],
                 psx[:, 0:512],
                 wait=[("T", kb.now("T"))] +
                      ([("FS", rdone[tt - 2])] if tt - 2 in rdone else []),
                 inc=[("V", 1)])
        kb.dma("sync", t["out"][tt * 128:(tt + 1) * 128, :],
               scr[:, buf:buf + 2048], wait=[("V", kb.now("V"))])
        rdone[tt] = kb.relay("sync")
    kb.fence()


def emit_stream(kb, t):
    stage = DEBUG_STAGE
    t.pop("_ffn_prev", None)
    phase_consts(kb, t)
    if stage >= 2:
        phase_wprep(kb, t)
    phase_xload(kb, t)
    if stage == 0:
        phase_store(kb, t, t["x"], F32)
        return
    phase_ln(kb, t)
    if stage == 1:
        phase_store(kb, t, t["a32"], BF16)
        return
    phase_ffn(kb, t)
    if stage == 2:
        phase_store(kb, t, t["x"], F32)
        return
    phase_ln(kb, t)
    phase_qkv(kb, t)
    if stage == 3:
        phase_store(kb, t, t["x"], F32)
        return
    phase_attention(kb, t)
    if stage == 4:
        phase_store(kb, t, t["x"], F32)
        return
    phase_ln(kb, t)
    phase_ffn(kb, t)
    phase_store(kb, t, t["x"], F32)


def build_nc():
    nc = bass.Bass("TRN2", target_bir_lowering=False, debug=False)
    es = ExitStack()
    t = declare(nc, es)
    sems = {}
    for name in SEMS:
        sems[name] = es.enter_context(nc.semaphore("sem" + name))
    block = es.enter_context(nc.Block())
    counts = {}

    def make(eng_name):
        def run(eng):
            kbx = KB(nc, eng_name, eng, sems, t)
            emit_stream(kbx, t)
            counts[eng_name] = dict(kbx.c)
        return run

    for eng_name in ENGS:
        getattr(block, eng_name)(make(eng_name))
    vals = list(counts.values())
    for v in vals[1:]:
        assert v == vals[0], f"sem count mismatch: {counts}"
    es.close()
    return nc


def host_inputs(full):
    x = np.asarray(full["L0_attn_x"]).reshape(S, E)
    w1 = np.asarray(full["L0_ffn_w1"])
    w2 = np.asarray(full["L0_ffn_w2"])
    b1 = np.ascontiguousarray(np.asarray(full["L0_ffn_b1"]).reshape(64, 128))
    b2h = np.ascontiguousarray(
        (np.asarray(full["L0_ffn_b2"]) * 0.5).astype(np.float32).reshape(16, 128))

    pos = np.arange(S, dtype=np.float64)
    inv = 1.0 / (10000.0 ** (np.arange(0, D, 2, dtype=np.float64) / D))
    ang = pos[:, None] * inv[None, :]
    cos = np.cos(ang).T
    sin = np.sin(ang).T
    cosf = np.concatenate([cos, cos], 0).astype(np.float32)
    sinf = np.concatenate([-sin, sin], 0).astype(np.float32)

    kk = np.arange(128)[:, None]
    qq = np.arange(512)[None, :]
    masks = np.stack([(qq >= kk + 128 * j) for j in range(4)]) \
        .astype(ml_dtypes.bfloat16)

    shared = {
        "wq": np.asarray(full["L0_attn_wq"]), "wk": np.asarray(full["L0_attn_wk"]),
        "wv": np.asarray(full["L0_attn_wv"]), "wo": np.asarray(full["L0_attn_wo"]),
        "b1": b1, "b2h": b2h, "masks": np.asarray(masks),
        "idbf": np.asarray(np.eye(128, dtype=ml_dtypes.bfloat16)),
        "idf": np.eye(128, dtype=np.float32),
        "onesf": np.ones((128, 128), np.float32),
        "onesb": np.asarray(np.ones((128, 8), ml_dtypes.bfloat16)),
    }
    maps = []
    for c in range(NCORE):
        m = dict(shared)
        m["xs"] = np.ascontiguousarray(x[c * T:(c + 1) * T])
        m["w1s"] = np.ascontiguousarray(w1[c * FSL:(c + 1) * FSL])
        m["w2s"] = np.ascontiguousarray(w2[:, c * FSL:(c + 1) * FSL])
        m["cosf"] = np.ascontiguousarray(cosf[:, c * T:(c + 1) * T])
        m["sinf"] = np.ascontiguousarray(sinf[:, c * T:(c + 1) * T])
        maps.append(m)
    return maps


def kernel(**inputs):
    maps = host_inputs(inputs)
    nc = build_nc()
    res = run_bass_kernel_spmd(nc, maps, core_ids=list(range(NCORE)))
    out = np.concatenate([res.results[c]["out"] for c in range(NCORE)], axis=0)
    return np.ascontiguousarray(out.reshape(1, S, E).astype(np.float32))


# version 11
# speedup vs baseline: 1.3970x; 1.0264x over previous
import numpy as np
import ml_dtypes
import concourse.bass as bass
import concourse.mybir as mybir
from concourse.bass_utils import run_bass_kernel_spmd
from contextlib import ExitStack

F32 = mybir.dt.float32
BF16 = mybir.dt.bfloat16
AF = mybir.ActivationFunctionType
ALU = mybir.AluOpType
AX = mybir.AxisListType

NCORE = 8
S, E, H, D, FF = 8192, 2048, 16, 128, 8192
T = S // NCORE
NDC = E // 128
FSL = FF // NCORE
NHL = 2
ISD = float(1.0 / np.sqrt(128.0))

DEBUG_STAGE = 99
DBG_SKIP_AG2 = False
DBG_SKIP_QKV_TAIL = False

ENGS = ("tensor", "vector", "scalar", "sync", "gpsimd")
SEMS = ("L", "L0", "L1", "L2", "G", "G0", "G1", "T", "V", "S", "C", "FS", "FG")

SC_A = 0
SC_B = 2048
SC_RSTD = 4096
SC_NMR = 5120
SC_T0 = 6144
SC_T1 = 7168
SM_THR1, SM_THR2, SM_NEGC0, SM_NEGC1, SM_ABS1, SM_ABS2 = 0, 1, 2, 3, 4, 5
SM_M0 = 6
SM_RES = 40


class KB:
    def __init__(self, nc, which, eng, sems, t):
        self.nc, self.which, self.eng, self.sems, self.t = nc, which, eng, sems, t
        self.c = {k: 0 for k in SEMS}
        self._pidv = None

    def pid(self):
        if self._pidv is None:
            self._pidv = self.eng.partition_id()
        return self._pidv

    def hsel(self, lh):
        if not hasattr(self, "_hsel"):
            p = self.pid()
            self._hsel = [p * NHL + j for j in range(NHL)]
        return self._hsel[lh]

    def town(self):
        if not hasattr(self, "_town"):
            self._town = self.pid() * T
        return self._town

    def on(self, name):
        return self.which == name

    def now(self, s):
        return self.c[s]

    def _go(self, engname, fn, inc, wait):
        if self.which == engname:
            for s, v in wait:
                if v > 0:
                    self.eng.wait_ge(self.sems[s], v)
            ins = fn(self.eng)
            for s, n in inc:
                ins = ins.then_inc(self.sems[s], n)
        for s, n in inc:
            self.c[s] += n

    def relay(self, q, qsem=None, extra_wait=()):
        if qsem is None:
            qsem = "L" if q == "sync" else "G"
        rsem = "FS" if q == "sync" else "FG"
        if self.which == q:
            for s, v in extra_wait:
                if v > 0:
                    self.eng.wait_ge(self.sems[s], v)
            if self.c[qsem] > 0:
                self.eng.wait_ge(self.sems[qsem], self.c[qsem])
            self.eng.sem_inc(self.sems[rsem], 1)
        self.c[rsem] += 1
        return self.c[rsem]

    def fence(self):
        if self.which == "sync":
            for s in ("L", "L0", "L1", "L2"):
                if self.c[s] > 0:
                    self.eng.wait_ge(self.sems[s], self.c[s])
            self.eng.sem_inc(self.sems["FS"], 1)
        if self.which == "gpsimd":
            for s in ("G", "G0", "G1"):
                if self.c[s] > 0:
                    self.eng.wait_ge(self.sems[s], self.c[s])
            self.eng.sem_inc(self.sems["FG"], 1)
        self.c["FS"] += 1
        self.c["FG"] += 1
        tgt = [(s, self.c[s]) for s in ("FS", "FG", "T", "V", "S", "C")]
        for s, v in tgt:
            if v > 0:
                self.eng.wait_ge(self.sems[s], v)

    def mm(self, out, lhsT, rhs, start=True, stop=True, inc=(), wait=()):
        self._go("tensor",
                 lambda e: e.matmul(out, lhsT, rhs, start=start, stop=stop),
                 inc, wait)

    def tr(self, out, in_, ident, inc=(), wait=()):
        if not isinstance(ident, bass.AP):
            ident = ident[:, :]
        self._go("tensor", lambda e: e.transpose(out, in_, ident), inc, wait)

    def act(self, out, in_, func, bias=0.0, scale=1.0, inc=(), wait=()):
        w = [("S", self.c["S"])] + list(wait)
        ii = [x for x in inc if x[0] != "S"] + [("S", 1)]
        self._go("scalar",
                 lambda e: e.activation(out=out, in_=in_, func=func,
                                        bias=bias, scale=scale), ii, w)

    def scopy(self, out, in_, inc=(), wait=()):
        w = [("S", self.c["S"])] + list(wait)
        ii = [x for x in inc if x[0] != "S"] + [("S", 1)]
        self._go("scalar", lambda e: e.copy(out=out, in_=in_), ii, w)

    def v(self, method, *a, inc=(), wait=(), **kw):
        w = [("V", self.c["V"])] + list(wait)
        ii = [x for x in inc if x[0] != "V"] + [("V", 1)]
        self._go("vector", lambda e: getattr(e, method)(*a, **kw), ii, w)

    def dma(self, q, out, in_, inc=None, wait=(), slow=False, qsem=None):
        if inc is None:
            inc = [((qsem or ("L" if q == "sync" else "G")), 16)]
        self._go(q, lambda e: e.dma_start(out=out, in_=in_,
                                          allow_slow_non_contiguous=slow),
                 inc, wait)

    def dma_dyn(self, q, out, in_fn, inc=None, wait=(), qsem=None):
        if inc is None:
            inc = [((qsem or ("L" if q == "sync" else "G")), 16)]
        self._go(q, lambda e: e.dma_start(out=out, in_=in_fn(self.pid())),
                 inc, wait)

    def coll(self, kind, op, ins, outs, inc=(), wait=()):
        self._go("gpsimd",
                 lambda e: e.collective_compute(
                     kind, op, replica_groups=[list(range(NCORE))],
                     ins=[a.opt() for a in ins], outs=[a.opt() for a in outs]),
                 inc, wait)


def declare(nc, es):
    t = {}
    din = lambda n, sh, dt: nc.dram_tensor(n, sh, dt, kind="ExternalInput").ap()
    t["xs"] = din("xs", [T, E], F32)
    t["w1s"] = din("w1s", [FSL, E], F32)
    t["w2s"] = din("w2s", [E, FSL], F32)
    t["wq"] = din("wq", [E, E], F32)
    t["wk"] = din("wk", [E, E], F32)
    t["wv"] = din("wv", [E, E], F32)
    t["wo"] = din("wo", [E, E], F32)
    t["b1"] = din("b1", [64, 128], F32)
    t["b2h"] = din("b2h", [16, 128], F32)
    t["cosf"] = din("cosf", [128, T], F32)
    t["sinf"] = din("sinf", [128, T], F32)
    t["masks"] = din("masks", [4, 128, 512], BF16)
    t["idbf"] = din("idbf", [128, 128], BF16)
    t["idf"] = din("idf", [128, 128], F32)
    t["onesf"] = din("onesf", [128, 128], F32)
    t["onesb"] = din("onesb", [128, 8], BF16)

    t["out"] = nc.dram_tensor("out", [T, E], F32, kind="ExternalOutput").ap()

    t["ar_in"] = nc.dram_tensor("ar_in", [1, 2], F32).ap()
    t["ar_out"] = nc.dram_tensor("ar_out", [1, 2], F32).ap()
    t["ag1_in0"] = nc.dram_tensor("ag1_in0", [128, 128, 128], BF16).ap()
    t["ag1_in1"] = nc.dram_tensor("ag1_in1", [128, 128, 128], BF16).ap()
    t["ag1_out0"] = nc.dram_tensor("ag1_out0", [NCORE, 128, 128, 128], BF16).ap()
    t["ag1_out1"] = nc.dram_tensor("ag1_out1", [NCORE, 128, 128, 128], BF16).ap()
    t["ag2q_in"] = nc.dram_tensor("ag2q_in", [H, 128 * T], BF16).ap()
    t["ag2k_in"] = nc.dram_tensor("ag2k_in", [H, 128 * T], BF16).ap()
    t["ag2v_in"] = nc.dram_tensor("ag2v_in", [H, 128 * T], BF16).ap()
    t["ag2q_out"] = nc.dram_tensor("ag2q_out", [NCORE, H, 128 * T], BF16).ap()
    t["ag2k_out"] = nc.dram_tensor("ag2k_out", [NCORE, H, 128 * T], BF16).ap()
    t["ag2v_out"] = nc.dram_tensor("ag2v_out", [NCORE, H, 128 * T], BF16).ap()
    t["nmx_in"] = nc.dram_tensor("nmx_in", [2 * H], F32).ap()
    t["nmx_out"] = nc.dram_tensor("nmx_out", [NCORE, 2 * H], F32).ap()
    t["negc_d"] = nc.dram_tensor("negc_d", [H], F32).ap()
    t["kvl"] = nc.dram_tensor("kvl", [2, NHL, 128, S], BF16).ap()
    t["ag3a_in"] = nc.dram_tensor("ag3a_in", [128, S], BF16).ap()
    t["ag3b_in"] = nc.dram_tensor("ag3b_in", [128, S], BF16).ap()
    t["ag3a_out"] = nc.dram_tensor("ag3a_out", [NCORE, 128, S], BF16).ap()
    t["ag3b_out"] = nc.dram_tensor("ag3b_out", [NCORE, 128, S], BF16).ap()

    sb = lambda n, sh, dt: es.enter_context(nc.sbuf_tensor(n, sh, dt))
    t["x"] = sb("x", [128, NDC * T], F32)
    t["a32"] = sb("a32", [128, 16384], BF16)
    t["b32"] = sb("b32", [128, 16384], BF16)
    t["wstr"] = sb("wstr", [128, 8192], BF16)
    t["kbuf"] = sb("kbuf", [128, 2048], BF16)
    t["vbuf"] = sb("vbuf", [128, 2048], BF16)
    t["expb"] = sb("expb", [128, 1024], BF16)
    t["qst"] = sb("qst", [128, 2048], BF16)
    t["scr"] = sb("scr", [128, 8192], F32)
    t["cos"] = sb("cos", [128, T], F32)
    t["sin"] = sb("sin", [128, T], F32)
    t["mask"] = sb("mask", [128, 2048], BF16)
    t["idbf_t"] = sb("idbf_t", [128, 128], BF16)
    t["idf_t"] = sb("idf_t", [128, 128], F32)
    t["onesf_t"] = sb("onesf_t", [128, 128], F32)
    t["onesb_t"] = sb("onesb_t", [128, 8], BF16)
    t["b1_t"] = sb("b1_t", [128, 64], F32)
    t["b2h_t"] = sb("b2h_t", [128, 16], F32)
    t["small"] = sb("small", [128, 64], F32)

    ps = lambda n, sh, dt: es.enter_context(nc.psum_tensor(n, sh, dt))
    t["ps0"] = ps("ps0", [128, 1024], F32)
    t["ps1"] = ps("ps1", [128, 1024], F32)
    t["ps2"] = ps("ps2", [128, 1024], F32)
    t["psT"] = ps("psT", [128, 1024], BF16)
    t["psU"] = ps("psU", [128, 512], F32)
    return t


def phase_consts(kb, t):
    scr = t["scr"]
    for name, dst in (("cosf", t["cos"]), ("sinf", t["sin"]),
                      ("idbf", t["idbf_t"]), ("idf", t["idf_t"]),
                      ("onesf", t["onesf_t"]), ("onesb", t["onesb_t"])):
        kb.dma("sync", dst[:, :], t[name][:, :])
    for j in range(4):
        kb.dma("sync", t["mask"][:, j * 512:(j + 1) * 512], t["masks"][j])
    kb.dma("sync", scr[0:64, 0:128], t["b1"][:, :])
    kb.dma("sync", scr[0:16, 128:256], t["b2h"][:, :])
    kb.v("memset", t["small"][:, :], 0.0, inc=[("V", 1)])
    r = kb.relay("sync")
    kb.tr(t["psU"][:, 0:64], scr[0:64, 0:128], t["idf_t"][0:64, 0:64],
          wait=[("FS", r)])
    kb.tr(t["psU"][:, 64:80], scr[0:16, 128:256], t["idf_t"][0:16, 0:16],
          inc=[("T", 1)])
    kb.v("tensor_copy", t["b1_t"][:, :], t["psU"][:, 0:64],
         wait=[("T", kb.now("T"))])
    kb.v("tensor_copy", t["b2h_t"][:, :], t["psU"][:, 64:80], inc=[("V", 1)])
    kb.fence()


def phase_wprep(kb, t):
    scr, small = t["scr"], t["small"]
    ps2, psU = t["ps2"], t["psU"]
    onesf = t["onesf_t"]

    bguard = {SC_A: 0, SC_B: 0}
    for mat in range(2):
        nch = 8 if mat == 0 else 16
        wid = 2048 if mat == 0 else 1024
        src = t["w1s"] if mat == 0 else t["w2s"]
        acc = SM_ABS1 if mat == 0 else SM_ABS2
        for j in range(nch):
            buf = SC_A if j % 2 == 0 else SC_B
            kb.dma("sync", scr[:, buf:buf + wid],
                   src[j * 128:(j + 1) * 128, :],
                   wait=([("V", bguard[buf])] if bguard[buf] else []))
            r = kb.relay("sync")
            kb.v("reduce_sum", scr[0:128, SC_T0:SC_T0 + 1],
                 scr[:, buf:buf + wid], axis=AX.X, apply_absolute_value=True,
                 wait=[("FS", r)])
            kb.v("tensor_add", small[:, acc:acc + 1], small[:, acc:acc + 1],
                 scr[0:128, SC_T0:SC_T0 + 1], inc=[("V", 1)])
            bguard[buf] = kb.now("V")
    kb.mm(ps2[0:1, 0:2], onesf[:, 0:1], small[:, SM_ABS1:SM_ABS1 + 2],
          wait=[("V", kb.now("V"))], inc=[("T", 1)])
    kb.v("tensor_copy", scr[0:1, SC_T0:SC_T0 + 2], ps2[0:1, 0:2],
         wait=[("T", kb.now("T"))], inc=[("V", 1)])
    kb.dma("sync", t["ar_in"][:, :], scr[0:1, SC_T0:SC_T0 + 2],
           wait=[("V", kb.now("V"))])
    r = kb.relay("sync")
    kb.coll("AllReduce", ALU.add, [t["ar_in"]], [t["ar_out"]],
            wait=[("FS", r)], inc=[("C", 1)])
    kb.dma("sync", scr[0:1, SC_T0:SC_T0 + 2], t["ar_out"][:, :],
           wait=[("C", kb.now("C"))])
    r = kb.relay("sync")
    kb.v("tensor_scalar_mul", scr[0:1, SC_T0:SC_T0 + 1],
         scr[0:1, SC_T0:SC_T0 + 1], float(1.0 / (FF * E)), wait=[("FS", r)])
    kb.v("tensor_scalar_mul", scr[0:1, SC_T0 + 1:SC_T0 + 2],
         scr[0:1, SC_T0 + 1:SC_T0 + 2], float(1.0 / (E * FF)), inc=[("V", 1)])
    kb.mm(psU[:, 100:102], onesf[0:1, :], scr[0:1, SC_T0:SC_T0 + 2],
          wait=[("V", kb.now("V"))], inc=[("T", 1)])
    kb.v("tensor_copy", small[:, SM_M0:SM_M0 + 2], psU[:, 100:102],
         wait=[("T", kb.now("T"))], inc=[("V", 1)])
    Vm0 = kb.now("V")

    kb.v("memset", small[:, SM_ABS1:SM_ABS1 + 2], 0.0, wait=[("V", Vm0)],
         inc=[("V", 1)])
    bguard = {SC_A: kb.now("V"), SC_B: kb.now("V")}
    for mat in range(2):
        nch = 8 if mat == 0 else 16
        wid = 2048 if mat == 0 else 1024
        src = t["w1s"] if mat == 0 else t["w2s"]
        acc = SM_ABS1 if mat == 0 else SM_ABS2
        m0c = small[:, SM_M0 + mat:SM_M0 + mat + 1]
        for j in range(nch):
            buf = SC_A if j % 2 == 0 else SC_B
            kb.dma("sync", scr[:, buf:buf + wid],
                   src[j * 128:(j + 1) * 128, :], wait=[("V", bguard[buf])])
            r = kb.relay("sync")
            kb.v("tensor_scalar_mul", scr[:, SC_RSTD:SC_RSTD + wid],
                 scr[:, buf:buf + wid], -1.0, wait=[("FS", r)])
            kb.v("tensor_max", scr[:, SC_RSTD:SC_RSTD + wid],
                 scr[:, SC_RSTD:SC_RSTD + wid], scr[:, buf:buf + wid])
            kb.v("tensor_scalar_sub", scr[:, SC_RSTD:SC_RSTD + wid],
                 scr[:, SC_RSTD:SC_RSTD + wid], m0c)
            kb.v("reduce_sum", scr[0:128, SC_T1:SC_T1 + 1],
                 scr[:, SC_RSTD:SC_RSTD + wid], axis=AX.X)
            kb.v("tensor_add", small[:, acc:acc + 1], small[:, acc:acc + 1],
                 scr[0:128, SC_T1:SC_T1 + 1], inc=[("V", 1)])
            bguard[buf] = kb.now("V")
    kb.mm(ps2[0:1, 0:2], onesf[:, 0:1], small[:, SM_ABS1:SM_ABS1 + 2],
          wait=[("V", kb.now("V"))], inc=[("T", 1)])
    kb.v("tensor_copy", scr[0:1, SC_T0:SC_T0 + 2], ps2[0:1, 0:2],
         wait=[("T", kb.now("T"))], inc=[("V", 1)])
    kb.dma("sync", t["ar_in"][:, :], scr[0:1, SC_T0:SC_T0 + 2],
           wait=[("V", kb.now("V"))])
    r = kb.relay("sync")
    kb.coll("AllReduce", ALU.add, [t["ar_in"]], [t["ar_out"]],
            wait=[("FS", r)], inc=[("C", 1)])
    kb.dma("sync", scr[0:1, SC_T0:SC_T0 + 2], t["ar_out"][:, :],
           wait=[("C", kb.now("C"))])
    r = kb.relay("sync")
    kb.v("tensor_scalar_mul", scr[0:1, SC_T0:SC_T0 + 1],
         scr[0:1, SC_T0:SC_T0 + 1], float(1.0 / (FF * E)), wait=[("FS", r)])
    kb.v("tensor_scalar_mul", scr[0:1, SC_T0 + 1:SC_T0 + 2],
         scr[0:1, SC_T0 + 1:SC_T0 + 2], float(1.0 / (E * FF)))
    kb.v("tensor_add", scr[0:1, SC_T0:SC_T0 + 2], scr[0:1, SC_T0:SC_T0 + 2],
         small[0:1, SM_M0:SM_M0 + 2])
    kb.v("tensor_scalar_mul", scr[0:1, SC_T0:SC_T0 + 2],
         scr[0:1, SC_T0:SC_T0 + 2], float(1.0 + 3.51e-7))
    kb.v("tensor_scalar_max", scr[0:1, SC_T0:SC_T0 + 2],
         scr[0:1, SC_T0:SC_T0 + 2], 1e-8)
    kb.v("tensor_scalar_mul", scr[0:1, SC_T0:SC_T0 + 2],
         scr[0:1, SC_T0:SC_T0 + 2], 0.5, inc=[("V", 1)])
    kb.mm(psU[:, 100:102], onesf[0:1, :], scr[0:1, SC_T0:SC_T0 + 2],
          wait=[("V", kb.now("V"))], inc=[("T", 1)])
    kb.v("tensor_copy", small[:, SM_THR1:SM_THR1 + 2], psU[:, 100:102],
         wait=[("T", kb.now("T"))], inc=[("V", 1)])
    Vthr = kb.now("V")

    bguard = {SC_A: Vthr, SC_B: Vthr}
    for mat in range(2):
        nch = 8 if mat == 0 else 16
        wid = 2048 if mat == 0 else 1024
        src = t["w1s"] if mat == 0 else t["w2s"]
        dst = t["a32"] if mat == 0 else t["b32"]
        thc = SM_THR1 if mat == 0 else SM_THR2
        thr = small[:, thc:thc + 1]
        for j in range(nch):
            buf = SC_A if j % 2 == 0 else SC_B
            kb.dma("sync", scr[:, buf:buf + wid],
                   src[j * 128:(j + 1) * 128, :], wait=[("V", bguard[buf])])
            r = kb.relay("sync")
            kb.v("tensor_scalar", scr[:, SC_T0:SC_T0 + wid],
                 scr[:, buf:buf + wid], -1.0, thr,
                 op0=ALU.mult, op1=ALU.is_gt, wait=[("FS", r)])
            if mat == 0:
                kb.v("scalar_tensor_tensor", dst[:, j * wid:(j + 1) * wid],
                     scr[:, buf:buf + wid], thr, scr[:, SC_T0:SC_T0 + wid],
                     op0=ALU.is_gt, op1=ALU.subtract, inc=[("V", 1)])
            else:
                kb.v("scalar_tensor_tensor", scr[:, SC_T0:SC_T0 + wid],
                     scr[:, buf:buf + wid], thr, scr[:, SC_T0:SC_T0 + wid],
                     op0=ALU.is_gt, op1=ALU.subtract)
                kb.v("tensor_scalar_mul", dst[:, j * wid:(j + 1) * wid],
                     scr[:, SC_T0:SC_T0 + wid], 0.5, inc=[("V", 1)])
            bguard[buf] = kb.now("V")
    Vq = kb.now("V")

    slot_relay = {0: 0, 1: 0}
    for mat in range(2):
        for grp in range(16):
            g = mat * 16 + grp
            slot = g % 2
            for k in range(8):
                tile = grp * 8 + k
                if mat == 0:
                    fl, dc = tile // 16, tile % 16
                    src = t["a32"][:, fl * 2048 + dc * 128:fl * 2048 + dc * 128 + 128]
                else:
                    dc, fl = tile // 8, tile % 8
                    src = t["b32"][:, dc * 1024 + fl * 128:dc * 1024 + fl * 128 + 128]
                w = []
                if k == 0:
                    w = [("V", kb.now("V"))] if g > 0 else [("V", Vq)]
                kb.tr(t["psT"][:, k * 128:(k + 1) * 128], src, t["idbf_t"],
                      wait=w, inc=([("T", 1)] if k == 7 else []))
            kb.v("tensor_copy", t["qst"][:, slot * 1024:slot * 1024 + 1024],
                 t["psT"][:, :],
                 wait=[("T", kb.now("T"))] +
                      ([("FS", slot_relay[slot])] if slot_relay[slot] else []),
                 inc=[("V", 1)])
            agt = t["ag1_in0"] if mat == 0 else t["ag1_in1"]
            kb.dma("sync",
                   agt[grp * 8:(grp + 1) * 8].rearrange("a p f -> p a f"),
                   t["qst"][:, slot * 1024:slot * 1024 + 1024]
                   .rearrange("p (a f) -> p a f", f=128),
                   wait=[("V", kb.now("V"))])
            slot_relay[slot] = kb.relay("sync")
        if mat == 0:
            r = kb.relay("sync")
            kb.coll("AllGather", ALU.bypass, [t["ag1_in0"]], [t["ag1_out0"]],
                    wait=[("FS", r)], inc=[("C", 1)])
    r = kb.relay("sync")
    kb.coll("AllGather", ALU.bypass, [t["ag1_in1"]], [t["ag1_out1"]],
            wait=[("FS", r)], inc=[("C", 1)])
    t["C_w1"] = kb.now("C") - 1
    t["C_w2"] = kb.now("C")
    tgt = [("FS", kb.now("FS")), ("T", kb.now("T")), ("V", kb.now("V"))]
    for s, v in tgt:
        if v > 0:
            kb.eng.wait_ge(kb.sems[s], v)


def phase_xload(kb, t):
    scr, x, psU, idf = t["scr"], t["x"], t["psU"], t["idf_t"]
    vdone = {}
    for tt in range(8):
        buf = SC_A if tt % 2 == 0 else SC_B
        w = [("V", vdone[tt - 2])] if tt - 2 in vdone else []
        kb.dma("sync", scr[:, buf:buf + 2048],
               t["xs"][tt * 128:(tt + 1) * 128, :], wait=w)
        r = kb.relay("sync")
        for q in range(4):
            for k in range(4):
                dc = q * 4 + k
                kb.tr(psU[:, k * 128:(k + 1) * 128],
                      scr[:, buf + dc * 128:buf + dc * 128 + 128], idf,
                      wait=([("FS", r)] if q == 0 and k == 0 else []) +
                           ([("V", kb.now("V"))] if k == 0 else []),
                      inc=([("T", 1)] if k == 3 else []))
            out = x.rearrange("p (d u) -> p d u", u=T)[:, q * 4:(q + 1) * 4,
                                                       tt * 128:(tt + 1) * 128]
            kb.v("tensor_copy", out,
                 psU[:, 0:512].rearrange("p (d u) -> p d u", u=128),
                 wait=[("T", kb.now("T"))], inc=[("V", 1)])
        vdone[tt] = kb.now("V")
    kb.fence()


def phase_ln(kb, t):
    x, scr, a32 = t["x"], t["scr"], t["a32"]
    ps0, ps2, psU = t["ps0"], t["ps2"], t["psU"]
    onesb, onesf = t["onesb_t"], t["onesf_t"]
    loc = {("sx", 0): (ps2, 0), ("sx", 1): (ps2, 512),
           ("sq", 0): (ps0, 0), ("sq", 1): (ps0, 512)}
    for th in range(2):
        for dc in range(NDC):
            it = th * NDC + dc
            sl = x[:, dc * T + th * 512: dc * T + th * 512 + 512]
            cb = (dc % 2) * 512
            kb.v("tensor_copy", t["expb"][:, cb:cb + 512], sl,
                 wait=([("T", kb.now("T"))] if it >= 2 else []), inc=[("V", 1)])
            kb.act(t["qst"][:, cb:cb + 512], sl, AF.Square,
                   wait=([("T", kb.now("T"))] if it >= 2 else []), inc=[("S", 1)])
            pp, c0 = loc[("sx", th)]
            kb.mm(pp[0:1, c0:c0 + 512], onesb[:, 0:1], t["expb"][:, cb:cb + 512],
                  start=(dc == 0), stop=(dc == NDC - 1),
                  wait=[("V", kb.now("V"))])
            pp, c0 = loc[("sq", th)]
            kb.mm(pp[0:1, c0:c0 + 512], onesb[:, 0:1], t["qst"][:, cb:cb + 512],
                  start=(dc == 0), stop=(dc == NDC - 1),
                  wait=[("S", kb.now("S"))], inc=[("T", 1)])
    Tst = kb.now("T")
    for th in range(2):
        pp, c0 = loc[("sx", th)]
        kb.v("tensor_scalar_mul", scr[0:1, SC_T0 + th * 512:SC_T0 + th * 512 + 512],
             pp[0:1, c0:c0 + 512], 1.0 / E, wait=[("T", Tst)])
        pp, c0 = loc[("sq", th)]
        kb.v("tensor_scalar_mul", scr[0:1, SC_T1 + th * 512:SC_T1 + th * 512 + 512],
             pp[0:1, c0:c0 + 512], 1.0 / E)
    kb.v("tensor_mul", scr[0:1, SC_NMR:SC_NMR + 1024], scr[0:1, SC_T0:SC_T0 + 1024],
         scr[0:1, SC_T0:SC_T0 + 1024])
    kb.v("tensor_sub", scr[0:1, SC_T1:SC_T1 + 1024], scr[0:1, SC_T1:SC_T1 + 1024],
         scr[0:1, SC_NMR:SC_NMR + 1024])
    kb.v("tensor_scalar_add", scr[0:1, SC_T1:SC_T1 + 1024],
         scr[0:1, SC_T1:SC_T1 + 1024], 1e-5, inc=[("V", 1)])
    kb.act(scr[0:1, SC_T1:SC_T1 + 1024], scr[0:1, SC_T1:SC_T1 + 1024], AF.Ln,
           wait=[("V", kb.now("V"))], inc=[("S", 1)])
    kb.act(scr[0:1, SC_T1:SC_T1 + 1024], scr[0:1, SC_T1:SC_T1 + 1024], AF.Exp,
           scale=-0.5, inc=[("S", 1)])
    kb.v("tensor_mul", scr[0:1, SC_T0:SC_T0 + 1024], scr[0:1, SC_T0:SC_T0 + 1024],
         scr[0:1, SC_T1:SC_T1 + 1024], wait=[("S", kb.now("S"))])
    kb.v("tensor_scalar_mul", scr[0:1, SC_T0:SC_T0 + 1024],
         scr[0:1, SC_T0:SC_T0 + 1024], -1.0, inc=[("V", 1)])
    for th in range(2):
        kb.mm(psU[:, 0:512], onesf[0:1, :],
              scr[0:1, SC_T1 + th * 512:SC_T1 + th * 512 + 512],
              wait=[("V", kb.now("V"))], inc=[("T", 1)])
        kb.v("tensor_copy", scr[:, SC_RSTD + th * 512:SC_RSTD + th * 512 + 512],
             psU[:, 0:512], wait=[("T", kb.now("T"))], inc=[("V", 1)])
        kb.mm(psU[:, 0:512], onesf[0:1, :],
              scr[0:1, SC_T0 + th * 512:SC_T0 + th * 512 + 512],
              wait=[("V", kb.now("V"))], inc=[("T", 1)])
        kb.v("tensor_copy", scr[:, SC_NMR + th * 512:SC_NMR + th * 512 + 512],
             psU[:, 0:512], wait=[("T", kb.now("T"))], inc=[("V", 1)])
    Vbc = kb.now("V")
    for dc in range(NDC):
        for th in range(2):
            sl = x[:, dc * T + th * 512: dc * T + th * 512 + 512]
            kb.v("tensor_mul", scr[:, SC_T1:SC_T1 + 512], sl,
                 scr[:, SC_RSTD + th * 512:SC_RSTD + th * 512 + 512],
                 wait=([("V", Vbc)] if dc == 0 and th == 0 else []))
            kb.v("tensor_add", a32[:, dc * T + th * 512: dc * T + th * 512 + 512],
                 scr[:, SC_T1:SC_T1 + 512],
                 scr[:, SC_NMR + th * 512:SC_NMR + th * 512 + 512],
                 inc=[("V", 1)])
    kb.fence()


def phase_ffn(kb, t):
    a32, b32, x, wstr = t["a32"], t["b32"], t["x"], t["wstr"]
    ps0, ps1 = t["ps0"], t["ps1"]
    ag1w1, ag1w2 = t["ag1_out0"], t["ag1_out1"]
    prev = t.setdefault("_ffn_prev", {"T2": 0, "S30": 0, "S31": 0, "V2a": 0, "V2b": 0})
    for th in range(2):
        for fh in range(2):
            tdone, sdone, rr_ = {}, {}, {}
            Tb1 = kb.now("T")
            for fl in range(32):
                fc = fh * 32 + fl
                rk, fli = fc // 8, fc % 8
                ws = (fl % 4) * 2048
                w = [("T", Tb1 + fl - 3)] if fl >= 4 else \
                    [("T", prev["T2"]), ("C", t.get("C_w1", 0))]
                kb.dma("sync", wstr[:, ws:ws + 2048]
                       .rearrange("p (a f) -> p a f", f=128),
                       ag1w1[rk, fli * 16:(fli + 1) * 16]
                       .rearrange("a p f -> p a f"), wait=w,
                       qsem="L" + str(fl % 3))
                rr_[fl] = kb.relay("sync", qsem="L" + str(fl % 3))
            for fl in range(32):
                fc = fh * 32 + fl
                ws = (fl % 4) * 2048
                pslot = (fl % 2) * 512
                for dc in range(NDC):
                    w = []
                    if dc == 0:
                        w = [("FS", rr_[fl])]
                        if fl - 2 in sdone:
                            w.append(("S", sdone[fl - 2]))
                        else:
                            w.append(("S", prev["S30"] if fl == 0 else prev["S31"]))
                    kb.mm(ps0[:, pslot:pslot + 512],
                          wstr[:, ws + dc * 128:ws + dc * 128 + 128],
                          a32[:, dc * T + th * 512:dc * T + th * 512 + 512],
                          start=(dc == 0), stop=(dc == NDC - 1),
                          wait=w, inc=([("T", 1)] if dc == NDC - 1 else []))
                tdone[fl] = kb.now("T")
                assert tdone[fl] == Tb1 + fl + 1
                kb.act(b32[:, fl * 512:fl * 512 + 512], ps0[:, pslot:pslot + 512],
                       AF.Gelu, bias=t["b1_t"][:, fc:fc + 1],
                       wait=[("T", tdone[fl])], inc=[("S", 1)])
                sdone[fl] = kb.now("S")
            Sh = kb.now("S")
            tdone2, vdone2, r2_ = {}, {}, {}
            Tmm1 = kb.now("T")
            for dc in range(NDC):
                ws = (dc % 2) * 4096
                w = [("T", Tmm1 + dc - 1)] if dc >= 2 else \
                    [("T", Tmm1), ("C", t.get("C_w2", 0))]
                for rr in range(4):
                    rk = fh * 4 + rr
                    kb.dma("sync", wstr[:, ws + rr * 1024:ws + rr * 1024 + 1024]
                           .rearrange("p (a f) -> p a f", f=128),
                           ag1w2[rk, dc * 8:(dc + 1) * 8]
                           .rearrange("a p f -> p a f"),
                           wait=(w if rr == 0 else []),
                           qsem="L" + str(dc % 2))
                r2_[dc] = kb.relay("sync", qsem="L" + str(dc % 2))
            for dc in range(NDC):
                ws = (dc % 2) * 4096
                pslot = (dc % 2) * 512
                for fl in range(32):
                    w = []
                    if fl == 0:
                        w = [("FS", r2_[dc]), ("S", Sh)]
                        if dc - 2 in vdone2:
                            w.append(("V", vdone2[dc - 2]))
                    if dc < 2 and fl == 0:
                        w.append(("V", prev["V2a"] if dc == 0 else prev["V2b"]))
                    kb.mm(ps1[:, pslot:pslot + 512],
                          wstr[:, ws + fl * 128:ws + fl * 128 + 128],
                          b32[:, fl * 512:fl * 512 + 512],
                          start=(fl == 0), stop=(fl == 31),
                          wait=w, inc=([("T", 1)] if fl == 31 else []))
                tdone2[dc] = kb.now("T")
                assert tdone2[dc] == Tmm1 + dc + 1
                xs = x[:, dc * T + th * 512: dc * T + th * 512 + 512]
                if fh == 0:
                    kb.v("scalar_tensor_tensor", xs, ps1[:, pslot:pslot + 512],
                         t["b2h_t"][:, dc:dc + 1], xs, op0=ALU.add, op1=ALU.add,
                         wait=[("T", tdone2[dc])], inc=[("V", 1)])
                else:
                    kb.v("tensor_add", xs, xs, ps1[:, pslot:pslot + 512],
                         wait=[("T", tdone2[dc])], inc=[("V", 1)])
                vdone2[dc] = kb.now("V")
            prev["T2"] = tdone2[NDC - 1]
            prev["S30"], prev["S31"] = sdone[30], sdone[31]
            prev["V2a"], prev["V2b"] = vdone2[NDC - 2], vdone2[NDC - 1]


def phase_qkv(kb, t):
    a32, scr, qst, wstr = t["a32"], t["scr"], t["qst"], t["wstr"]
    ps0, ps1, ps2, psT = t["ps0"], t["ps1"], t["ps2"], t["psT"]
    wsrc = [t["wk"], t["wv"], t["wq"]]
    tdone, edone, urelay = {}, {}, {}
    Tq0 = kb.now("T")
    rg_ = {}

    def wload(v):
        ws = (v % 2) * 4096
        w = [("T", Tq0 + 2 * (v - 2) + 1)] if v >= 2 else []
        for piece in range(2):
            kb.dma("gpsimd", wstr[:, ws + piece * 1024:ws + piece * 1024 + 1024]
                   .rearrange("p (a f) -> p a f", f=128),
                   wsrc[v // H][piece * 1024:(piece + 1) * 1024,
                                (v % H) * 128:(v % H) * 128 + 128]
                   .rearrange("(a p) f -> p a f", p=128),
                   wait=(w if piece == 0 else []),
                   qsem="G" + str(v % 2))
        rg_[v] = kb.relay("gpsimd", qsem="G" + str(v % 2))

    for v in range(4):
        wload(v)
    for pi in range(3):
        for ec in range(H):
            u = pi * H + ec
            if u + 4 < 3 * H:
                wload(u + 4)
            ws = (u % 2) * 4096
            psel = ps0 if u % 2 == 0 else ps1
            r = rg_[u]
            for th in range(2):
                for dc in range(NDC):
                    w = []
                    if th == 0 and dc == 0:
                        w = [("FG", r)]
                        if u - 2 in edone:
                            w.append(("S", edone[u - 2][0]))
                            w.append(("V", edone[u - 2][1]))
                    kb.mm(psel[:, th * 512:th * 512 + 512],
                          wstr[:, ws + dc * 128:ws + dc * 128 + 128],
                          a32[:, dc * T + th * 512:dc * T + th * 512 + 512],
                          start=(dc == 0), stop=(dc == NDC - 1), wait=w,
                          inc=([("T", 1)] if th == 1 and dc == NDC - 1 else []))
            tdone[u] = kb.now("T")
            assert tdone[u] == Tq0 + 2 * u + 1, (u, tdone[u], Tq0)
            if u == H:
                kb.coll("AllGather", ALU.bypass, [t["ag2k_in"]], [t["ag2k_out"]],
                        wait=[("FS", urelay[H - 1])], inc=[("C", 1)])
                t["C_k"] = kb.now("C")
            if u == 2 * H:
                kb.coll("AllGather", ALU.bypass, [t["ag2v_in"]], [t["ag2v_out"]],
                        wait=[("FS", urelay[2 * H - 1])], inc=[("C", 1)])
                t["C_v"] = kb.now("C")
            qs = (u % 2) * 1024
            wsl = [("FS", urelay[u - 2])] if u - 2 in urelay else []
            wst = [("FS", urelay[u - 1])] if u - 1 in urelay else []
            if DBG_SKIP_QKV_TAIL:
                kb.scopy(qst[:, qs:qs + 1024], psel[:, :], wait=[("T", tdone[u])],
                         inc=[("S", 1)])
                edone[u] = (kb.now("S"), kb.now("V"))
            elif pi != 1:
                kb.scopy(scr[:, SC_T1:SC_T1 + 1024], psel[:, :],
                         wait=[("T", tdone[u])] +
                              ([("FS", urelay[u - 1])] if u - 1 in urelay else []),
                         inc=[("S", 1)])
                Sev = kb.now("S")
                kb.dma("sync", scr[0:64, SC_NMR:SC_NMR + 1024],
                       scr[64:128, SC_T1:SC_T1 + 1024], wait=[("S", Sev)])
                kb.dma("sync", scr[64:128, SC_NMR:SC_NMR + 1024],
                       scr[0:64, SC_T1:SC_T1 + 1024])
                rsw = kb.relay("sync")
                kb.v("tensor_mul", scr[:, SC_A:SC_A + 1024],
                     scr[:, SC_T1:SC_T1 + 1024], t["cos"][:, :],
                     wait=[("S", Sev)])
                kb.v("tensor_mul", scr[:, SC_B:SC_B + 1024],
                     scr[:, SC_NMR:SC_NMR + 1024], t["sin"][:, :],
                     wait=[("FS", rsw)])
                kb.v("tensor_add", qst[:, qs:qs + 1024], scr[:, SC_A:SC_A + 1024],
                     scr[:, SC_B:SC_B + 1024], inc=[("V", 1)], wait=wsl)
                kb.act(scr[:, SC_A:SC_A + 1024], qst[:, qs:qs + 1024], AF.Square,
                       wait=[("V", kb.now("V"))], inc=[("S", 1)])
                for th in range(2):
                    kb.mm(ps2[0:1, th * 512:th * 512 + 512], t["onesf_t"][:, 0:1],
                          scr[:, SC_A + th * 512:SC_A + th * 512 + 512],
                          wait=([("S", kb.now("S")), ("V", kb.now("V"))]
                                if th == 0 else []),
                          inc=([("T", 1)] if th == 1 else []))
                nsl = 8 + (0 if pi == 0 else H) + ec
                kb.v("reduce_max", t["small"][0:1, nsl:nsl + 1],
                     ps2[0:1, 0:1024], axis=AX.X,
                     wait=[("T", kb.now("T"))], inc=[("V", 1)])
                agt = t["ag2k_in"] if pi == 0 else t["ag2q_in"]
                kb.dma("sync",
                       agt[ec].rearrange("(p u) -> p u", p=128),
                       qst[:, qs:qs + 1024], wait=[("V", kb.now("V"))])
                urelay[u] = kb.relay("sync")
                edone[u] = (kb.now("S"), kb.now("V"))
            else:
                kb.scopy(qst[:, qs:qs + 1024], psel[:, :],
                         wait=[("T", tdone[u])] + wsl, inc=[("S", 1)])
                for tc in range(8):
                    kb.tr(psT[:, tc * 128:(tc + 1) * 128],
                          qst[:, qs + tc * 128:qs + tc * 128 + 128], t["idbf_t"],
                          wait=([("S", kb.now("S")), ("V", kb.now("V"))]
                                if tc == 0 else []),
                          inc=([("T", 1)] if tc == 7 else []))
                kb.v("tensor_copy", t["expb"][:, :], psT[:, :],
                     wait=[("T", kb.now("T"))] + wst, inc=[("V", 1)])
                kb.dma("sync",
                       t["ag2v_in"][ec].rearrange("(p a u) -> p a u", p=128, a=8),
                       t["expb"][:, :].rearrange("p (a u) -> p a u", a=8),
                       wait=[("V", kb.now("V"))])
                urelay[u] = kb.relay("sync")
                edone[u] = (kb.now("S"), kb.now("V"))
    r = urelay[3 * H - 1]
    kb.coll("AllGather", ALU.bypass, [t["ag2q_in"]], [t["ag2q_out"]],
            wait=[("FS", r)], inc=[("C", 1)])
    t["C_q"] = kb.now("C")
    kb.dma("sync", t["nmx_in"].rearrange("(h c) -> h c", h=1),
           t["small"][0:1, 8:8 + 2 * H], wait=[("V", kb.now("V"))])
    r = kb.relay("sync")
    kb.coll("AllGather", ALU.bypass, [t["nmx_in"]], [t["nmx_out"]],
            wait=[("FS", r), ("FG", kb.now("FG"))], inc=[("C", 1)])
    t["C_nmx"] = kb.now("C")
    t["qkv_S"] = kb.now("S")
    t["qkv_V"] = kb.now("V")


def phase_attention(kb, t):
    b32, scr, small = t["b32"], t["scr"], t["small"]
    ps0, ps1, ps2, psU = t["ps0"], t["ps1"], t["ps2"], t["psU"]
    onesf, onesb = t["onesf_t"], t["onesb_t"]

    kb.dma("gpsimd", scr[0:16, SC_T0:SC_T0 + 16],
           t["nmx_out"].rearrange("r (p h) -> h (r p)", p=2), slow=True,
           wait=[("C", t["C_nmx"])])
    r = kb.relay("gpsimd")
    kb.v("reduce_max", scr[0:16, SC_T1:SC_T1 + 1],
         scr[0:16, SC_T0:SC_T0 + 16:2], axis=AX.X, wait=[("FG", r)])
    kb.v("reduce_max", scr[0:16, SC_T1 + 1:SC_T1 + 2],
         scr[0:16, SC_T0 + 1:SC_T0 + 16:2], axis=AX.X)
    kb.v("tensor_mul", scr[0:16, SC_T1 + 2:SC_T1 + 3],
         scr[0:16, SC_T1:SC_T1 + 1], scr[0:16, SC_T1 + 1:SC_T1 + 2],
         inc=[("V", 1)])
    kb.act(scr[0:16, SC_T1 + 2:SC_T1 + 3], scr[0:16, SC_T1 + 2:SC_T1 + 3],
           AF.Sqrt, wait=[("V", kb.now("V"))], inc=[("S", 1)])
    kb.v("tensor_scalar_mul", scr[0:16, SC_T1 + 2:SC_T1 + 3],
         scr[0:16, SC_T1 + 2:SC_T1 + 3], -ISD * 1.001,
         wait=[("S", kb.now("S"))], inc=[("V", 1)])
    kb.dma("gpsimd", t["negc_d"].rearrange("(h c) -> h c", c=1),
           scr[0:16, SC_T1 + 2:SC_T1 + 3], wait=[("V", kb.now("V"))])
    r = kb.relay("gpsimd")
    for lh in range(NHL):
        kb.dma_dyn("gpsimd", scr[0:1, SC_T0 + 40 + lh:SC_T0 + 41 + lh],
                   lambda pid, lh=lh: t["negc_d"]
                   [bass.ds(kb.hsel(lh), 1)].rearrange("(h c) -> h c", c=1),
                   wait=([("FG", r)] if lh == 0 else []))
    r = kb.relay("gpsimd")
    for lh in range(NHL):
        kb.mm(psU[:, 200 + lh:201 + lh], onesf[0:1, :],
              scr[0:1, SC_T0 + 40 + lh:SC_T0 + 41 + lh],
              wait=([("FG", r)] if lh == 0 else []), inc=[("T", 1)])
        kb.v("tensor_copy", small[:, SM_NEGC0 + lh:SM_NEGC0 + lh + 1],
             psU[:, 200 + lh:201 + lh], wait=[("T", kb.now("T"))], inc=[("V", 1)])
    Vnegc = kb.now("V")

    piece_tcons = {}
    ps0_free = {0: t["qkv_S"], 1: t["qkv_S"]}
    expb_free = {0: 0, 1: 0}
    av_free = {0: 0, 1: 0}
    sum_free = {0: t["qkv_V"], 512: t["qkv_V"]}
    gseq = 0
    rbr = {}
    for lh in range(NHL):
        kb.dma_dyn("sync",
                   t["kvl"][0, lh].rearrange("p (r u) -> p r u", u=T),
                   lambda pid, lh=lh:
                   t["ag2k_out"][0:NCORE, bass.ds(kb.hsel(lh), 1)]
                   .rearrange("r h (p u) -> p (r h) u", p=128),
                   wait=([("C", t["C_k"])] if lh == 0 else []))
        kb.dma_dyn("sync",
                   t["kvl"][1, lh].rearrange("p (r g) -> p r g", g=T),
                   lambda pid, lh=lh:
                   t["ag2v_out"][0:NCORE, bass.ds(kb.hsel(lh), 1)]
                   .rearrange("r h (p a u) -> p (r h) (a u)", p=128, a=8),
                   wait=([("C", t["C_v"])] if lh == 0 else []))
        kb.dma_dyn("sync",
                   b32[:, lh * S:(lh + 1) * S].rearrange("p (r u) -> p r u", u=T),
                   lambda pid, lh=lh:
                   t["ag2q_out"][0:NCORE, bass.ds(kb.hsel(lh), 1)]
                   .rearrange("r h (p u) -> p (r h) u", p=128),
                   wait=([("C", t["C_q"])] if lh == 0 else []))
        rbr[lh] = kb.relay("sync")
        for i in range(16):
            nkb = 4 * (i + 1)
            tile = lh * 16 + i
            sumcol = (tile % 2) * 512
            avs = (tile % 2) * 512
            qsl = b32[:, lh * S + i * 512: lh * S + i * 512 + 512]
            npieces = (nkb + 7) // 8
            base_gseq = gseq
            Tt0 = kb.now("T")
            piece_r = {}
            for p_ in range(npieces):
                p = base_gseq + p_
                rk = p_
                ks = (p % 2) * 1024
                w = [("T", piece_tcons[p - 2])] if p - 2 in piece_tcons else []
                kb.dma("sync", t["kbuf"][:, ks:ks + 1024],
                       t["kvl"][0, lh][:, rk * T:(rk + 1) * T], wait=w,
                       qsem="L" + str(p % 3))
                kb.dma("sync", t["vbuf"][:, ks:ks + 1024],
                       t["kvl"][1, lh][:, rk * T:(rk + 1) * T],
                       qsem="L" + str(p % 3))
                piece_r[p] = kb.relay("sync", qsem="L" + str(p % 3))
                last_j = min(8 * p_ + 7, nkb - 1)
                piece_tcons[p] = (Tt0 + 2 * last_j + 3 if last_j < nkb - 1
                                  else Tt0 + 2 * nkb)
            gseq += npieces

            def unit_tail(k):
                es = (k % 2) * 512
                diag = k >= nkb - 4
                gate = [("V", vgate[k])] if diag else [("S", sgate[k])]
                w = list(gate)
                if k == 0 and sum_free[sumcol]:
                    w.append(("V", sum_free[sumcol]))
                kb.mm(ps2[0:1, sumcol:sumcol + 512], onesb[:, 0:1],
                      t["expb"][:, es:es + 512],
                      start=(k == 0), stop=(k == nkb - 1), wait=w)
                w = []
                if k == 0 and av_free[tile % 2]:
                    w.append(("V", av_free[tile % 2]))
                if k == 0 and tile == 0:
                    w.append(("S", t["qkv_S"]))
                koff = ((base_gseq + k // 8) % 2) * 1024 + (k % 8) * 128
                kb.mm(ps1[:, avs:avs + 512], t["vbuf"][:, koff:koff + 128],
                      t["expb"][:, es:es + 512],
                      start=(k == 0), stop=(k == nkb - 1),
                      wait=w, inc=[("T", 1)])
                expb_free[k % 2] = kb.now("T")

            sgate, vgate = {}, {}
            for k in range(nkb):
                p = base_gseq + k // 8
                ks = (p % 2) * 1024
                koff = ks + (k % 8) * 128
                ss = (k % 2) * 512
                es = (k % 2) * 512
                w = [("FS", piece_r[p])]
                if i == 0 and k == 0:
                    w.append(("FS", rbr[lh]))
                if ps0_free[k % 2]:
                    w.append(("S", ps0_free[k % 2]))
                kb.mm(ps0[:, ss:ss + 512], t["kbuf"][:, koff:koff + 128], qsl,
                      start=True, stop=True, wait=w, inc=[("T", 1)])
                tsc = kb.now("T")
                assert tsc == (Tt0 + 1 if k == 0 else Tt0 + 2 * k), \
                    (tile, k, tsc, Tt0)
                we = [("T", max(tsc, expb_free[k % 2]))]
                if lh == 0 and i == 0 and k == 0:
                    we.append(("V", Vnegc))
                kb.act(t["expb"][:, es:es + 512], ps0[:, ss:ss + 512], AF.Exp,
                       bias=small[:, SM_NEGC0 + lh:SM_NEGC0 + lh + 1], scale=ISD,
                       wait=we, inc=[("S", 1)])
                ps0_free[k % 2] = kb.now("S")
                sgate[k] = kb.now("S")
                if k >= nkb - 4:
                    j = k - (nkb - 4)
                    kb.v("tensor_mul", t["expb"][:, es:es + 512],
                         t["expb"][:, es:es + 512],
                         t["mask"][:, j * 512:(j + 1) * 512],
                         wait=[("S", sgate[k])], inc=[("V", 1)])
                    vgate[k] = kb.now("V")
                if k >= 1:
                    unit_tail(k - 1)
            unit_tail(nkb - 1)
            Ttile = kb.now("T")
            kb.v("tensor_copy", scr[0:1, SC_T0:SC_T0 + 512],
                 ps2[0:1, sumcol:sumcol + 512], wait=[("T", Ttile)])
            kb.v("reciprocal", scr[0:1, SC_T0:SC_T0 + 512],
                 scr[0:1, SC_T0:SC_T0 + 512], inc=[("V", 1)])
            sum_free[sumcol] = kb.now("V")
            kb.mm(psU[:, 0:512], onesf[0:1, :], scr[0:1, SC_T0:SC_T0 + 512],
                  wait=[("V", kb.now("V"))], inc=[("T", 1)])
            kb.v("tensor_copy", scr[:, SC_NMR:SC_NMR + 512], psU[:, 0:512],
                 wait=[("T", kb.now("T"))])
            kb.v("tensor_mul", qsl, ps1[:, avs:avs + 512],
                 scr[:, SC_NMR:SC_NMR + 512], inc=[("V", 1)])
            av_free[tile % 2] = kb.now("V")
        agt = t["ag3a_in"] if lh == 0 else t["ag3b_in"]
        ago = t["ag3a_out"] if lh == 0 else t["ag3b_out"]
        kb.dma("gpsimd", agt[:, :], b32[:, lh * S:(lh + 1) * S],
               wait=[("V", kb.now("V"))], qsem="G" + str(lh))
        rr = kb.relay("gpsimd", qsem="G" + str(lh))
        kb.coll("AllGather", ALU.bypass, [agt], [ago],
                wait=[("FG", rr)], inc=[("C", 1)])
        t["C_ag3" + str(lh)] = kb.now("C")
    a32v = t["a32"][:, :].rearrange("p (g u) -> p g u", u=T)
    kb.dma_dyn("sync", a32v[:, 0:16:2, :],
               lambda pid: t["ag3a_out"][0:NCORE, :, bass.ds(kb.town(), T)]
               .rearrange("r p u -> p r u"),
               wait=[("C", t["C_ag30"])])
    rA = kb.relay("sync")
    kb.dma_dyn("sync", a32v[:, 1:16:2, :],
               lambda pid: t["ag3b_out"][0:NCORE, :, bass.ds(kb.town(), T)]
               .rearrange("r p u -> p r u"),
               wait=[("C", t["C_ag31"])])
    rB = kb.relay("sync")

    tdone, vdone = {}, {}
    Vflash = kb.now("V")
    for dc in range(NDC):
        ws = (dc % 2) * 4096
        w = [("T", tdone[dc - 2])] if dc - 2 in tdone else []
        for piece in range(2):
            kb.dma("gpsimd", t["wstr"][:, ws + piece * 1024:ws + piece * 1024 + 1024]
                   .rearrange("p (a f) -> p a f", f=128),
                   t["wo"][piece * 1024:(piece + 1) * 1024, dc * 128:(dc + 1) * 128]
                   .rearrange("(a p) f -> p a f", p=128),
                   wait=(w if piece == 0 else []))
        r = kb.relay("gpsimd")
        for th in range(2):
            pslot = ((dc * 2 + th) % 2) * 512
            ecs = list(range(0, H, 2)) + list(range(1, H, 2))
            for n_, ec in enumerate(ecs):
                w = []
                if n_ == 0:
                    w = [("FG", r), ("FS", rA), ("V", Vflash)]
                    k = dc * 2 + th - 2
                    if k in vdone:
                        w.append(("V", vdone[k]))
                if n_ == H // 2:
                    w = [("FS", rB)]
                kb.mm(t["ps1"][:, pslot:pslot + 512],
                      t["wstr"][:, ws + ec * 128:ws + ec * 128 + 128],
                      t["a32"][:, ec * T + th * 512:ec * T + th * 512 + 512],
                      start=(n_ == 0), stop=(n_ == H - 1),
                      wait=w, inc=([("T", 1)] if n_ == H - 1 else []))
            xs = t["x"][:, dc * T + th * 512: dc * T + th * 512 + 512]
            kb.v("tensor_add", xs, xs, t["ps1"][:, pslot:pslot + 512],
                 wait=[("T", kb.now("T"))], inc=[("V", 1)])
            vdone[dc * 2 + th] = kb.now("V")
        tdone[dc] = kb.now("T")
    kb.fence()


def phase_store(kb, t, src, dtype):
    scr = t["scr"]
    ident = t["idf_t"] if dtype == F32 else t["idbf_t"]
    psx = t["psU"] if dtype == F32 else t["psT"]
    rdone = {}
    for tt in range(8):
        buf = SC_A if tt % 2 == 0 else SC_B
        for q in range(4):
            for k in range(4):
                dc = q * 4 + k
                sl = src[:, dc * T + tt * 128: dc * T + tt * 128 + 128]
                w = []
                if k == 0:
                    w = [("V", kb.now("V"))]
                    if tt - 2 in rdone and q == 0:
                        w.append(("FS", rdone[tt - 2]))
                kb.tr(psx[:, k * 128:(k + 1) * 128], sl, ident,
                      wait=w, inc=([("T", 1)] if k == 3 else []))
            kb.v("tensor_copy", scr[:, buf + q * 512:buf + q * 512 + 512],
                 psx[:, 0:512],
                 wait=[("T", kb.now("T"))] +
                      ([("FS", rdone[tt - 2])] if tt - 2 in rdone else []),
                 inc=[("V", 1)])
        kb.dma("sync", t["out"][tt * 128:(tt + 1) * 128, :],
               scr[:, buf:buf + 2048], wait=[("V", kb.now("V"))])
        rdone[tt] = kb.relay("sync")
    kb.fence()


def emit_stream(kb, t):
    stage = DEBUG_STAGE
    t.pop("_ffn_prev", None)
    phase_consts(kb, t)
    if stage >= 2:
        phase_wprep(kb, t)
    phase_xload(kb, t)
    if stage == 0:
        phase_store(kb, t, t["x"], F32)
        return
    phase_ln(kb, t)
    if stage == 1:
        phase_store(kb, t, t["a32"], BF16)
        return
    phase_ffn(kb, t)
    if stage == 2:
        phase_store(kb, t, t["x"], F32)
        return
    phase_ln(kb, t)
    phase_qkv(kb, t)
    if stage == 3:
        phase_store(kb, t, t["x"], F32)
        return
    phase_attention(kb, t)
    if stage == 4:
        phase_store(kb, t, t["x"], F32)
        return
    phase_ln(kb, t)
    phase_ffn(kb, t)
    phase_store(kb, t, t["x"], F32)


def build_nc():
    nc = bass.Bass("TRN2", target_bir_lowering=False, debug=False)
    es = ExitStack()
    t = declare(nc, es)
    sems = {}
    for name in SEMS:
        sems[name] = es.enter_context(nc.semaphore("sem" + name))
    block = es.enter_context(nc.Block())
    counts = {}

    def make(eng_name):
        def run(eng):
            kbx = KB(nc, eng_name, eng, sems, t)
            emit_stream(kbx, t)
            counts[eng_name] = dict(kbx.c)
        return run

    for eng_name in ENGS:
        getattr(block, eng_name)(make(eng_name))
    vals = list(counts.values())
    for v in vals[1:]:
        assert v == vals[0], f"sem count mismatch: {counts}"
    es.close()
    return nc


def host_inputs(full):
    x = np.asarray(full["L0_attn_x"]).reshape(S, E)
    w1 = np.asarray(full["L0_ffn_w1"])
    w2 = np.asarray(full["L0_ffn_w2"])
    b1 = np.ascontiguousarray(np.asarray(full["L0_ffn_b1"]).reshape(64, 128))
    b2h = np.ascontiguousarray(
        (np.asarray(full["L0_ffn_b2"]) * 0.5).astype(np.float32).reshape(16, 128))

    pos = np.arange(S, dtype=np.float64)
    inv = 1.0 / (10000.0 ** (np.arange(0, D, 2, dtype=np.float64) / D))
    ang = pos[:, None] * inv[None, :]
    cos = np.cos(ang).T
    sin = np.sin(ang).T
    cosf = np.concatenate([cos, cos], 0).astype(np.float32)
    sinf = np.concatenate([-sin, sin], 0).astype(np.float32)

    kk = np.arange(128)[:, None]
    qq = np.arange(512)[None, :]
    masks = np.stack([(qq >= kk + 128 * j) for j in range(4)]) \
        .astype(ml_dtypes.bfloat16)

    shared = {
        "wq": np.asarray(full["L0_attn_wq"]), "wk": np.asarray(full["L0_attn_wk"]),
        "wv": np.asarray(full["L0_attn_wv"]), "wo": np.asarray(full["L0_attn_wo"]),
        "b1": b1, "b2h": b2h, "masks": np.asarray(masks),
        "idbf": np.asarray(np.eye(128, dtype=ml_dtypes.bfloat16)),
        "idf": np.eye(128, dtype=np.float32),
        "onesf": np.ones((128, 128), np.float32),
        "onesb": np.asarray(np.ones((128, 8), ml_dtypes.bfloat16)),
    }
    maps = []
    for c in range(NCORE):
        m = dict(shared)
        m["xs"] = np.ascontiguousarray(x[c * T:(c + 1) * T])
        m["w1s"] = np.ascontiguousarray(w1[c * FSL:(c + 1) * FSL])
        m["w2s"] = np.ascontiguousarray(w2[:, c * FSL:(c + 1) * FSL])
        m["cosf"] = np.ascontiguousarray(cosf[:, c * T:(c + 1) * T])
        m["sinf"] = np.ascontiguousarray(sinf[:, c * T:(c + 1) * T])
        maps.append(m)
    return maps


def kernel(**inputs):
    maps = host_inputs(inputs)
    nc = build_nc()
    res = run_bass_kernel_spmd(nc, maps, core_ids=list(range(NCORE)))
    out = np.concatenate([res.results[c]["out"] for c in range(NCORE)], axis=0)
    return np.ascontiguousarray(out.reshape(1, S, E).astype(np.float32))
